# version 17
# baseline (speedup 1.0000x reference)
import numpy as np
import ml_dtypes
import concourse.bass as bass
import concourse.mybir as mybir
from concourse.bass_utils import run_bass_kernel_spmd
from contextlib import ExitStack

F32 = mybir.dt.float32
BF16 = mybir.dt.bfloat16
I8 = mybir.dt.int8
I32 = mybir.dt.int32
AX = mybir.AxisListType.X
ALU = mybir.AluOpType
ACTF = mybir.ActivationFunctionType

T_TOK = 4096
D = 1024
H = 16
DH = 64
C = 128
NCH = T_TOK // C
SCALE = float(DH ** (-0.25))
CM = 12582912.0
INV127 = float(np.float32(1.0) / np.float32(127.0))


def build_nc():
    nc = bass.Bass("TRN2", target_bir_lowering=False)

    x_hbm = nc.dram_tensor("x", [T_TOK, D], F32, kind="ExternalInput").ap()
    xT_hbm = nc.dram_tensor("xT", [D, T_TOK], F32, kind="ExternalInput").ap()
    wqkv_hbm = nc.dram_tensor("wqkv", [D, 3 * D], F32, kind="ExternalInput").ap()
    wout_hbm = nc.dram_tensor("wout", [D, D], F32, kind="ExternalInput").ap()
    wi8_hbm = nc.dram_tensor("wi8", [D, D], I8, kind="ExternalInput").ap()
    wsc_hbm = nc.dram_tensor("wsc", [1, D], F32, kind="ExternalInput").ap()
    bias_hbm = nc.dram_tensor("bias", [1, D], F32, kind="ExternalInput").ap()
    cos_hbm = nc.dram_tensor("cost", [C, NCH * DH], F32, kind="ExternalInput").ap()
    sin_hbm = nc.dram_tensor("sint", [C, NCH * DH], F32, kind="ExternalInput").ap()
    mask_hbm = nc.dram_tensor("maskT", [C, C], F32, kind="ExternalInput").ap()
    eye_hbm = nc.dram_tensor("eye", [C, C], F32, kind="ExternalInput").ap()
    out_hbm = nc.dram_tensor("out", [T_TOK, D], F32, kind="ExternalOutput").ap()
    part_hbm = nc.dram_tensor("part_scratch", [T_TOK, D], BF16).ap()
    cc_in = nc.dram_tensor("cc_in", [C, 1], F32)
    cc_out = nc.dram_tensor("cc_out", [C, 1], F32)
    gm_dram = nc.dram_tensor("gm_dram", [1, 1], F32)

    ctx = ExitStack()
    sb = lambda name, sh, dt: ctx.enter_context(nc.sbuf_tensor(name, sh, dt))
    psum = lambda name, sh: ctx.enter_context(nc.psum_tensor(name, sh, F32))
    sem = lambda name: ctx.enter_context(nc.semaphore(name))

    with ctx:
        wqkv_sb = sb("wqkv_sb", [C, 8 * 3 * D], BF16)
        wout_sb = sb("wout_sb", [C, 8 * D], BF16)
        wffnT_sb = sb("wffnT_sb", [C, 8 * D], BF16)
        swqk_bc = sb("swqk_bc", [C, 2 * D], BF16)
        ws_bc = sb("ws_bc", [C, D], F32)
        sws_bc = sb("sws_bc", [C, D], F32)
        bias_row = sb("bias_row", [1, D], BF16)
        wsc_row = sb("wsc_row", [1, D], F32)
        swqk_row = sb("swqk_row", [1, 2 * D], BF16)
        swv_row = sb("swv_row", [1, D], F32)
        swv_cols = sb("swv_cols", [C, 8], F32)
        dv_sb = sb("dv_sb", [C, 8 * H], BF16)
        posG = sb("posG", [H, D], BF16)
        costab = sb("costab", [C, NCH * DH], BF16)
        sintab = sb("sintab", [C, NCH * DH], BF16)
        maskT_sb = sb("maskT_sb", [C, C], BF16)
        id_bf = sb("id_bf", [C, C], BF16)
        ones_col = sb("ones_col", [C, 1], BF16)
        ones_row = sb("ones_row", [1, C], BF16)
        ones_row_f = sb("ones_row_f", [1, C], F32)
        ones11_f = sb("ones11_f", [1, 1], F32)
        ind_lo = sb("ind_lo", [C, 1], F32)
        ind_hi = sb("ind_hi", [C, 1], F32)
        wi8_stage = sb("wi8_stage", [C, D], BF16)
        epsln = sb("epsln", [C, 1], F32)
        neghalf = sb("neghalf", [C, 1], F32)
        c15 = sb("c15", [C, 1], F32)
        scsc = sb("scsc", [C, 1], F32)

        xa = [sb(f"xa{i}", [C, D], F32) for i in range(2)]
        xTa = [sb(f"xTa{i}", [C, D], BF16) for i in range(2)]
        qkln = sb("qkln", [C, 2 * D], BF16)
        rt1 = sb("rt1", [C, 2 * D], BF16)
        rt2 = sb("rt2", [C, 2 * D], BF16)
        qkp = [sb(f"qkp{i}", [C, 2 * D], BF16) for i in range(2)]
        qkT = sb("qkT", [DH, 2 * H * C], BF16)
        vpp = [sb(f"vpp{i}", [C, H * 66], BF16) for i in range(2)]
        smb2 = [sb(f"smb{i}", [C, 4 * C], BF16) for i in range(2)]
        smb = [smb2[0][:, :], smb2[1][:, :],
               wi8_stage[:, 0:512], wi8_stage[:, 512:1024]]
        attn_sb = sb("attn_sb", [C, D], BF16)
        attnT_sb = sb("attnT_sb", [C, D], BF16)
        m1d = sb("m1d", [C, H], F32)
        m1d_bf = sb("m1d_bf", [C, H], BF16)
        m1dT = sb("m1dT", [H, C], BF16)
        kvf = [sb(f"kvf{i}", [DH, H * 66], F32) for i in range(2)]
        kv_bf = sb("kv_bf", [DH, H * 66], BF16)
        partial = [sb(f"partial{i}", [C, D], BF16) for i in range(2)]

        bn6 = sb("bn6", [C, 12], F32)
        bn2 = [sb(f"bn2_{i}", [C, 2], F32) for i in range(2)]
        mean_all = sb("mean_all", [C, NCH], F32)
        rstd_all = sb("rstd_all", [C, NCH], F32)
        srst_all = sb("srst_all", [C, NCH], F32)
        maxx_all = sb("maxx_all", [C, NCH], F32)
        minx_all = sb("minx_all", [C, NCH], F32)
        tmp1c = sb("tmp1c", [C, NCH], F32)
        tmp2c = sb("tmp2c", [C, NCH], F32)
        s_ve = [sb(f"s_ve{i}", [C, 1], F32) for i in range(2)]
        s_r2 = [sb(f"s_r2{i}", [C, 1], F32) for i in range(2)]
        s_tt = [sb(f"s_tt{i}", [C, 1], F32) for i in range(2)]
        s_uu = [sb(f"s_uu{i}", [C, 1], F32) for i in range(2)]
        s_sd = [sb(f"s_sd{i}", [C, 1], F32) for i in range(2)]
        dent = sb("dent", [C, 4], F32)
        rcp = sb("rcp", [C, 4], F32)
        locmax = sb("locmax", [C, 1], F32)
        grow = sb("grow", [1, C], F32)
        g11 = sb("g11", [1, 1], F32)
        gmax_bc = sb("gmax_bc", [C, 1], F32)
        xsc = sb("xsc", [C, 1], F32)
        rs_col = sb("rs_col", [C, 1], F32)
        a_col = [sb(f"a_col{i}", [C, 1], F32) for i in range(2)]
        b_col = [sb(f"b_col{i}", [C, 1], F32) for i in range(2)]

        ti32 = sb("ti32", [C, D], I32)
        pb_sb = [sb(f"pb_sb{i}", [C, D], BF16) for i in range(2)]
        ob = [sb(f"ob{i}", [C, D], F32) for i in range(2)]

        pb = [psum(f"pb{i}", [C, 512]) for i in range(8)]

        o1t = qkln[:, :].bitcast(F32)
        g2t = costab[:, :].bitcast(F32)
        xqT = rt2
        ztb = [rt1[:, :].bitcast(F32), sintab[:, :].bitcast(F32)]
        tib = [ti32[:, :], wqkv_sb[:, 0:2048].bitcast(I32)]
        xa4 = [xa[0][:, :], xa[1][:, :],
               qkp[0][:, :].bitcast(F32), qkp[1][:, :].bitcast(F32)]
        pb4_sb = [pb_sb[0][:, :], pb_sb[1][:, :], attn_sb[:, :], attnT_sb[:, :]]

        dxs = [sem("dx0"), sem("dx1")]
        dxts = [sem("dxt0"), sem("dxt1")]
        dwin = sem("dwin")
        dsm = sem("dsm")
        dwi8 = sem("dwi8")
        dsin = sem("dsin")
        dps = [sem("dps0"), sem("dps1")]
        dpl = [sem("dpl0"), sem("dpl1")]
        dpb4 = [sem(f"dpb4_{i}") for i in range(4)]
        dpl4 = [sem(f"dpl4_{i}") for i in range(4)]
        sv_g2t = sem("sv_g2t")
        dos = [sem("dos0"), sem("dos1")]
        dcc = sem("dcc")
        sg_i = sem("sg_i")
        st_i = sem("st_i")
        ss_i = sem("ss_i")
        sv_i = sem("sv_i")
        sv_stats = sem("sv_stats")
        ss_stats = sem("ss_stats")
        st_qkv = sem("st_qkv")
        sv_qkevac = sem("sv_qkevac")
        ss_vevac = sem("ss_vevac")
        sv_rope = sem("sv_rope")
        ss_exp = sem("ss_exp")
        sv_elu = sem("sv_elu")
        st_qkT = sem("st_qkT")
        ss_qkTevac = sem("ss_qkTevac")
        sv_qkTq = sem("sv_qkTq")
        sv_qkTek = sem("sv_qkTek")
        st_stile = sem("st_stile")
        sv_smevac = sem("sv_smevac")
        st_attnout = sem("st_attnout")
        st_kv = sem("st_kv")
        sv_divevac = sem("sv_divevac")
        sv_m1d = sem("sv_m1d")
        sv_kvadd = sem("sv_kvadd")
        ss_kvcast = sem("ss_kvcast")
        st_attnT = sem("st_attnT")
        ss_attnTevac = sem("ss_attnTevac")
        st_outproj = sem("st_outproj")
        ss_partial = sem("ss_partial")
        sv_locmax = sem("sv_locmax")
        sv_g11 = sem("sv_g11")
        sg_cc = sem("sg_cc")
        scc_fin = sem("scc_fin")
        sv_xsc = sem("sv_xsc")
        sv_ab = sem("sv_ab")
        ss_z = sem("ss_z")
        sv_round = sem("sv_round")
        ss_xq = sem("ss_xq")
        st_xqT = sem("st_xqT")
        ss_xqTevac = sem("ss_xqTevac")
        st_ffn = sem("st_ffn")
        sv_o1 = sem("sv_o1")
        sg_o2 = sem("sg_o2")

        block = ctx.enter_context(nc.Block())
        IS_TOTAL = 22

        @block.sync
        def _(sync):
            sync.dma_start(out=wsc_row[:], in_=wsc_hbm).then_inc(dsin, 16)
            for c in range(NCH):
                p = c % 2
                if c >= 2:
                    sync.wait_ge(sv_stats, c - 1)
                sync.dma_start(out=xa[p][:], in_=x_hbm[c * C:(c + 1) * C, :]).then_inc(dxs[p], 16)
                if c >= 2:
                    sync.wait_ge(ss_partial, c - 1)
                    sync.dma_start(out=part_hbm[(c - 2) * C:(c - 1) * C, :],
                                   in_=partial[c % 2][:]).then_inc(dps[c % 2], 16)
            for cc2 in (NCH - 2, NCH - 1):
                sync.wait_ge(ss_partial, cc2 + 1)
                sync.dma_start(out=part_hbm[cc2 * C:(cc2 + 1) * C, :],
                               in_=partial[cc2 % 2][:]).then_inc(dps[cc2 % 2], 16)
            for c in range(NCH):
                p = c % 2
                b4 = c % 4
                if c < 2:
                    sync.wait_ge(ss_partial, NCH)
                    sync.wait_ge(sv_stats, NCH)
                elif c < 4:
                    sync.wait_ge(st_kv, NCH)
                    sync.wait_ge(ss_vevac, NCH)
                    sync.wait_ge(sv_elu, 2 * NCH)
                else:
                    sync.wait_ge(ss_z, c - 3)
                    sync.wait_ge(sv_g2t, c - 3)
                sync.dma_start(out=xa4[b4], in_=x_hbm[c * C:(c + 1) * C, :]).then_inc(dpb4[b4], 16)
                if c < 2:
                    sync.wait_ge(dps[0], 16 * (NCH // 2))
                    sync.wait_ge(dps[1], 16 * (NCH // 2))
                elif c < 4:
                    sync.wait_ge(st_outproj, NCH)
                else:
                    sync.wait_ge(sg_o2, c - 3)
                sync.dma_start(out=pb4_sb[b4],
                               in_=part_hbm[c * C:(c + 1) * C, :]).then_inc(dpl4[b4], 16)
            sync.wait_ge(dos[0], 16 * (NCH // 2))
            sync.wait_ge(dos[1], 16 * (NCH // 2))

        @block.gpsimd
        def _(g):
            g.dma_start(out=id_bf[:], in_=eye_hbm).then_inc(dsm, 16)
            g.dma_start(out=maskT_sb[:], in_=mask_hbm).then_inc(dsm, 16)
            g.dma_start(out=costab[:], in_=cos_hbm).then_inc(dsm, 16)
            g.dma_start(out=sintab[:], in_=sin_hbm).then_inc(dsm, 16)
            g.dma_start(out=bias_row[:], in_=bias_hbm).then_inc(dsm, 16)
            g.memset(ones_col[:], 1.0)
            g.memset(ones_row[:], 1.0)
            g.memset(ones_row_f[:], 1.0)
            g.memset(ones11_f[:], 1.0)
            g.memset(ind_lo[0:64, :], 1.0)
            g.memset(ind_lo[64:128, :], 0.0)
            g.memset(ind_hi[0:64, :], 0.0)
            g.memset(ind_hi[64:128, :], 1.0)
            g.memset(dv_sb[:], 0.0)
            g.memset(kvf[0][:], 0.0)
            g.memset(kvf[1][:], 0.0)
            g.memset(kv_bf[:], 0.0)
            g.memset(epsln[:], 1e-5)
            g.memset(neghalf[:], -0.5)
            g.memset(c15[:], 1.5)
            g.memset(scsc[:], SCALE)
            g.memset(bass.AP(vpp[0][:, :].tensor, 65, [[H * 66, C], [66, H]]), 1.0)
            g.memset(bass.AP(vpp[1][:, :].tensor, 65, [[H * 66, C], [66, H]]), 1.0)
            g.drain()
            g.sem_inc(sg_i, 1)
            g.dma_start(out=wi8_stage[:], in_=wi8_hbm[0:C, :]).then_inc(dwi8, 16)
            src = bass.AP(wqkv_hbm.tensor, 0, [[3 * D, C], [C * 3 * D, 8], [1, 3 * D]])
            g.dma_start(out=wqkv_sb[:, :].rearrange("p (f o) -> p f o", f=8), in_=src).then_inc(dwin, 16)
            src = bass.AP(wout_hbm.tensor, 0, [[D, C], [C * D, 8], [1, D]])
            g.dma_start(out=wout_sb[:, :].rearrange("p (f o) -> p f o", f=8), in_=src).then_inc(dwin, 16)
            for nj in range(1, 8):
                g.wait_ge(st_i, 2 * nj)
                g.dma_start(out=wi8_stage[:], in_=wi8_hbm[nj * C:(nj + 1) * C, :]).then_inc(dwi8, 16)
            def kvcast(c):
                g.wait_ge(sv_kvadd, c + 1)
                g.drain()
                g.tensor_copy(kv_bf[:], kvf[(c + 1) % 2][:])
                g.drain()
                g.sem_inc(ss_kvcast, 1)

            for c in range(NCH):
                p = c % 2
                if c >= 2:
                    g.wait_ge(st_qkv, 3 * (c - 2) + 3)
                src = bass.AP(xT_hbm.tensor, c * C, [[T_TOK, C], [C * T_TOK, 8], [1, C]])
                g.dma_start(out=xTa[p][:, :].rearrange("p (f t) -> p f t", f=8), in_=src).then_inc(dxts[p], 16)
                if c >= 2:
                    kvcast(c - 2)
            g.wait_ge(sv_locmax, 1)
            g.dma_start(out=cc_in.ap(), in_=locmax[:]).then_inc(dcc, 16)
            g.wait_ge(dcc, 16)
            g.collective_compute(
                "AllReduce", ALU.max,
                replica_groups=[list(range(8))],
                ins=[cc_in.ap()],
                outs=[cc_out.ap()],
            ).then_inc(scc_fin, 1)
            kvcast(NCH - 2)
            g.wait_ge(scc_fin, 1)
            g.dma_start(out=grow[0:1, :], in_=bass.AP(cc_out, 0, [[0, 1], [1, C]])).then_inc(dcc, 16)
            g.wait_ge(dcc, 32)
            g.wait_ge(sv_g11, 1)
            g.dma_start(out=gm_dram.ap(), in_=g11[0:1, 0:1]).then_inc(dcc, 16)
            g.wait_ge(dcc, 48)
            g.dma_start(out=gmax_bc[:], in_=bass.AP(gm_dram, 0, [[0, C], [1, 1]])).then_inc(dcc, 16)
            g.wait_ge(dcc, 64)
            g.drain()
            g.sem_inc(sg_cc, 1)
            for c in range(NCH):
                p = c % 2
                g.wait_ge(sv_g2t, c + 1)
                g.wait_ge(dpl4[c % 4], 16 * (c // 4 + 1))
                if c >= 2:
                    g.wait_ge(dos[p], 16 * (c // 2))
                g.drain()
                g.tensor_tensor(ob[p][:], g2t[:], pb4_sb[c % 4], ALU.add)
                g.drain()
                g.sem_inc(sg_o2, 1)
                g.dma_start(out=out_hbm[c * C:(c + 1) * C, :],
                            in_=ob[p][:]).then_inc(dos[p], 16)

        @block.tensor
        def _(t):
            t.wait_ge(sg_i, 1)
            t.wait_ge(dsm, 16 * 5)
            g_idx = 0
            for nj in range(8):
                t.wait_ge(dwi8, 16 * (nj + 1))
                for grp in range(2):
                    bk = 6 + grp
                    if g_idx >= 2:
                        t.wait_ge(ss_i, g_idx - 1)
                    for i in range(4):
                        kc = grp * 4 + i
                        ins = t.matmul(pb[bk][0:C, i * C:(i + 1) * C],
                                       wi8_stage[:, kc * C:(kc + 1) * C], id_bf[:],
                                       start=(i == 0), stop=(i == 3))
                    ins.then_inc(st_i, 1)
                    g_idx += 1
            t.wait_ge(dwin, 16 * 2)
            for obk in range(4):
                for f in range(8):
                    ins = t.matmul(pb[obk][0:1, 0:512], ones_col[:, 0:1],
                                   wqkv_sb[:, f * 3 * D + obk * 512: f * 3 * D + obk * 512 + 512],
                                   start=(f == 0), stop=(f == 7))
            ins.then_inc(st_i, 1)
            t.wait_ge(ss_i, 17)
            for obk in range(4):
                ins = t.matmul(pb[obk][0:C, 0:512], ones_row[0:1, :],
                               swqk_row[0:1, obk * 512:(obk + 1) * 512],
                               start=True, stop=True)
            ins.then_inc(st_i, 1)
            for ov in range(2):
                for f in range(8):
                    ins = t.matmul(pb[4 + ov][0:1, 0:512], ones_col[:, 0:1],
                                   wqkv_sb[:, f * 3 * D + 2048 + ov * 512: f * 3 * D + 2048 + ov * 512 + 512],
                                   start=(f == 0), stop=(f == 7))
            ins.then_inc(st_i, 1)
            t.wait_ge(ss_i, 19)
            for fc in range(8):
                ins = t.matmul(pb[4][0:C, fc:fc + 1], swv_row[0:1, fc * C:(fc + 1) * C],
                               ones11_f[0:1, 0:1], start=(fc == 0), stop=(fc == 7))
            ins.then_inc(st_i, 1)
            t.wait_ge(dsin, 16)
            t.wait_ge(ss_i, 18)
            for obk in range(2):
                ins = t.matmul(pb[obk][0:C, 0:512], ones_row_f[0:1, :],
                               wsc_row[0:1, obk * 512:(obk + 1) * 512], start=True, stop=True)
            ins.then_inc(st_i, 1)
            t.wait_ge(sv_i, 1)
            t.wait_ge(ss_i, 16)
            for ov in range(2):
                for fc in range(8):
                    ins = t.matmul(pb[6 + ov][0:H, 0:512], dv_sb[:, fc * H:(fc + 1) * H],
                                   wout_sb[:, fc * D + ov * 512: fc * D + ov * 512 + 512],
                                   start=(fc == 0), stop=(fc == 7))
            ins.then_inc(st_i, 1)
            t.wait_ge(ss_i, IS_TOTAL)

            BMAP = [2, 3, 6, 7]

            def wave(c, w):
                p = c % 2
                if w == 0:
                    t.wait_ge(dxts[p], 16 * (c // 2 + 1))
                    if c >= 1:
                        t.wait_ge(ss_vevac, c)
                else:
                    t.wait_ge(sv_qkevac, 2 * c + w)
                for f in range(8):
                    for j in range(2):
                        obk = 2 * w + j
                        ins = t.matmul(pb[j][0:C, 0:512],
                                       xTa[p][:, f * C:(f + 1) * C],
                                       wqkv_sb[:, f * 3 * D + obk * 512: f * 3 * D + obk * 512 + 512],
                                       start=(f == 0), stop=(f == 7))
                ins.then_inc(st_qkv, 1)

            def transposes(c):
                for g8 in range(8):
                    bk = BMAP[g8 % 4]
                    t.wait_ge(sv_elu, 2 * c + (1 if g8 < 4 else 2))
                    if g8 == 0 and c >= 1:
                        t.wait_ge(sv_kvadd, c)
                    if g8 == 2 and c >= 1:
                        t.wait_ge(sv_divevac, 4 * c)
                        t.wait_ge(ss_attnTevac, 3 * c)
                    if g8 in (4, 5):
                        t.wait_ge(ss_qkTevac, 2 * c + g8 - 3)
                    elif g8 >= 6:
                        t.wait_ge(sv_qkTq, 2 * c + g8 - 5)
                    for i in range(4):
                        hh = g8 * 4 + i
                        ins = t.matmul(pb[bk][0:DH, i * C:(i + 1) * C],
                                       qkp[c % 2][:, hh * DH:(hh + 1) * DH], id_bf[:],
                                       start=(i == 0), stop=(i == 3))
                    ins.then_inc(st_qkT, 1)

            def stile(c):
                for g4 in range(4):
                    bk = BMAP[g4]
                    if g4 < 2:
                        t.wait_ge(ss_qkTevac, 2 * c + g4 + 1)
                    else:
                        t.wait_ge(sv_qkTq, 2 * c + g4 - 1)
                    t.wait_ge(sv_qkTek, 4 * c + g4 + 1)
                    for i in range(4):
                        h = g4 * 4 + i
                        ins = t.matmul(pb[bk][0:C, i * C:(i + 1) * C],
                                       qkT[:, (H + h) * C:(H + h + 1) * C],
                                       qkT[:, h * C:(h + 1) * C],
                                       start=(i == 0), stop=(i == 3))
                    ins.then_inc(st_stile, 1)

            def outkv(c):
                t.wait_ge(ss_kvcast, c)
                t.wait_ge(ss_vevac, c + 1)
                t.wait_ge(sv_qkevac, 2 * (c + 1))
                t.wait_ge(sv_qkTek, 4 * (c + 1))
                if c >= 1:
                    t.wait_ge(ss_partial, c)
                    t.wait_ge(sv_kvadd, c)
                for b in range(4):
                    t.wait_ge(sv_smevac, 4 * c + b + 1)
                    for i in range(4):
                        h = 4 * b + i
                        slot = i * C
                        t.matmul(pb[4 + b][0:C, slot:slot + 66],
                                 smb[b][:, i * C:(i + 1) * C],
                                 vpp[c % 2][:, h * 66:(h + 1) * 66],
                                 start=(i == 0), stop=False)
                        ins = t.matmul(pb[4 + b][0:C, slot:slot + 66],
                                       qkT[:, h * C:(h + 1) * C],
                                       kv_bf[:, h * 66:(h + 1) * 66],
                                       start=False, stop=(i == 3))
                    if b == 3:
                        for h in range(16):
                            ins = t.matmul(pb[7][0:DH, 450 + h * 2:450 + (h + 1) * 2],
                                           qkp[c % 2][:, D + h * DH: D + (h + 1) * DH],
                                           vpp[c % 2][:, h * 66 + 64:(h + 1) * 66],
                                           start=(h == 0), stop=(h == 15))
                    ins.then_inc(st_attnout, 1)
                t.wait_ge(sv_smevac, 4 * (c + 1))
                for h in range(8):
                    t.matmul(pb[2][0:DH, h * 64:(h + 1) * 64],
                             qkp[c % 2][:, D + h * DH: D + (h + 1) * DH],
                             vpp[c % 2][:, h * 66:h * 66 + 64],
                             start=(h == 0), stop=(h == 7))
                for h in range(8, 16):
                    ins = t.matmul(pb[3][0:DH, (h - 8) * 64:(h - 7) * 64],
                                   qkp[c % 2][:, D + h * DH: D + (h + 1) * DH],
                                   vpp[c % 2][:, h * 66:h * 66 + 64],
                                   start=(h == 8), stop=(h == 15))
                ins.then_inc(st_kv, 1)

            def attn_tail(c):
                t.wait_ge(sv_divevac, 4 * (c + 1))
                for i in range(4):
                    ins = t.matmul(pb[4][0:C, i * C:(i + 1) * C],
                                   attn_sb[:, i * C:(i + 1) * C], id_bf[:],
                                   start=(i == 0), stop=(i == 3))
                ins.then_inc(st_attnT, 1)
                for i in range(4):
                    ins = t.matmul(pb[5][0:C, i * C:(i + 1) * C],
                                   attn_sb[:, (4 + i) * C:(5 + i) * C], id_bf[:],
                                   start=(i == 0), stop=(i == 3))
                ins.then_inc(st_attnT, 1)
                t.wait_ge(sv_m1d, c + 1)
                t.matmul(pb[6][0:H, 0:C], m1d_bf[:, 0:H], id_bf[:],
                         start=True, stop=True).then_inc(st_attnT, 1)
                t.wait_ge(ss_attnTevac, 3 * c + 1)
                for j in range(4):
                    t.matmul(pb[4][0:C, 0:512], attnT_sb[:, j * C:(j + 1) * C],
                             wout_sb[:, j * D: j * D + 512], start=(j == 0), stop=False)
                t.wait_ge(ss_attnTevac, 3 * c + 2)
                for j in range(4, 8):
                    t.matmul(pb[4][0:C, 0:512], attnT_sb[:, j * C:(j + 1) * C],
                             wout_sb[:, j * D: j * D + 512], start=False, stop=False)
                for j in range(8):
                    t.matmul(pb[5][0:C, 0:512], attnT_sb[:, j * C:(j + 1) * C],
                             wout_sb[:, j * D + 512: (j + 1) * D], start=(j == 0), stop=False)
                t.matmul(pb[4][0:C, 0:512], ones_row[0:1, :], bias_row[0:1, 0:512],
                         start=False, stop=False)
                t.matmul(pb[5][0:C, 0:512], ones_row[0:1, :], bias_row[0:1, 512:1024],
                         start=False, stop=False)
                t.wait_ge(ss_attnTevac, 3 * (c + 1))
                t.matmul(pb[4][0:C, 0:512], m1dT[0:H, :], posG[0:H, 0:512],
                         start=False, stop=True)
                t.matmul(pb[5][0:C, 0:512], m1dT[0:H, :], posG[0:H, 512:1024],
                         start=False, stop=True).then_inc(st_outproj, 1)

            for c in range(NCH):
                wave(c, 0)
                if c >= 2:
                    attn_tail(c - 2)
                if c >= 1:
                    transposes(c - 1)
                wave(c, 1)
                if c >= 1:
                    stile(c - 1)
                wave(c, 2)
                if c >= 1:
                    outkv(c - 1)
            attn_tail(NCH - 2)
            transposes(NCH - 1)
            stile(NCH - 1)
            outkv(NCH - 1)
            attn_tail(NCH - 1)

            for c in range(NCH):
                p = c % 2
                t.wait_ge(ss_xq, c + 1)
                for gx in range(2):
                    bk = 6 + gx
                    if c == 0 and gx == 0:
                        t.wait_ge(sv_divevac, 4 * NCH)
                        t.wait_ge(sv_kvadd, NCH)
                    if c >= 1:
                        t.wait_ge(ss_xqTevac, 2 * c - 1 + gx)
                    for i in range(4):
                        j = gx * 4 + i
                        ins = t.matmul(pb[bk][0:C, i * C:(i + 1) * C],
                                       xTa[p][:, j * C:(j + 1) * C], id_bf[:],
                                       start=(i == 0), stop=(i == 3))
                    ins.then_inc(st_xqT, 1)
                if c == 0:
                    t.wait_ge(ss_vevac, NCH)
                else:
                    t.wait_ge(sv_o1, c)
                t.wait_ge(ss_xqTevac, 2 * (c + 1))
                for kc in range(8):
                    t.matmul(pb[0][0:C, 0:512], xqT[:, kc * C:(kc + 1) * C],
                             wffnT_sb[:, kc * D: kc * D + 512],
                             start=(kc == 0), stop=(kc == 7))
                    ins = t.matmul(pb[1][0:C, 0:512], xqT[:, kc * C:(kc + 1) * C],
                                   wffnT_sb[:, kc * D + 512: (kc + 1) * D],
                                   start=(kc == 0), stop=(kc == 7))
                ins.then_inc(st_ffn, 1)

        @block.vector
        def _(v):
            v.wait_ge(ss_i, 20)
            v.wait_ge(sg_i, 1)
            for fc in range(8):
                v.tensor_scalar(dv_sb[:, fc * H + 2 * fc: fc * H + 2 * fc + 1],
                                ind_lo[:, 0:1], swv_cols[:, fc:fc + 1], None, ALU.mult)
                v.tensor_scalar(dv_sb[:, fc * H + 2 * fc + 1: fc * H + 2 * fc + 2],
                                ind_hi[:, 0:1], swv_cols[:, fc:fc + 1], None, ALU.mult)
            v.drain()
            v.sem_inc(sv_i, 1)
            v.wait_ge(ss_i, IS_TOTAL)
            v.wait_ge(dsm, 16 * 5)

            def stats(c):
                p = c % 2
                v.wait_ge(dxs[p], 16 * (c // 2 + 1))
                if c >= 2:
                    v.wait_ge(ss_stats, c - 1)
                v.drain()
                v.bn_stats(bn6[:, 0:6], xa[p][:, 0:512])
                v.bn_stats(bn6[:, 6:12], xa[p][:, 512:1024])
                v.drain()
                v.bn_aggr(bn2[p][:], bn6[:])
                v.tensor_reduce(maxx_all[:, c:c + 1], xa[p][:], axis=AX, op=ALU.max)
                v.tensor_reduce(minx_all[:, c:c + 1], xa[p][:], axis=AX, op=ALU.min)
                v.drain()
                v.tensor_scalar(mean_all[:, c:c + 1], bn2[p][:, 0:1], -1.0, None, ALU.mult)
                v.drain()
                v.sem_inc(sv_stats, 1)

            def qk_evac(c, half):
                v.wait_ge(st_qkv, 3 * c + half + 1)
                v.drain()
                for b in range(2):
                    v.scalar_tensor_tensor(qkln[:, half * D + b * 512:half * D + (b + 1) * 512],
                                           swqk_bc[:, half * D + b * 512:half * D + (b + 1) * 512],
                                           mean_all[:, c:c + 1],
                                           pb[b][0:C, 0:512],
                                           ALU.mult, ALU.add)
                v.drain()
                v.sem_inc(sv_qkevac, 1)

            def rope_half(c, half):
                off = half * D
                if half == 0:
                    v.wait_ge(ss_stats, c + 1)
                if c >= 2:
                    v.wait_ge(st_qkT, 8 * (c - 2) + 8)
                    v.wait_ge(st_kv, c - 1)
                cosap = bass.AP(costab[:, :].tensor, c * DH, [[NCH * DH, C], [0, 16], [1, DH]])
                sinap = bass.AP(sintab[:, :].tensor, c * DH,
                                [[NCH * DH, C], [0, 16], [32, 2], [1, 32]])
                qk3 = qkln[:, off:off + D].rearrange("p (g n) -> p g n", g=16)
                v.tensor_tensor(rt1[:, off:off + D].rearrange("p (g n) -> p g n", g=16),
                                qk3, cosap, ALU.mult)
                gat = bass.AP(qkln[:, :].tensor, off + 32,
                              [[2 * D, C], [DH, 16], [-32, 2], [1, 32]])
                v.tensor_tensor(rt2[:, off:off + D].rearrange("p (g r n) -> p g r n", g=16, r=2),
                                gat, sinap, ALU.mult)
                v.drain()
                v.tensor_add(qkp[c % 2][:, off:off + D], rt1[:, off:off + D], rt2[:, off:off + D])
                v.drain()
                v.sem_inc(sv_rope, 1)
                v.tensor_scalar(rt1[:, off:off + D], qkp[c % 2][:, off:off + D],
                                srst_all[:, c:c + 1], 0.0, ALU.mult, ALU.max)

            def elu_half(c, half):
                off = half * D
                v.wait_ge(ss_exp, 2 * c + half + 1)
                v.drain()
                v.scalar_tensor_tensor(qkp[c % 2][:, off:off + D], rt2[:, off:off + D], 1.0,
                                       rt1[:, off:off + D], ALU.min, ALU.add)
                v.drain()
                v.sem_inc(sv_elu, 1)

            def qkT_evacs_q23(c):
                for g8 in range(2, 4):
                    v.wait_ge(st_qkT, 8 * c + g8 + 1)
                    v.drain()
                    v.tensor_copy(qkT[0:DH, g8 * 512:(g8 + 1) * 512],
                                  pb[[6, 7][g8 - 2]][0:DH, 0:512])
                    v.drain()
                    v.sem_inc(sv_qkTq, 1)

            def qkT_evacs_k(c):
                for g8 in range(4, 8):
                    v.wait_ge(st_qkT, 8 * c + g8 + 1)
                    v.drain()
                    v.tensor_copy(qkT[0:DH, g8 * 512:(g8 + 1) * 512],
                                  pb[[2, 3, 6, 7][g8 - 4]][0:DH, 0:512])
                    v.drain()
                    v.sem_inc(sv_qkTek, 1)

            def mask_evacs(c, lo, hi):
                for g4 in range(lo, hi):
                    v.wait_ge(st_stile, 4 * c + g4 + 1)
                    if c >= 1:
                        v.wait_ge(st_attnout, 4 * (c - 1) + g4 + 1)
                    v.drain()
                    mk = bass.AP(maskT_sb[:, :].tensor, 0, [[C, C], [0, 4], [1, C]])
                    v.tensor_tensor(smb[g4][:, :].rearrange("p (i n) -> p i n", i=4),
                                    pb[[2, 3, 6, 7][g4]][0:C, 0:512].rearrange("p (i n) -> p i n", i=4),
                                    mk, ALU.mult)
                    v.drain()
                    v.sem_inc(sv_smevac, 1)

            def div_evacs(c):
                for b in range(4):
                    v.wait_ge(st_attnout, 4 * c + b + 1)
                    v.drain()
                    denap = bass.AP(pb[4 + b][:, :].tensor, 65, [[512, C], [C, 4]])
                    v.tensor_scalar(dent[:, 0:4], denap, 1e-6, None, ALU.max)
                    v.drain()
                    v.reciprocal(rcp[:, 0:4], dent[:, 0:4])
                    v.drain()
                    numap = bass.AP(pb[4 + b][:, :].tensor, 0, [[512, C], [C, 4], [1, 64]])
                    rcpb = bass.AP(rcp[:, :].tensor, 0, [[4, C], [1, 4], [0, 64]])
                    v.tensor_tensor(attn_sb[:, b * 256:(b + 1) * 256].rearrange("p (i n) -> p i n", i=4),
                                    numap, rcpb, ALU.mult)
                    m1ap = bass.AP(pb[4 + b][:, :].tensor, 64, [[512, C], [C, 4]])
                    v.tensor_tensor(m1d[:, 4 * b:4 * b + 4], m1ap, rcp[:, 0:4], ALU.mult)
                    v.drain()
                    v.sem_inc(sv_divevac, 1)
                v.drain()
                v.tensor_copy(m1d_bf[:], m1d[:])
                v.drain()
                v.sem_inc(sv_m1d, 1)

            def kv_add(c):
                v.wait_ge(st_kv, c + 1)
                if c >= 1:
                    v.wait_ge(ss_kvcast, c)
                v.drain()
                for half in range(2):
                    dst = bass.AP(kvf[(c + 1) % 2][:, :].tensor, half * 8 * 66,
                                  [[H * 66, DH], [66, 8], [1, 64]])
                    srcf = bass.AP(kvf[c % 2][:, :].tensor, half * 8 * 66,
                                   [[H * 66, DH], [66, 8], [1, 64]])
                    v.tensor_tensor(dst, srcf,
                                    pb[2 + half][0:DH, 0:512].rearrange("p (h n) -> p h n", h=8),
                                    ALU.add)
                dste = bass.AP(kvf[(c + 1) % 2][:, :].tensor, 64, [[H * 66, DH], [66, H], [1, 2]])
                srce = bass.AP(kvf[c % 2][:, :].tensor, 64, [[H * 66, DH], [66, H], [1, 2]])
                v.tensor_tensor(dste, srce,
                                pb[7][0:DH, 450:482].rearrange("p (h n) -> p h n", h=H),
                                ALU.add)
                v.drain()
                v.sem_inc(sv_kvadd, 1)

            stats(0)
            for c in range(NCH):
                qk_evac(c, 0)
                rope_half(c, 0)
                if c == NCH - 1:
                    v.wait_ge(ss_stats, NCH)
                    v.drain()
                    v.tensor_add(tmp1c[:], maxx_all[:], mean_all[:])
                    v.tensor_add(tmp2c[:], minx_all[:], mean_all[:])
                    v.drain()
                    v.tensor_scalar(minx_all[:], tmp2c[:], -1.0, None, ALU.mult)
                    v.drain()
                    v.tensor_tensor(tmp2c[:], tmp1c[:], minx_all[:], ALU.max)
                    v.drain()
                    v.tensor_mul(tmp1c[:], tmp2c[:], rstd_all[:])
                    v.drain()
                    v.reduce_max(locmax[:], tmp1c[:], axis=AX)
                    v.drain()
                    v.sem_inc(sv_locmax, 1)
                if c >= 1:
                    qkT_evacs_q23(c - 1)
                    qkT_evacs_k(c - 1)
                qk_evac(c, 1)
                if c >= 1:
                    mask_evacs(c - 1, 0, 4)
                rope_half(c, 1)
                elu_half(c, 0)
                elu_half(c, 1)
                if c >= 1:
                    div_evacs(c - 1)
                    kv_add(c - 1)
                if c + 1 < NCH:
                    stats(c + 1)
            qkT_evacs_q23(NCH - 1)
            qkT_evacs_k(NCH - 1)
            mask_evacs(NCH - 1, 0, 4)
            div_evacs(NCH - 1)
            kv_add(NCH - 1)

            v.wait_ge(dcc, 32)
            v.drain()
            v.reduce_max(g11[0:1, 0:1], grow[0:1, :], axis=AX)
            v.drain()
            v.sem_inc(sv_g11, 1)
            v.wait_ge(sg_cc, 1)
            v.drain()
            v.tensor_scalar(xsc[:], gmax_bc[:], INV127, 1e-5, ALU.mult, ALU.max)
            v.drain()
            v.reciprocal(rs_col[:], xsc[:])
            v.drain()
            v.tensor_scalar(sws_bc[:], ws_bc[:], xsc[:, 0:1], None, ALU.mult)
            v.drain()
            v.sem_inc(sv_xsc, 1)

            for c in range(NCH):
                p = c % 2
                if c >= 2:
                    v.wait_ge(ss_z, c - 1)
                v.drain()
                v.tensor_scalar(a_col[p][:], rstd_all[:, c:c + 1], rs_col[:, 0:1], None, ALU.mult)
                v.drain()
                v.tensor_scalar(b_col[p][:], mean_all[:, c:c + 1], a_col[p][:, 0:1], None, ALU.mult)
                v.drain()
                v.sem_inc(sv_ab, 1)
                v.wait_ge(ss_z, c + 1)
                if c >= 2:
                    v.wait_ge(ss_xq, c - 1)
                v.drain()
                v.tensor_scalar(tib[p], ztb[p], CM, -CM, ALU.add, ALU.add)
                v.drain()
                v.sem_inc(sv_round, 1)
                if c >= 1:
                    v.wait_ge(st_ffn, c)
                    v.drain()
                    v.tensor_mul(o1t[:, 0:512], pb[0][0:C, 0:512], sws_bc[:, 0:512])
                    v.tensor_mul(o1t[:, 512:1024], pb[1][0:C, 0:512], sws_bc[:, 512:1024])
                    v.drain()
                    v.sem_inc(sv_o1, 1)
                    if c >= 2:
                        v.wait_ge(sg_o2, c - 1)
                    v.wait_ge(dpb4[(c - 1) % 4], 16 * ((c - 1) // 4 + 1))
                    v.drain()
                    v.tensor_tensor(g2t[:], o1t[:], xa4[(c - 1) % 4], ALU.add)
                    v.drain()
                    v.sem_inc(sv_g2t, 1)

            def o1g2t_last():
                v.wait_ge(st_ffn, NCH)
                v.drain()
                v.tensor_mul(o1t[:, 0:512], pb[0][0:C, 0:512], sws_bc[:, 0:512])
                v.tensor_mul(o1t[:, 512:1024], pb[1][0:C, 0:512], sws_bc[:, 512:1024])
                v.drain()
                v.sem_inc(sv_o1, 1)
                v.wait_ge(sg_o2, NCH - 1)
                v.wait_ge(dpb4[(NCH - 1) % 4], 16 * ((NCH - 1) // 4 + 1))
                v.drain()
                v.tensor_tensor(g2t[:], o1t[:], xa4[(NCH - 1) % 4], ALU.add)
                v.drain()
                v.sem_inc(sv_g2t, 1)
            o1g2t_last()

        @block.scalar
        def _(s):
            k = 0
            for nj in range(8):
                for grp in range(2):
                    k += 1
                    s.wait_ge(st_i, k)
                    dst = wffnT_sb[:, :].rearrange("p (kc n) -> p kc n", kc=8)[:, grp * 4:(grp + 1) * 4,
                                                                              nj * C:(nj + 1) * C]
                    s.copy(dst, pb[6 + grp][0:C, 0:512].rearrange("p (i n) -> p i n", i=4))
                    s.drain()
                    s.sem_inc(ss_i, 1)
            s.wait_ge(st_i, 17)
            for obk in range(4):
                s.copy(swqk_row[0:1, obk * 512:(obk + 1) * 512], pb[obk][0:1, 0:512])
            s.drain()
            s.sem_inc(ss_i, 1)
            s.wait_ge(st_i, 18)
            for obk in range(4):
                s.copy(swqk_bc[:, obk * 512:(obk + 1) * 512], pb[obk][0:C, 0:512])
            s.drain()
            s.sem_inc(ss_i, 1)
            s.wait_ge(st_i, 19)
            for ov in range(2):
                s.copy(swv_row[0:1, ov * 512:(ov + 1) * 512], pb[4 + ov][0:1, 0:512])
            s.drain()
            s.sem_inc(ss_i, 1)
            s.wait_ge(st_i, 20)
            s.copy(swv_cols[:, 0:8], pb[4][0:C, 0:8])
            s.drain()
            s.sem_inc(ss_i, 1)
            s.wait_ge(st_i, 21)
            for obk in range(2):
                s.copy(ws_bc[:, obk * 512:(obk + 1) * 512], pb[obk][0:C, 0:512])
            s.drain()
            s.sem_inc(ss_i, 1)
            s.wait_ge(st_i, 22)
            for ov in range(2):
                s.copy(posG[0:H, ov * 512:(ov + 1) * 512], pb[6 + ov][0:H, 0:512])
            s.drain()
            s.sem_inc(ss_i, 1)

            def rstd(c):
                p = c % 2
                s.wait_ge(sv_stats, c + 1)
                s.drain()
                s.activation(s_ve[p][:], bn2[p][:, 1:2], ACTF.Identity, bias=epsln[:, 0:1])
                s.drain()
                s.activation(s_tt[p][:], s_ve[p][:], ACTF.Ln)
                s.drain()
                s.activation(s_sd[p][:], s_tt[p][:], ACTF.Exp, scale=neghalf[:, 0:1])
                s.drain()
                s.activation(s_r2[p][:], s_sd[p][:], ACTF.Square)
                s.drain()
                s.mul(s_tt[p][:], s_ve[p][:], s_r2[p][:, 0:1])
                s.drain()
                s.activation(s_uu[p][:], s_tt[p][:], ACTF.Identity,
                             scale=neghalf[:, 0:1], bias=c15[:, 0:1])
                s.drain()
                s.mul(rstd_all[:, c:c + 1], s_sd[p][:], s_uu[p][:, 0:1])
                s.drain()
                s.activation(srst_all[:, c:c + 1], rstd_all[:, c:c + 1], ACTF.Identity,
                             scale=scsc[:, 0:1])
                s.drain()
                s.sem_inc(ss_stats, 1)

            def vevac(c):
                s.wait_ge(st_qkv, 3 * c + 3)
                s.drain()
                for ov in range(2):
                    dst = bass.AP(vpp[c % 2][:, :].tensor, ov * 8 * 66, [[H * 66, C], [66, 8], [1, 64]])
                    s.activation(dst, pb[ov][0:C, 0:512].rearrange("p (h n) -> p h n", h=8),
                                 ACTF.Identity, scale=rstd_all[:, c:c + 1])
                s.drain()
                rmdst = bass.AP(vpp[c % 2][:, :].tensor, 64, [[H * 66, C], [66, H]])
                negm_b = bass.AP(mean_all[:, :].tensor, c, [[NCH, C], [0, H]])
                s.activation(rmdst, negm_b, ACTF.Identity, scale=rstd_all[:, c:c + 1])
                s.drain()
                s.sem_inc(ss_vevac, 1)

            def exps_half(c, half):
                off = half * D
                s.wait_ge(sv_rope, 2 * c + half + 1)
                s.drain()
                s.activation(rt2[:, off:off + D], qkp[c % 2][:, off:off + D], ACTF.Exp,
                             scale=srst_all[:, c:c + 1])
                s.drain()
                s.sem_inc(ss_exp, 1)

            def qkT_evacs(c):
                for g8 in range(2):
                    s.wait_ge(st_qkT, 8 * c + g8 + 1)
                    s.drain()
                    s.copy(qkT[0:DH, g8 * 512:(g8 + 1) * 512], pb[[2, 3][g8]][0:DH, 0:512])
                    s.drain()
                    s.sem_inc(ss_qkTevac, 1)

            def attnT_evacs(c):
                for gx in range(2):
                    s.wait_ge(st_attnT, 3 * c + gx + 1)
                    s.drain()
                    s.copy(attnT_sb[:, gx * 512:(gx + 1) * 512], pb[4 + gx][0:C, 0:512])
                    s.drain()
                    s.sem_inc(ss_attnTevac, 1)
                s.wait_ge(st_attnT, 3 * c + 3)
                s.drain()
                s.copy(m1dT[0:H, :], pb[6][0:H, 0:C])
                s.drain()
                s.sem_inc(ss_attnTevac, 1)

            def partial_evac(c):
                s.wait_ge(st_outproj, c + 1)
                if c >= 2:
                    s.wait_ge(dps[c % 2], 16 * (c // 2))
                s.drain()
                s.copy(partial[c % 2][:, 0:512], pb[4][0:C, 0:512])
                s.copy(partial[c % 2][:, 512:1024], pb[5][0:C, 0:512])
                s.drain()
                s.sem_inc(ss_partial, 1)

            for c in range(NCH):
                if c >= 2:
                    attnT_evacs(c - 2)
                rstd(c)
                if c >= 1:
                    qkT_evacs(c - 1)
                if c >= 2:
                    partial_evac(c - 2)
                exps_half(c, 0)
                exps_half(c, 1)
                vevac(c)
            attnT_evacs(NCH - 2)
            partial_evac(NCH - 2)
            qkT_evacs(NCH - 1)
            attnT_evacs(NCH - 1)
            partial_evac(NCH - 1)

            s.wait_ge(st_qkv, 3 * NCH)
            s.wait_ge(dps[0], 16 * (NCH // 2))
            s.wait_ge(dps[1], 16 * (NCH // 2))
            for c in range(NCH):
                p = c % 2
                s.wait_ge(sv_ab, c + 1)
                s.wait_ge(dpb4[c % 4], 16 * (c // 4 + 1))
                if c >= 2:
                    s.wait_ge(sv_round, c - 1)
                s.drain()
                s.activation(ztb[p], xa4[c % 4], ACTF.Identity,
                             scale=a_col[p][:, 0:1], bias=b_col[p][:, 0:1])
                s.drain()
                s.sem_inc(ss_z, 1)
                s.wait_ge(sv_round, c + 1)
                if c >= 2:
                    s.wait_ge(st_xqT, 2 * (c - 1))
                s.drain()
                s.copy(xTa[p][:], tib[p])
                s.drain()
                s.sem_inc(ss_xq, 1)
                for gx in range(2):
                    s.wait_ge(st_xqT, 2 * c + gx + 1)
                    s.drain()
                    s.copy(xqT[:, gx * 512:(gx + 1) * 512], pb[6 + gx][0:C, 0:512])
                    s.drain()
                    s.sem_inc(ss_xqTevac, 1)

    return nc


def _host_tables():
    half = DH // 2
    inv = (np.float32(1.0) / (np.float32(10000.0) **
           (np.arange(half, dtype=np.float32) / np.float32(half)))).astype(np.float32)
    pos = np.arange(T_TOK, dtype=np.float32)
    fr = pos[:, None] * inv[None, :]
    co = np.cos(fr).astype(np.float32)
    si = np.sin(fr).astype(np.float32)
    cost = np.zeros((C, NCH * DH), np.float32)
    sint = np.zeros((C, NCH * DH), np.float32)
    for c in range(NCH):
        rows = slice(c * C, (c + 1) * C)
        cost[:, c * DH:c * DH + half] = co[rows, :]
        cost[:, c * DH + half:(c + 1) * DH] = co[rows, :]
        sint[:, c * DH:c * DH + half] = -si[rows, :]
        sint[:, c * DH + half:(c + 1) * DH] = si[rows, :]
    maskT = np.triu(np.ones((C, C), np.float32))
    eye = np.eye(C, dtype=np.float32)
    return cost, sint, maskT, eye


def kernel(L0_attn_x, L0_attn_w_qkv, L0_attn_w_out, L0_ffn_weight_int8,
           L0_ffn_weight_scale, L0_ffn_bias):
    x = np.ascontiguousarray(np.asarray(L0_attn_x, dtype=np.float32))
    wqkv = np.ascontiguousarray(np.asarray(L0_attn_w_qkv, dtype=np.float32))
    wout = np.ascontiguousarray(np.asarray(L0_attn_w_out, dtype=np.float32))
    wi8 = np.ascontiguousarray(np.asarray(L0_ffn_weight_int8, dtype=np.int8))
    wsc = np.ascontiguousarray(np.asarray(L0_ffn_weight_scale, dtype=np.float32).reshape(1, D))
    bias = np.ascontiguousarray(np.asarray(L0_ffn_bias, dtype=np.float32).reshape(1, D))
    cost, sint, maskT, eye = _host_tables()

    nc = build_nc()
    in_maps = []
    for b in range(8):
        xb = np.ascontiguousarray(x[b])
        in_maps.append({
            "x": xb,
            "xT": np.ascontiguousarray(xb.T),
            "wqkv": wqkv,
            "wout": wout,
            "wi8": wi8,
            "wsc": wsc,
            "bias": bias,
            "cost": cost,
            "sint": sint,
            "maskT": maskT,
            "eye": eye,
        })
    res = run_bass_kernel_spmd(nc, in_maps, core_ids=list(range(8)))
    out = np.stack([np.asarray(r["out"]) for r in res.results], axis=0)
    return out.astype(np.float32)


# version 19
# speedup vs baseline: 1.0356x; 1.0356x over previous
import numpy as np
import ml_dtypes
import concourse.bass as bass
import concourse.mybir as mybir
from concourse.bass_utils import run_bass_kernel_spmd
from contextlib import ExitStack

F32 = mybir.dt.float32
BF16 = mybir.dt.bfloat16
I8 = mybir.dt.int8
I32 = mybir.dt.int32
AX = mybir.AxisListType.X
ALU = mybir.AluOpType
ACTF = mybir.ActivationFunctionType

T_TOK = 4096
D = 1024
H = 16
DH = 64
C = 128
NCH = T_TOK // C
SCALE = float(DH ** (-0.25))
CM = 12582912.0
INV127 = float(np.float32(1.0) / np.float32(127.0))


def build_nc():
    nc = bass.Bass("TRN2", target_bir_lowering=False)

    x_hbm = nc.dram_tensor("x", [T_TOK, D], F32, kind="ExternalInput").ap()
    xT_hbm = nc.dram_tensor("xT", [D, T_TOK], F32, kind="ExternalInput").ap()
    wqkv_hbm = nc.dram_tensor("wqkv", [D, 3 * D], F32, kind="ExternalInput").ap()
    wout_hbm = nc.dram_tensor("wout", [D, D], F32, kind="ExternalInput").ap()
    wi8_hbm = nc.dram_tensor("wi8", [D, D], I8, kind="ExternalInput").ap()
    wsc_hbm = nc.dram_tensor("wsc", [1, D], F32, kind="ExternalInput").ap()
    bias_hbm = nc.dram_tensor("bias", [1, D], F32, kind="ExternalInput").ap()
    cos_hbm = nc.dram_tensor("cost", [C, NCH * DH], F32, kind="ExternalInput").ap()
    sin_hbm = nc.dram_tensor("sint", [C, NCH * DH], F32, kind="ExternalInput").ap()
    mask_hbm = nc.dram_tensor("maskT", [C, C], F32, kind="ExternalInput").ap()
    eye_hbm = nc.dram_tensor("eye", [C, C], F32, kind="ExternalInput").ap()
    out_hbm = nc.dram_tensor("out", [T_TOK, D], F32, kind="ExternalOutput").ap()
    part_hbm = nc.dram_tensor("part_scratch", [T_TOK, D], BF16).ap()
    cc_in = nc.dram_tensor("cc_in", [C, 1], F32)
    cc_out = nc.dram_tensor("cc_out", [C, 1], F32)
    gm_dram = nc.dram_tensor("gm_dram", [1, 1], F32)

    ctx = ExitStack()
    sb = lambda name, sh, dt: ctx.enter_context(nc.sbuf_tensor(name, sh, dt))
    psum = lambda name, sh: ctx.enter_context(nc.psum_tensor(name, sh, F32))
    sem = lambda name: ctx.enter_context(nc.semaphore(name))

    with ctx:
        wqkv_sb = sb("wqkv_sb", [C, 8 * 3 * D], BF16)
        wout_sb = sb("wout_sb", [C, 8 * D], BF16)
        wffnT_sb = sb("wffnT_sb", [C, 8 * D], BF16)
        swqk_bc = sb("swqk_bc", [C, 2 * D], BF16)
        ws_bc = sb("ws_bc", [C, D], F32)
        sws_bc = sb("sws_bc", [C, D], F32)
        bias_row = sb("bias_row", [1, D], BF16)
        wsc_row = sb("wsc_row", [1, D], F32)
        swqk_row = sb("swqk_row", [1, 2 * D], BF16)
        swv_row = sb("swv_row", [1, D], F32)
        swv_cols = sb("swv_cols", [C, 8], F32)
        dv_sb = sb("dv_sb", [C, 8 * H], BF16)
        posG = sb("posG", [H, D], BF16)
        costab = sb("costab", [C, NCH * DH], BF16)
        sintab = sb("sintab", [C, NCH * DH], BF16)
        maskT_sb = sb("maskT_sb", [C, C], BF16)
        id_bf = sb("id_bf", [C, C], BF16)
        ones_col = sb("ones_col", [C, 1], BF16)
        ones_row = sb("ones_row", [1, C], BF16)
        ones_row_f = sb("ones_row_f", [1, C], F32)
        ones11_f = sb("ones11_f", [1, 1], F32)
        ind_lo = sb("ind_lo", [C, 1], F32)
        ind_hi = sb("ind_hi", [C, 1], F32)
        wi8_stage = sb("wi8_stage", [C, D], BF16)
        epsln = sb("epsln", [C, 1], F32)
        neghalf = sb("neghalf", [C, 1], F32)
        c15 = sb("c15", [C, 1], F32)
        scsc = sb("scsc", [C, 1], F32)

        xa = [sb(f"xa{i}", [C, D], F32) for i in range(2)]
        xTa = [sb(f"xTa{i}", [C, D], BF16) for i in range(2)]
        qkln = sb("qkln", [C, 2 * D], BF16)
        rt1 = sb("rt1", [C, 2 * D], BF16)
        rt2 = sb("rt2", [C, 2 * D], BF16)
        qkp = [sb(f"qkp{i}", [C, 2 * D], BF16) for i in range(2)]
        qkT = sb("qkT", [DH, 2 * H * C], BF16)
        vpp = [sb(f"vpp{i}", [C, H * 66], BF16) for i in range(2)]
        smb2 = [sb(f"smb{i}", [C, 4 * C], BF16) for i in range(2)]
        smb = [smb2[0][:, :], smb2[1][:, :],
               wi8_stage[:, 0:512], wi8_stage[:, 512:1024]]
        attn_sb = sb("attn_sb", [C, D], BF16)
        attnT_sb = sb("attnT_sb", [C, D], BF16)
        m1d = sb("m1d", [C, H], F32)
        m1d_bf = sb("m1d_bf", [C, H], BF16)
        m1dT = sb("m1dT", [H, C], BF16)
        kvf = [sb(f"kvf{i}", [DH, H * 66], F32) for i in range(2)]
        kv_bf = sb("kv_bf", [DH, H * 66], BF16)
        partial = [sb(f"partial{i}", [C, D], BF16) for i in range(2)]

        bn6 = sb("bn6", [C, 12], F32)
        bn2 = [sb(f"bn2_{i}", [C, 2], F32) for i in range(2)]
        mean_all = sb("mean_all", [C, NCH], F32)
        rstd_all = sb("rstd_all", [C, NCH], F32)
        srst_all = sb("srst_all", [C, NCH], F32)
        maxx_all = sb("maxx_all", [C, NCH], F32)
        minx_all = sb("minx_all", [C, NCH], F32)
        tmp1c = sb("tmp1c", [C, NCH], F32)
        tmp2c = sb("tmp2c", [C, NCH], F32)
        s_ve = [sb(f"s_ve{i}", [C, 1], F32) for i in range(2)]
        s_r2 = [sb(f"s_r2{i}", [C, 1], F32) for i in range(2)]
        s_tt = [sb(f"s_tt{i}", [C, 1], F32) for i in range(2)]
        s_uu = [sb(f"s_uu{i}", [C, 1], F32) for i in range(2)]
        s_sd = [sb(f"s_sd{i}", [C, 1], F32) for i in range(2)]
        dent = sb("dent", [C, 4], F32)
        rcp = sb("rcp", [C, 4], F32)
        locmax = sb("locmax", [C, 1], F32)
        grow = sb("grow", [1, C], F32)
        g11 = sb("g11", [1, 1], F32)
        gmax_bc = sb("gmax_bc", [C, 1], F32)
        xsc = sb("xsc", [C, 1], F32)
        rs_col = sb("rs_col", [C, 1], F32)
        a_col = [sb(f"a_col{i}", [C, 1], F32) for i in range(2)]
        b_col = [sb(f"b_col{i}", [C, 1], F32) for i in range(2)]

        ti32 = sb("ti32", [C, D], I32)
        pb_sb = [sb(f"pb_sb{i}", [C, D], BF16) for i in range(2)]
        ob = [sb(f"ob{i}", [C, D], F32) for i in range(2)]

        pb = [psum(f"pb{i}", [C, 512]) for i in range(8)]

        o1t = qkln[:, :].bitcast(F32)
        g2t = costab[:, :].bitcast(F32)
        xqT = rt2
        ztb = [rt1[:, :].bitcast(F32), sintab[:, :].bitcast(F32)]
        tib = [ti32[:, :], wqkv_sb[:, 0:2048].bitcast(I32)]
        xa4 = [xa[0][:, :], xa[1][:, :],
               qkp[0][:, :].bitcast(F32), qkp[1][:, :].bitcast(F32)]
        pb4_sb = [pb_sb[0][:, :], pb_sb[1][:, :], attn_sb[:, :], attnT_sb[:, :]]
        wi8_i8 = qkp[0][:, :].bitcast(I8)

        dxs = [sem("dx0"), sem("dx1")]
        dxts = [sem("dxt0"), sem("dxt1")]
        dwin = sem("dwin")
        dsm = sem("dsm")
        dwi8 = sem("dwi8")
        dsin = sem("dsin")
        dps = [sem("dps0"), sem("dps1")]
        dpl = [sem("dpl0"), sem("dpl1")]
        dpb4 = [sem(f"dpb4_{i}") for i in range(4)]
        dpl4 = [sem(f"dpl4_{i}") for i in range(4)]
        sv_g2t = sem("sv_g2t")
        dwr = [sem("dwr0"), sem("dwr1")]
        sv_wc = sem("sv_wc")
        dos = [sem("dos0"), sem("dos1")]
        dcc = sem("dcc")
        sg_i = sem("sg_i")
        st_i = sem("st_i")
        ss_i = sem("ss_i")
        sv_i = sem("sv_i")
        sv_stats = sem("sv_stats")
        ss_stats = sem("ss_stats")
        st_qkv = sem("st_qkv")
        sv_qkevac = sem("sv_qkevac")
        ss_vevac = sem("ss_vevac")
        sv_rope = sem("sv_rope")
        ss_exp = sem("ss_exp")
        sv_elu = sem("sv_elu")
        st_qkT = sem("st_qkT")
        ss_qkTevac = sem("ss_qkTevac")
        sv_qkTek = sem("sv_qkTek")
        st_stile = sem("st_stile")
        sv_smevac = sem("sv_smevac")
        st_attnout = sem("st_attnout")
        st_kv = sem("st_kv")
        sv_divevac = sem("sv_divevac")
        sv_m1d = sem("sv_m1d")
        sv_kvadd = sem("sv_kvadd")
        ss_kvcast = sem("ss_kvcast")
        st_attnT = sem("st_attnT")
        ss_attnTevac = sem("ss_attnTevac")
        st_outproj = sem("st_outproj")
        ss_partial = sem("ss_partial")
        sv_locmax = sem("sv_locmax")
        sv_g11 = sem("sv_g11")
        sg_cc = sem("sg_cc")
        scc_fin = sem("scc_fin")
        sv_xsc = sem("sv_xsc")
        sv_ab = sem("sv_ab")
        ss_z = sem("ss_z")
        sv_round = sem("sv_round")
        ss_xq = sem("ss_xq")
        st_xqT = sem("st_xqT")
        ss_xqTevac = sem("ss_xqTevac")
        st_ffn = sem("st_ffn")
        sv_o1 = sem("sv_o1")
        sg_o2 = sem("sg_o2")

        block = ctx.enter_context(nc.Block())
        IS_TOTAL = 22

        @block.sync
        def _(sync):
            for c in range(2):
                sync.dma_start(out=xa[c][:], in_=x_hbm[c * C:(c + 1) * C, :]).then_inc(dxs[c], 16)
            for nj in range(8):
                if nj >= 2:
                    sync.wait_ge(sv_wc, nj - 1)
                sync.dma_start(out=wi8_i8[:, (nj % 2) * 1024:(nj % 2 + 1) * 1024],
                               in_=wi8_hbm[nj * C:(nj + 1) * C, :]).then_inc(dwr[nj % 2], 16)
                if nj == 1:
                    sync.dma_start(out=wsc_row[:], in_=wsc_hbm).then_inc(dsin, 16)
            for c in range(2, NCH):
                p = c % 2
                sync.wait_ge(sv_stats, c - 1)
                sync.dma_start(out=xa[p][:], in_=x_hbm[c * C:(c + 1) * C, :]).then_inc(dxs[p], 16)
                sync.wait_ge(ss_partial, c - 1)
                sync.dma_start(out=part_hbm[(c - 2) * C:(c - 1) * C, :],
                               in_=partial[c % 2][:]).then_inc(dps[c % 2], 16)
            for cc2 in (NCH - 2, NCH - 1):
                sync.wait_ge(ss_partial, cc2 + 1)
                sync.dma_start(out=part_hbm[cc2 * C:(cc2 + 1) * C, :],
                               in_=partial[cc2 % 2][:]).then_inc(dps[cc2 % 2], 16)
            for c in range(NCH):
                p = c % 2
                b4 = c % 4
                if c < 2:
                    sync.wait_ge(ss_partial, NCH)
                    sync.wait_ge(sv_stats, NCH)
                elif c < 4:
                    sync.wait_ge(st_kv, NCH)
                    sync.wait_ge(ss_vevac, NCH)
                    sync.wait_ge(sv_elu, 2 * NCH)
                else:
                    sync.wait_ge(ss_z, c - 3)
                    sync.wait_ge(sv_g2t, c - 3)
                sync.dma_start(out=xa4[b4], in_=x_hbm[c * C:(c + 1) * C, :]).then_inc(dpb4[b4], 16)
                if c < 2:
                    sync.wait_ge(dps[0], 16 * (NCH // 2))
                    sync.wait_ge(dps[1], 16 * (NCH // 2))
                elif c < 4:
                    sync.wait_ge(st_outproj, NCH)
                else:
                    sync.wait_ge(sg_o2, c - 3)
                sync.dma_start(out=pb4_sb[b4],
                               in_=part_hbm[c * C:(c + 1) * C, :]).then_inc(dpl4[b4], 16)
            sync.wait_ge(dos[0], 16 * (NCH // 2))
            sync.wait_ge(dos[1], 16 * (NCH // 2))

        @block.gpsimd
        def _(g):
            g.dma_start(out=id_bf[:], in_=eye_hbm).then_inc(dsm, 16)
            g.dma_start(out=maskT_sb[:], in_=mask_hbm).then_inc(dsm, 16)
            g.dma_start(out=costab[:], in_=cos_hbm).then_inc(dsm, 16)
            g.dma_start(out=sintab[:], in_=sin_hbm).then_inc(dsm, 16)
            g.dma_start(out=bias_row[:], in_=bias_hbm).then_inc(dsm, 16)
            g.memset(ones_col[:], 1.0)
            g.memset(ones_row[:], 1.0)
            g.memset(ones_row_f[:], 1.0)
            g.memset(ones11_f[:], 1.0)
            g.memset(ind_lo[0:64, :], 1.0)
            g.memset(ind_lo[64:128, :], 0.0)
            g.memset(ind_hi[0:64, :], 0.0)
            g.memset(ind_hi[64:128, :], 1.0)
            g.memset(dv_sb[:], 0.0)
            g.memset(kvf[0][:], 0.0)
            g.memset(kvf[1][:], 0.0)
            g.memset(kv_bf[:], 0.0)
            g.memset(epsln[:], 1e-5)
            g.memset(neghalf[:], -0.5)
            g.memset(c15[:], 1.5)
            g.memset(scsc[:], SCALE)
            g.memset(bass.AP(vpp[0][:, :].tensor, 65, [[H * 66, C], [66, H]]), 1.0)
            g.memset(bass.AP(vpp[1][:, :].tensor, 65, [[H * 66, C], [66, H]]), 1.0)
            g.drain()
            g.sem_inc(sg_i, 1)
            src = bass.AP(wqkv_hbm.tensor, 0, [[3 * D, C], [C * 3 * D, 8], [1, 3 * D]])
            g.dma_start(out=wqkv_sb[:, :].rearrange("p (f o) -> p f o", f=8), in_=src).then_inc(dwin, 16)
            src = bass.AP(wout_hbm.tensor, 0, [[D, C], [C * D, 8], [1, D]])
            g.dma_start(out=wout_sb[:, :].rearrange("p (f o) -> p f o", f=8), in_=src).then_inc(dwin, 16)
            def kvcast(c):
                g.wait_ge(sv_kvadd, c + 1)
                g.drain()
                g.tensor_copy(kv_bf[:], kvf[(c + 1) % 2][:])
                g.drain()
                g.sem_inc(ss_kvcast, 1)

            for c in range(NCH):
                p = c % 2
                if c >= 2:
                    g.wait_ge(st_qkv, 3 * (c - 2) + 3)
                src = bass.AP(xT_hbm.tensor, c * C, [[T_TOK, C], [C * T_TOK, 8], [1, C]])
                g.dma_start(out=xTa[p][:, :].rearrange("p (f t) -> p f t", f=8), in_=src).then_inc(dxts[p], 16)
                if c >= 2:
                    kvcast(c - 2)
            g.wait_ge(sv_locmax, 1)
            g.dma_start(out=cc_in.ap(), in_=locmax[:]).then_inc(dcc, 16)
            g.wait_ge(dcc, 16)
            g.collective_compute(
                "AllReduce", ALU.max,
                replica_groups=[list(range(8))],
                ins=[cc_in.ap()],
                outs=[cc_out.ap()],
            ).then_inc(scc_fin, 1)
            kvcast(NCH - 2)
            g.wait_ge(scc_fin, 1)
            g.dma_start(out=grow[0:1, :], in_=bass.AP(cc_out, 0, [[0, 1], [1, C]])).then_inc(dcc, 16)
            g.wait_ge(dcc, 32)
            g.wait_ge(sv_g11, 1)
            g.dma_start(out=gm_dram.ap(), in_=g11[0:1, 0:1]).then_inc(dcc, 16)
            g.wait_ge(dcc, 48)
            g.dma_start(out=gmax_bc[:], in_=bass.AP(gm_dram, 0, [[0, C], [1, 1]])).then_inc(dcc, 16)
            g.wait_ge(dcc, 64)
            g.drain()
            g.sem_inc(sg_cc, 1)
            for c in range(NCH):
                p = c % 2
                g.wait_ge(sv_g2t, c + 1)
                g.wait_ge(dpl4[c % 4], 16 * (c // 4 + 1))
                if c >= 2:
                    g.wait_ge(dos[p], 16 * (c // 2))
                g.drain()
                g.tensor_tensor(ob[p][:], g2t[:], pb4_sb[c % 4], ALU.add)
                g.drain()
                g.sem_inc(sg_o2, 1)
                g.dma_start(out=out_hbm[c * C:(c + 1) * C, :],
                            in_=ob[p][:]).then_inc(dos[p], 16)

        @block.tensor
        def _(t):
            t.wait_ge(sg_i, 1)
            t.wait_ge(dsm, 16 * 5)
            g_idx = 0
            for nj in range(8):
                t.wait_ge(sv_wc, nj + 1)
                for grp in range(2):
                    bk = 6 + grp
                    if g_idx >= 2:
                        t.wait_ge(ss_i, g_idx - 1)
                    for i in range(4):
                        kc = grp * 4 + i
                        ins = t.matmul(pb[bk][0:C, i * C:(i + 1) * C],
                                       wi8_stage[:, kc * C:(kc + 1) * C], id_bf[:],
                                       start=(i == 0), stop=(i == 3))
                    ins.then_inc(st_i, 1)
                    g_idx += 1
            t.wait_ge(dwin, 16 * 2)
            for obk in range(4):
                for f in range(8):
                    ins = t.matmul(pb[obk][0:1, 0:512], ones_col[:, 0:1],
                                   wqkv_sb[:, f * 3 * D + obk * 512: f * 3 * D + obk * 512 + 512],
                                   start=(f == 0), stop=(f == 7))
            ins.then_inc(st_i, 1)
            t.wait_ge(ss_i, 17)
            for obk in range(4):
                ins = t.matmul(pb[obk][0:C, 0:512], ones_row[0:1, :],
                               swqk_row[0:1, obk * 512:(obk + 1) * 512],
                               start=True, stop=True)
            ins.then_inc(st_i, 1)
            for ov in range(2):
                for f in range(8):
                    ins = t.matmul(pb[4 + ov][0:1, 0:512], ones_col[:, 0:1],
                                   wqkv_sb[:, f * 3 * D + 2048 + ov * 512: f * 3 * D + 2048 + ov * 512 + 512],
                                   start=(f == 0), stop=(f == 7))
            ins.then_inc(st_i, 1)
            t.wait_ge(ss_i, 19)
            for fc in range(8):
                ins = t.matmul(pb[4][0:C, fc:fc + 1], swv_row[0:1, fc * C:(fc + 1) * C],
                               ones11_f[0:1, 0:1], start=(fc == 0), stop=(fc == 7))
            ins.then_inc(st_i, 1)
            t.wait_ge(dsin, 16)
            t.wait_ge(ss_i, 18)
            for obk in range(2):
                ins = t.matmul(pb[obk][0:C, 0:512], ones_row_f[0:1, :],
                               wsc_row[0:1, obk * 512:(obk + 1) * 512], start=True, stop=True)
            ins.then_inc(st_i, 1)
            t.wait_ge(sv_i, 1)
            t.wait_ge(ss_i, 16)
            for ov in range(2):
                for fc in range(8):
                    ins = t.matmul(pb[6 + ov][0:H, 0:512], dv_sb[:, fc * H:(fc + 1) * H],
                                   wout_sb[:, fc * D + ov * 512: fc * D + ov * 512 + 512],
                                   start=(fc == 0), stop=(fc == 7))
            ins.then_inc(st_i, 1)
            t.wait_ge(ss_i, IS_TOTAL)

            BMAP = [2, 3, 6, 7]

            def wave(c, w):
                p = c % 2
                if w == 0:
                    t.wait_ge(dxts[p], 16 * (c // 2 + 1))
                    if c >= 1:
                        t.wait_ge(ss_vevac, c)
                else:
                    t.wait_ge(sv_qkevac, 2 * c + w)
                for f in range(8):
                    for j in range(2):
                        obk = 2 * w + j
                        ins = t.matmul(pb[j][0:C, 0:512],
                                       xTa[p][:, f * C:(f + 1) * C],
                                       wqkv_sb[:, f * 3 * D + obk * 512: f * 3 * D + obk * 512 + 512],
                                       start=(f == 0), stop=(f == 7))
                ins.then_inc(st_qkv, 1)

            def transposes(c):
                for g8 in range(8):
                    bk = BMAP[g8 % 4]
                    t.wait_ge(sv_elu, 2 * c + (1 if g8 < 4 else 2))
                    if g8 == 0 and c >= 1:
                        t.wait_ge(sv_kvadd, c)
                    if g8 == 2 and c >= 1:
                        t.wait_ge(sv_divevac, 4 * c)
                        t.wait_ge(ss_attnTevac, 3 * c)
                    if g8 >= 4:
                        t.wait_ge(ss_qkTevac, 4 * c + g8 - 3)
                    for i in range(4):
                        hh = g8 * 4 + i
                        ins = t.matmul(pb[bk][0:DH, i * C:(i + 1) * C],
                                       qkp[c % 2][:, hh * DH:(hh + 1) * DH], id_bf[:],
                                       start=(i == 0), stop=(i == 3))
                    ins.then_inc(st_qkT, 1)

            def stile(c):
                for g4 in range(4):
                    bk = BMAP[g4]
                    t.wait_ge(ss_qkTevac, 4 * c + g4 + 1)
                    t.wait_ge(sv_qkTek, 4 * c + g4 + 1)
                    for i in range(4):
                        h = g4 * 4 + i
                        ins = t.matmul(pb[bk][0:C, i * C:(i + 1) * C],
                                       qkT[:, (H + h) * C:(H + h + 1) * C],
                                       qkT[:, h * C:(h + 1) * C],
                                       start=(i == 0), stop=(i == 3))
                    ins.then_inc(st_stile, 1)

            def outkv(c):
                t.wait_ge(ss_kvcast, c)
                t.wait_ge(ss_vevac, c + 1)
                t.wait_ge(sv_qkevac, 2 * (c + 1))
                t.wait_ge(sv_qkTek, 4 * (c + 1))
                if c >= 1:
                    t.wait_ge(ss_partial, c)
                    t.wait_ge(sv_kvadd, c)
                for b in range(4):
                    t.wait_ge(sv_smevac, 4 * c + b + 1)
                    for i in range(4):
                        h = 4 * b + i
                        slot = i * C
                        t.matmul(pb[4 + b][0:C, slot:slot + 66],
                                 smb[b][:, i * C:(i + 1) * C],
                                 vpp[c % 2][:, h * 66:(h + 1) * 66],
                                 start=(i == 0), stop=False)
                        ins = t.matmul(pb[4 + b][0:C, slot:slot + 66],
                                       qkT[:, h * C:(h + 1) * C],
                                       kv_bf[:, h * 66:(h + 1) * 66],
                                       start=False, stop=(i == 3))
                    if b == 3:
                        for h in range(16):
                            ins = t.matmul(pb[7][0:DH, 450 + h * 2:450 + (h + 1) * 2],
                                           qkp[c % 2][:, D + h * DH: D + (h + 1) * DH],
                                           vpp[c % 2][:, h * 66 + 64:(h + 1) * 66],
                                           start=(h == 0), stop=(h == 15))
                    ins.then_inc(st_attnout, 1)
                t.wait_ge(sv_smevac, 4 * (c + 1))
                for h in range(8):
                    t.matmul(pb[2][0:DH, h * 64:(h + 1) * 64],
                             qkp[c % 2][:, D + h * DH: D + (h + 1) * DH],
                             vpp[c % 2][:, h * 66:h * 66 + 64],
                             start=(h == 0), stop=(h == 7))
                for h in range(8, 16):
                    ins = t.matmul(pb[3][0:DH, (h - 8) * 64:(h - 7) * 64],
                                   qkp[c % 2][:, D + h * DH: D + (h + 1) * DH],
                                   vpp[c % 2][:, h * 66:h * 66 + 64],
                                   start=(h == 8), stop=(h == 15))
                ins.then_inc(st_kv, 1)

            def attn_tail(c):
                t.wait_ge(sv_divevac, 4 * (c + 1))
                for i in range(4):
                    ins = t.matmul(pb[4][0:C, i * C:(i + 1) * C],
                                   attn_sb[:, i * C:(i + 1) * C], id_bf[:],
                                   start=(i == 0), stop=(i == 3))
                ins.then_inc(st_attnT, 1)
                for i in range(4):
                    ins = t.matmul(pb[5][0:C, i * C:(i + 1) * C],
                                   attn_sb[:, (4 + i) * C:(5 + i) * C], id_bf[:],
                                   start=(i == 0), stop=(i == 3))
                ins.then_inc(st_attnT, 1)
                t.wait_ge(sv_m1d, c + 1)
                t.matmul(pb[6][0:H, 0:C], m1d_bf[:, 0:H], id_bf[:],
                         start=True, stop=True).then_inc(st_attnT, 1)
                t.wait_ge(ss_attnTevac, 3 * c + 1)
                for j in range(4):
                    t.matmul(pb[4][0:C, 0:512], attnT_sb[:, j * C:(j + 1) * C],
                             wout_sb[:, j * D: j * D + 512], start=(j == 0), stop=False)
                t.wait_ge(ss_attnTevac, 3 * c + 2)
                for j in range(4, 8):
                    t.matmul(pb[4][0:C, 0:512], attnT_sb[:, j * C:(j + 1) * C],
                             wout_sb[:, j * D: j * D + 512], start=False, stop=False)
                for j in range(8):
                    t.matmul(pb[5][0:C, 0:512], attnT_sb[:, j * C:(j + 1) * C],
                             wout_sb[:, j * D + 512: (j + 1) * D], start=(j == 0), stop=False)
                t.matmul(pb[4][0:C, 0:512], ones_row[0:1, :], bias_row[0:1, 0:512],
                         start=False, stop=False)
                t.matmul(pb[5][0:C, 0:512], ones_row[0:1, :], bias_row[0:1, 512:1024],
                         start=False, stop=False)
                t.wait_ge(ss_attnTevac, 3 * (c + 1))
                t.matmul(pb[4][0:C, 0:512], m1dT[0:H, :], posG[0:H, 0:512],
                         start=False, stop=True)
                t.matmul(pb[5][0:C, 0:512], m1dT[0:H, :], posG[0:H, 512:1024],
                         start=False, stop=True).then_inc(st_outproj, 1)

            for c in range(NCH):
                wave(c, 0)
                if c >= 2:
                    attn_tail(c - 2)
                if c >= 1:
                    transposes(c - 1)
                wave(c, 1)
                if c >= 1:
                    stile(c - 1)
                wave(c, 2)
                if c >= 1:
                    outkv(c - 1)
            attn_tail(NCH - 2)
            transposes(NCH - 1)
            stile(NCH - 1)
            outkv(NCH - 1)
            attn_tail(NCH - 1)

            for c in range(NCH):
                p = c % 2
                t.wait_ge(ss_xq, c + 1)
                for gx in range(2):
                    bk = 6 + gx
                    if c == 0 and gx == 0:
                        t.wait_ge(sv_divevac, 4 * NCH)
                        t.wait_ge(sv_kvadd, NCH)
                    if c >= 1:
                        t.wait_ge(ss_xqTevac, 2 * c - 1 + gx)
                    for i in range(4):
                        j = gx * 4 + i
                        ins = t.matmul(pb[bk][0:C, i * C:(i + 1) * C],
                                       xTa[p][:, j * C:(j + 1) * C], id_bf[:],
                                       start=(i == 0), stop=(i == 3))
                    ins.then_inc(st_xqT, 1)
                if c == 0:
                    t.wait_ge(ss_vevac, NCH)
                else:
                    t.wait_ge(sv_o1, c)
                t.wait_ge(ss_xqTevac, 2 * (c + 1))
                for kc in range(8):
                    t.matmul(pb[0][0:C, 0:512], xqT[:, kc * C:(kc + 1) * C],
                             wffnT_sb[:, kc * D: kc * D + 512],
                             start=(kc == 0), stop=(kc == 7))
                    ins = t.matmul(pb[1][0:C, 0:512], xqT[:, kc * C:(kc + 1) * C],
                                   wffnT_sb[:, kc * D + 512: (kc + 1) * D],
                                   start=(kc == 0), stop=(kc == 7))
                ins.then_inc(st_ffn, 1)

        @block.vector
        def _(v):
            for nj in range(8):
                v.wait_ge(dwr[nj % 2], 16 * (nj // 2 + 1))
                if nj >= 1:
                    v.wait_ge(st_i, 2 * nj)
                v.drain()
                v.tensor_copy(wi8_stage[:], wi8_i8[:, (nj % 2) * 1024:(nj % 2 + 1) * 1024])
                v.drain()
                v.sem_inc(sv_wc, 1)
            v.wait_ge(ss_i, 20)
            v.wait_ge(sg_i, 1)
            for fc in range(8):
                v.tensor_scalar(dv_sb[:, fc * H + 2 * fc: fc * H + 2 * fc + 1],
                                ind_lo[:, 0:1], swv_cols[:, fc:fc + 1], None, ALU.mult)
                v.tensor_scalar(dv_sb[:, fc * H + 2 * fc + 1: fc * H + 2 * fc + 2],
                                ind_hi[:, 0:1], swv_cols[:, fc:fc + 1], None, ALU.mult)
            v.drain()
            v.sem_inc(sv_i, 1)
            v.wait_ge(ss_i, IS_TOTAL)
            v.wait_ge(dsm, 16 * 5)

            def stats(c):
                p = c % 2
                v.wait_ge(dxs[p], 16 * (c // 2 + 1))
                if c >= 2:
                    v.wait_ge(ss_stats, c - 1)
                v.drain()
                v.bn_stats(bn6[:, 0:6], xa[p][:, 0:512])
                v.bn_stats(bn6[:, 6:12], xa[p][:, 512:1024])
                v.drain()
                v.bn_aggr(bn2[p][:], bn6[:])
                v.tensor_reduce(maxx_all[:, c:c + 1], xa[p][:], axis=AX, op=ALU.max)
                v.tensor_reduce(minx_all[:, c:c + 1], xa[p][:], axis=AX, op=ALU.min)
                v.drain()
                v.tensor_scalar(mean_all[:, c:c + 1], bn2[p][:, 0:1], -1.0, None, ALU.mult)
                v.drain()
                v.sem_inc(sv_stats, 1)

            def qk_evac(c, half):
                v.wait_ge(st_qkv, 3 * c + half + 1)
                v.drain()
                for b in range(2):
                    v.scalar_tensor_tensor(qkln[:, half * D + b * 512:half * D + (b + 1) * 512],
                                           swqk_bc[:, half * D + b * 512:half * D + (b + 1) * 512],
                                           mean_all[:, c:c + 1],
                                           pb[b][0:C, 0:512],
                                           ALU.mult, ALU.add)
                v.drain()
                v.sem_inc(sv_qkevac, 1)

            def rope_half(c, half):
                off = half * D
                if half == 0:
                    v.wait_ge(ss_stats, c + 1)
                if c >= 2:
                    v.wait_ge(st_qkT, 8 * (c - 2) + 8)
                    v.wait_ge(st_kv, c - 1)
                cosap = bass.AP(costab[:, :].tensor, c * DH, [[NCH * DH, C], [0, 16], [1, DH]])
                sinap = bass.AP(sintab[:, :].tensor, c * DH,
                                [[NCH * DH, C], [0, 16], [32, 2], [1, 32]])
                qk3 = qkln[:, off:off + D].rearrange("p (g n) -> p g n", g=16)
                v.tensor_tensor(rt1[:, off:off + D].rearrange("p (g n) -> p g n", g=16),
                                qk3, cosap, ALU.mult)
                gat = bass.AP(qkln[:, :].tensor, off + 32,
                              [[2 * D, C], [DH, 16], [-32, 2], [1, 32]])
                v.tensor_tensor(rt2[:, off:off + D].rearrange("p (g r n) -> p g r n", g=16, r=2),
                                gat, sinap, ALU.mult)
                v.drain()
                v.tensor_add(qkp[c % 2][:, off:off + D], rt1[:, off:off + D], rt2[:, off:off + D])
                v.drain()
                v.sem_inc(sv_rope, 1)
                v.tensor_scalar(rt1[:, off:off + D], qkp[c % 2][:, off:off + D],
                                srst_all[:, c:c + 1], 0.0, ALU.mult, ALU.max)

            def elu_half(c, half):
                off = half * D
                v.wait_ge(ss_exp, 2 * c + half + 1)
                v.drain()
                v.scalar_tensor_tensor(qkp[c % 2][:, off:off + D], rt2[:, off:off + D], 1.0,
                                       rt1[:, off:off + D], ALU.min, ALU.add)
                v.drain()
                v.sem_inc(sv_elu, 1)

            def qkT_evacs_k(c):
                for g8 in range(4, 8):
                    v.wait_ge(st_qkT, 8 * c + g8 + 1)
                    v.drain()
                    v.tensor_copy(qkT[0:DH, g8 * 512:(g8 + 1) * 512],
                                  pb[[2, 3, 6, 7][g8 - 4]][0:DH, 0:512])
                    v.drain()
                    v.sem_inc(sv_qkTek, 1)

            def mask_evacs(c, lo, hi):
                for g4 in range(lo, hi):
                    v.wait_ge(st_stile, 4 * c + g4 + 1)
                    if c >= 1:
                        v.wait_ge(st_attnout, 4 * (c - 1) + g4 + 1)
                    v.drain()
                    mk = bass.AP(maskT_sb[:, :].tensor, 0, [[C, C], [0, 4], [1, C]])
                    v.tensor_tensor(smb[g4][:, :].rearrange("p (i n) -> p i n", i=4),
                                    pb[[2, 3, 6, 7][g4]][0:C, 0:512].rearrange("p (i n) -> p i n", i=4),
                                    mk, ALU.mult)
                    v.drain()
                    v.sem_inc(sv_smevac, 1)

            def div_evacs(c):
                for b in range(4):
                    v.wait_ge(st_attnout, 4 * c + b + 1)
                    v.drain()
                    denap = bass.AP(pb[4 + b][:, :].tensor, 65, [[512, C], [C, 4]])
                    v.tensor_scalar(dent[:, 0:4], denap, 1e-6, None, ALU.max)
                    v.drain()
                    v.reciprocal(rcp[:, 0:4], dent[:, 0:4])
                    v.drain()
                    numap = bass.AP(pb[4 + b][:, :].tensor, 0, [[512, C], [C, 4], [1, 64]])
                    rcpb = bass.AP(rcp[:, :].tensor, 0, [[4, C], [1, 4], [0, 64]])
                    v.tensor_tensor(attn_sb[:, b * 256:(b + 1) * 256].rearrange("p (i n) -> p i n", i=4),
                                    numap, rcpb, ALU.mult)
                    m1ap = bass.AP(pb[4 + b][:, :].tensor, 64, [[512, C], [C, 4]])
                    v.tensor_tensor(m1d[:, 4 * b:4 * b + 4], m1ap, rcp[:, 0:4], ALU.mult)
                    v.drain()
                    v.sem_inc(sv_divevac, 1)
                v.drain()
                v.tensor_copy(m1d_bf[:], m1d[:])
                v.drain()
                v.sem_inc(sv_m1d, 1)

            def kv_add(c):
                v.wait_ge(st_kv, c + 1)
                if c >= 1:
                    v.wait_ge(ss_kvcast, c)
                v.drain()
                for half in range(2):
                    dst = bass.AP(kvf[(c + 1) % 2][:, :].tensor, half * 8 * 66,
                                  [[H * 66, DH], [66, 8], [1, 64]])
                    srcf = bass.AP(kvf[c % 2][:, :].tensor, half * 8 * 66,
                                   [[H * 66, DH], [66, 8], [1, 64]])
                    v.tensor_tensor(dst, srcf,
                                    pb[2 + half][0:DH, 0:512].rearrange("p (h n) -> p h n", h=8),
                                    ALU.add)
                dste = bass.AP(kvf[(c + 1) % 2][:, :].tensor, 64, [[H * 66, DH], [66, H], [1, 2]])
                srce = bass.AP(kvf[c % 2][:, :].tensor, 64, [[H * 66, DH], [66, H], [1, 2]])
                v.tensor_tensor(dste, srce,
                                pb[7][0:DH, 450:482].rearrange("p (h n) -> p h n", h=H),
                                ALU.add)
                v.drain()
                v.sem_inc(sv_kvadd, 1)

            stats(0)
            for c in range(NCH):
                qk_evac(c, 0)
                rope_half(c, 0)
                if c == NCH - 1:
                    v.wait_ge(ss_stats, NCH)
                    v.drain()
                    v.tensor_add(tmp1c[:], maxx_all[:], mean_all[:])
                    v.tensor_add(tmp2c[:], minx_all[:], mean_all[:])
                    v.drain()
                    v.tensor_scalar(minx_all[:], tmp2c[:], -1.0, None, ALU.mult)
                    v.drain()
                    v.tensor_tensor(tmp2c[:], tmp1c[:], minx_all[:], ALU.max)
                    v.drain()
                    v.tensor_mul(tmp1c[:], tmp2c[:], rstd_all[:])
                    v.drain()
                    v.reduce_max(locmax[:], tmp1c[:], axis=AX)
                    v.drain()
                    v.sem_inc(sv_locmax, 1)
                if c >= 1:
                    qkT_evacs_k(c - 1)
                qk_evac(c, 1)
                if c >= 1:
                    mask_evacs(c - 1, 0, 4)
                rope_half(c, 1)
                elu_half(c, 0)
                elu_half(c, 1)
                if c >= 1:
                    div_evacs(c - 1)
                    kv_add(c - 1)
                if c + 1 < NCH:
                    stats(c + 1)
            qkT_evacs_k(NCH - 1)
            mask_evacs(NCH - 1, 0, 4)
            div_evacs(NCH - 1)
            kv_add(NCH - 1)

            v.wait_ge(dcc, 32)
            v.drain()
            v.reduce_max(g11[0:1, 0:1], grow[0:1, :], axis=AX)
            v.drain()
            v.sem_inc(sv_g11, 1)
            v.wait_ge(sg_cc, 1)
            v.drain()
            v.tensor_scalar(xsc[:], gmax_bc[:], INV127, 1e-5, ALU.mult, ALU.max)
            v.drain()
            v.reciprocal(rs_col[:], xsc[:])
            v.drain()
            v.tensor_scalar(sws_bc[:], ws_bc[:], xsc[:, 0:1], None, ALU.mult)
            v.drain()
            v.sem_inc(sv_xsc, 1)

            for c in range(NCH):
                p = c % 2
                if c >= 2:
                    v.wait_ge(ss_z, c - 1)
                v.drain()
                v.tensor_scalar(a_col[p][:], rstd_all[:, c:c + 1], rs_col[:, 0:1], None, ALU.mult)
                v.drain()
                v.tensor_scalar(b_col[p][:], mean_all[:, c:c + 1], a_col[p][:, 0:1], None, ALU.mult)
                v.drain()
                v.sem_inc(sv_ab, 1)
                v.wait_ge(ss_z, c + 1)
                if c >= 2:
                    v.wait_ge(ss_xq, c - 1)
                v.drain()
                v.tensor_scalar(tib[p], ztb[p], CM, -CM, ALU.add, ALU.add)
                v.drain()
                v.sem_inc(sv_round, 1)
                if c >= 1:
                    v.wait_ge(st_ffn, c)
                    v.drain()
                    v.tensor_mul(o1t[:, 0:512], pb[0][0:C, 0:512], sws_bc[:, 0:512])
                    v.tensor_mul(o1t[:, 512:1024], pb[1][0:C, 0:512], sws_bc[:, 512:1024])
                    v.drain()
                    v.sem_inc(sv_o1, 1)
                    if c >= 2:
                        v.wait_ge(sg_o2, c - 1)
                    v.wait_ge(dpb4[(c - 1) % 4], 16 * ((c - 1) // 4 + 1))
                    v.drain()
                    v.tensor_tensor(g2t[:], o1t[:], xa4[(c - 1) % 4], ALU.add)
                    v.drain()
                    v.sem_inc(sv_g2t, 1)

            def o1g2t_last():
                v.wait_ge(st_ffn, NCH)
                v.drain()
                v.tensor_mul(o1t[:, 0:512], pb[0][0:C, 0:512], sws_bc[:, 0:512])
                v.tensor_mul(o1t[:, 512:1024], pb[1][0:C, 0:512], sws_bc[:, 512:1024])
                v.drain()
                v.sem_inc(sv_o1, 1)
                v.wait_ge(sg_o2, NCH - 1)
                v.wait_ge(dpb4[(NCH - 1) % 4], 16 * ((NCH - 1) // 4 + 1))
                v.drain()
                v.tensor_tensor(g2t[:], o1t[:], xa4[(NCH - 1) % 4], ALU.add)
                v.drain()
                v.sem_inc(sv_g2t, 1)
            o1g2t_last()

        @block.scalar
        def _(s):
            k = 0
            for nj in range(8):
                for grp in range(2):
                    k += 1
                    s.wait_ge(st_i, k)
                    dst = wffnT_sb[:, :].rearrange("p (kc n) -> p kc n", kc=8)[:, grp * 4:(grp + 1) * 4,
                                                                              nj * C:(nj + 1) * C]
                    s.copy(dst, pb[6 + grp][0:C, 0:512].rearrange("p (i n) -> p i n", i=4))
                    s.drain()
                    s.sem_inc(ss_i, 1)
            s.wait_ge(st_i, 17)
            for obk in range(4):
                s.copy(swqk_row[0:1, obk * 512:(obk + 1) * 512], pb[obk][0:1, 0:512])
            s.drain()
            s.sem_inc(ss_i, 1)
            s.wait_ge(st_i, 18)
            for obk in range(4):
                s.copy(swqk_bc[:, obk * 512:(obk + 1) * 512], pb[obk][0:C, 0:512])
            s.drain()
            s.sem_inc(ss_i, 1)
            s.wait_ge(st_i, 19)
            for ov in range(2):
                s.copy(swv_row[0:1, ov * 512:(ov + 1) * 512], pb[4 + ov][0:1, 0:512])
            s.drain()
            s.sem_inc(ss_i, 1)
            s.wait_ge(st_i, 20)
            s.copy(swv_cols[:, 0:8], pb[4][0:C, 0:8])
            s.drain()
            s.sem_inc(ss_i, 1)
            s.wait_ge(st_i, 21)
            for obk in range(2):
                s.copy(ws_bc[:, obk * 512:(obk + 1) * 512], pb[obk][0:C, 0:512])
            s.drain()
            s.sem_inc(ss_i, 1)
            s.wait_ge(st_i, 22)
            for ov in range(2):
                s.copy(posG[0:H, ov * 512:(ov + 1) * 512], pb[6 + ov][0:H, 0:512])
            s.drain()
            s.sem_inc(ss_i, 1)

            def rstd(c):
                p = c % 2
                s.wait_ge(sv_stats, c + 1)
                s.drain()
                s.activation(s_ve[p][:], bn2[p][:, 1:2], ACTF.Identity, bias=epsln[:, 0:1])
                s.drain()
                s.activation(s_tt[p][:], s_ve[p][:], ACTF.Ln)
                s.drain()
                s.activation(s_sd[p][:], s_tt[p][:], ACTF.Exp, scale=neghalf[:, 0:1])
                s.drain()
                s.activation(s_r2[p][:], s_sd[p][:], ACTF.Square)
                s.drain()
                s.mul(s_tt[p][:], s_ve[p][:], s_r2[p][:, 0:1])
                s.drain()
                s.activation(s_uu[p][:], s_tt[p][:], ACTF.Identity,
                             scale=neghalf[:, 0:1], bias=c15[:, 0:1])
                s.drain()
                s.mul(rstd_all[:, c:c + 1], s_sd[p][:], s_uu[p][:, 0:1])
                s.drain()
                s.activation(srst_all[:, c:c + 1], rstd_all[:, c:c + 1], ACTF.Identity,
                             scale=scsc[:, 0:1])
                s.drain()
                s.sem_inc(ss_stats, 1)

            def vevac(c):
                s.wait_ge(st_qkv, 3 * c + 3)
                s.drain()
                for ov in range(2):
                    dst = bass.AP(vpp[c % 2][:, :].tensor, ov * 8 * 66, [[H * 66, C], [66, 8], [1, 64]])
                    s.activation(dst, pb[ov][0:C, 0:512].rearrange("p (h n) -> p h n", h=8),
                                 ACTF.Identity, scale=rstd_all[:, c:c + 1])
                s.drain()
                rmdst = bass.AP(vpp[c % 2][:, :].tensor, 64, [[H * 66, C], [66, H]])
                negm_b = bass.AP(mean_all[:, :].tensor, c, [[NCH, C], [0, H]])
                s.activation(rmdst, negm_b, ACTF.Identity, scale=rstd_all[:, c:c + 1])
                s.drain()
                s.sem_inc(ss_vevac, 1)

            def exps_half(c, half):
                off = half * D
                s.wait_ge(sv_rope, 2 * c + half + 1)
                s.drain()
                s.activation(rt2[:, off:off + D], qkp[c % 2][:, off:off + D], ACTF.Exp,
                             scale=srst_all[:, c:c + 1])
                s.drain()
                s.sem_inc(ss_exp, 1)

            def qkT_evacs(c):
                for g8 in range(4):
                    s.wait_ge(st_qkT, 8 * c + g8 + 1)
                    s.drain()
                    s.copy(qkT[0:DH, g8 * 512:(g8 + 1) * 512], pb[[2, 3, 6, 7][g8]][0:DH, 0:512])
                    s.drain()
                    s.sem_inc(ss_qkTevac, 1)

            def attnT_evacs(c):
                for gx in range(2):
                    s.wait_ge(st_attnT, 3 * c + gx + 1)
                    s.drain()
                    s.copy(attnT_sb[:, gx * 512:(gx + 1) * 512], pb[4 + gx][0:C, 0:512])
                    s.drain()
                    s.sem_inc(ss_attnTevac, 1)
                s.wait_ge(st_attnT, 3 * c + 3)
                s.drain()
                s.copy(m1dT[0:H, :], pb[6][0:H, 0:C])
                s.drain()
                s.sem_inc(ss_attnTevac, 1)

            def partial_evac(c):
                s.wait_ge(st_outproj, c + 1)
                if c >= 2:
                    s.wait_ge(dps[c % 2], 16 * (c // 2))
                s.drain()
                s.copy(partial[c % 2][:, 0:512], pb[4][0:C, 0:512])
                s.copy(partial[c % 2][:, 512:1024], pb[5][0:C, 0:512])
                s.drain()
                s.sem_inc(ss_partial, 1)

            for c in range(NCH):
                if c >= 2:
                    attnT_evacs(c - 2)
                rstd(c)
                if c >= 1:
                    qkT_evacs(c - 1)
                if c >= 2:
                    partial_evac(c - 2)
                exps_half(c, 0)
                exps_half(c, 1)
                vevac(c)
            attnT_evacs(NCH - 2)
            partial_evac(NCH - 2)
            qkT_evacs(NCH - 1)
            attnT_evacs(NCH - 1)
            partial_evac(NCH - 1)

            s.wait_ge(st_qkv, 3 * NCH)
            s.wait_ge(dps[0], 16 * (NCH // 2))
            s.wait_ge(dps[1], 16 * (NCH // 2))
            for c in range(NCH):
                p = c % 2
                s.wait_ge(sv_ab, c + 1)
                s.wait_ge(dpb4[c % 4], 16 * (c // 4 + 1))
                if c >= 2:
                    s.wait_ge(sv_round, c - 1)
                s.drain()
                s.activation(ztb[p], xa4[c % 4], ACTF.Identity,
                             scale=a_col[p][:, 0:1], bias=b_col[p][:, 0:1])
                s.drain()
                s.sem_inc(ss_z, 1)
                s.wait_ge(sv_round, c + 1)
                if c >= 2:
                    s.wait_ge(st_xqT, 2 * (c - 1))
                s.drain()
                s.copy(xTa[p][:], tib[p])
                s.drain()
                s.sem_inc(ss_xq, 1)
                for gx in range(2):
                    s.wait_ge(st_xqT, 2 * c + gx + 1)
                    s.drain()
                    s.copy(xqT[:, gx * 512:(gx + 1) * 512], pb[6 + gx][0:C, 0:512])
                    s.drain()
                    s.sem_inc(ss_xqTevac, 1)

    return nc


def _host_tables():
    half = DH // 2
    inv = (np.float32(1.0) / (np.float32(10000.0) **
           (np.arange(half, dtype=np.float32) / np.float32(half)))).astype(np.float32)
    pos = np.arange(T_TOK, dtype=np.float32)
    fr = pos[:, None] * inv[None, :]
    co = np.cos(fr).astype(np.float32)
    si = np.sin(fr).astype(np.float32)
    cost = np.zeros((C, NCH * DH), np.float32)
    sint = np.zeros((C, NCH * DH), np.float32)
    for c in range(NCH):
        rows = slice(c * C, (c + 1) * C)
        cost[:, c * DH:c * DH + half] = co[rows, :]
        cost[:, c * DH + half:(c + 1) * DH] = co[rows, :]
        sint[:, c * DH:c * DH + half] = -si[rows, :]
        sint[:, c * DH + half:(c + 1) * DH] = si[rows, :]
    maskT = np.triu(np.ones((C, C), np.float32))
    eye = np.eye(C, dtype=np.float32)
    return cost, sint, maskT, eye


def kernel(L0_attn_x, L0_attn_w_qkv, L0_attn_w_out, L0_ffn_weight_int8,
           L0_ffn_weight_scale, L0_ffn_bias):
    x = np.ascontiguousarray(np.asarray(L0_attn_x, dtype=np.float32))
    wqkv = np.ascontiguousarray(np.asarray(L0_attn_w_qkv, dtype=np.float32))
    wout = np.ascontiguousarray(np.asarray(L0_attn_w_out, dtype=np.float32))
    wi8 = np.ascontiguousarray(np.asarray(L0_ffn_weight_int8, dtype=np.int8))
    wsc = np.ascontiguousarray(np.asarray(L0_ffn_weight_scale, dtype=np.float32).reshape(1, D))
    bias = np.ascontiguousarray(np.asarray(L0_ffn_bias, dtype=np.float32).reshape(1, D))
    cost, sint, maskT, eye = _host_tables()

    nc = build_nc()
    in_maps = []
    for b in range(8):
        xb = np.ascontiguousarray(x[b])
        in_maps.append({
            "x": xb,
            "xT": np.ascontiguousarray(xb.T),
            "wqkv": wqkv,
            "wout": wout,
            "wi8": wi8,
            "wsc": wsc,
            "bias": bias,
            "cost": cost,
            "sint": sint,
            "maskT": maskT,
            "eye": eye,
        })
    res = run_bass_kernel_spmd(nc, in_maps, core_ids=list(range(8)))
    out = np.stack([np.asarray(r["out"]) for r in res.results], axis=0)
    return out.astype(np.float32)


# version 20
# speedup vs baseline: 1.0423x; 1.0065x over previous
import numpy as np
import ml_dtypes
import concourse.bass as bass
import concourse.mybir as mybir
from concourse.bass_utils import run_bass_kernel_spmd
from contextlib import ExitStack

F32 = mybir.dt.float32
BF16 = mybir.dt.bfloat16
I8 = mybir.dt.int8
I32 = mybir.dt.int32
AX = mybir.AxisListType.X
ALU = mybir.AluOpType
ACTF = mybir.ActivationFunctionType

T_TOK = 4096
D = 1024
H = 16
DH = 64
C = 128
NCH = T_TOK // C
SCALE = float(DH ** (-0.25))
CM = 12582912.0
INV127 = float(np.float32(1.0) / np.float32(127.0))


def build_nc():
    nc = bass.Bass("TRN2", target_bir_lowering=False)

    x_hbm = nc.dram_tensor("x", [T_TOK, D], F32, kind="ExternalInput").ap()
    xT_hbm = nc.dram_tensor("xT", [D, T_TOK], F32, kind="ExternalInput").ap()
    wqkv_hbm = nc.dram_tensor("wqkv", [D, 3 * D], F32, kind="ExternalInput").ap()
    wout_hbm = nc.dram_tensor("wout", [D, D], F32, kind="ExternalInput").ap()
    wi8_hbm = nc.dram_tensor("wi8", [D, D], I8, kind="ExternalInput").ap()
    wsc_hbm = nc.dram_tensor("wsc", [1, D], F32, kind="ExternalInput").ap()
    bias_hbm = nc.dram_tensor("bias", [1, D], F32, kind="ExternalInput").ap()
    cos_hbm = nc.dram_tensor("cost", [C, NCH * DH], F32, kind="ExternalInput").ap()
    sin_hbm = nc.dram_tensor("sint", [C, NCH * DH], F32, kind="ExternalInput").ap()
    mask_hbm = nc.dram_tensor("maskT", [C, C], F32, kind="ExternalInput").ap()
    eye_hbm = nc.dram_tensor("eye", [C, C], F32, kind="ExternalInput").ap()
    out_hbm = nc.dram_tensor("out", [T_TOK, D], F32, kind="ExternalOutput").ap()
    part_hbm = nc.dram_tensor("part_scratch", [T_TOK, D], BF16).ap()
    cc_in = nc.dram_tensor("cc_in", [C, 1], F32)
    cc_out = nc.dram_tensor("cc_out", [C, 1], F32)
    gm_dram = nc.dram_tensor("gm_dram", [1, 1], F32)

    ctx = ExitStack()
    sb = lambda name, sh, dt: ctx.enter_context(nc.sbuf_tensor(name, sh, dt))
    psum = lambda name, sh: ctx.enter_context(nc.psum_tensor(name, sh, F32))
    sem = lambda name: ctx.enter_context(nc.semaphore(name))

    with ctx:
        wqkv_sb = sb("wqkv_sb", [C, 8 * 3 * D], BF16)
        wout_sb = sb("wout_sb", [C, 8 * D], BF16)
        wffnT_sb = sb("wffnT_sb", [C, 8 * D], BF16)
        swqk_bc = sb("swqk_bc", [C, 2 * D], BF16)
        ws_bc = sb("ws_bc", [C, D], F32)
        sws_bc = sb("sws_bc", [C, D], F32)
        bias_row = sb("bias_row", [1, D], BF16)
        wsc_row = sb("wsc_row", [1, D], F32)
        swqk_row = sb("swqk_row", [1, 2 * D], BF16)
        swv_row = sb("swv_row", [1, D], F32)
        swv_cols = sb("swv_cols", [C, 8], F32)
        dv_sb = sb("dv_sb", [C, 8 * H], BF16)
        posG = sb("posG", [H, D], BF16)
        costab = sb("costab", [C, NCH * DH], BF16)
        sintab = sb("sintab", [C, NCH * DH], BF16)
        maskT_sb = sb("maskT_sb", [C, C], BF16)
        id_bf = sb("id_bf", [C, C], BF16)
        ones_col = sb("ones_col", [C, 1], BF16)
        ones_row = sb("ones_row", [1, C], BF16)
        ones_row_f = sb("ones_row_f", [1, C], F32)
        ones11_f = sb("ones11_f", [1, 1], F32)
        ind_lo = sb("ind_lo", [C, 1], F32)
        ind_hi = sb("ind_hi", [C, 1], F32)
        wi8_stage = sb("wi8_stage", [C, D], BF16)
        epsln = sb("epsln", [C, 1], F32)
        neghalf = sb("neghalf", [C, 1], F32)
        c15 = sb("c15", [C, 1], F32)
        scsc = sb("scsc", [C, 1], F32)

        xa = [sb(f"xa{i}", [C, D], F32) for i in range(2)]
        xTa = [sb(f"xTa{i}", [C, D], BF16) for i in range(2)]
        qkln = sb("qkln", [C, 2 * D], BF16)
        rt1 = sb("rt1", [C, 2 * D], BF16)
        rt2 = sb("rt2", [C, 2 * D], BF16)
        qkp = [sb(f"qkp{i}", [C, 2 * D], BF16) for i in range(2)]
        qkT = sb("qkT", [DH, 2 * H * C], BF16)
        vpp = [sb(f"vpp{i}", [C, H * 66], BF16) for i in range(2)]
        smb2 = [sb(f"smb{i}", [C, 4 * C], BF16) for i in range(2)]
        smb = [smb2[0][:, :], smb2[1][:, :],
               wi8_stage[:, 0:512], wi8_stage[:, 512:1024]]
        attn_sb = sb("attn_sb", [C, D], BF16)
        attnT_sb = sb("attnT_sb", [C, D], BF16)
        m1d = sb("m1d", [C, H], F32)
        m1d_bf = sb("m1d_bf", [C, H], BF16)
        m1dT = sb("m1dT", [H, C], BF16)
        kvf = [sb(f"kvf{i}", [DH, H * 66], F32) for i in range(2)]
        kv_bf = sb("kv_bf", [DH, H * 66], BF16)
        partial = [sb(f"partial{i}", [C, D], BF16) for i in range(2)]

        bn6 = sb("bn6", [C, 12], F32)
        bn2 = [sb(f"bn2_{i}", [C, 2], F32) for i in range(2)]
        mean_all = sb("mean_all", [C, NCH], F32)
        rstd_all = sb("rstd_all", [C, NCH], F32)
        srst_all = sb("srst_all", [C, NCH], F32)
        maxx_all = sb("maxx_all", [C, NCH], F32)
        minx_all = sb("minx_all", [C, NCH], F32)
        tmp1c = sb("tmp1c", [C, NCH], F32)
        tmp2c = sb("tmp2c", [C, NCH], F32)
        s_ve = [sb(f"s_ve{i}", [C, 1], F32) for i in range(2)]
        s_r2 = [sb(f"s_r2{i}", [C, 1], F32) for i in range(2)]
        s_tt = [sb(f"s_tt{i}", [C, 1], F32) for i in range(2)]
        s_uu = [sb(f"s_uu{i}", [C, 1], F32) for i in range(2)]
        s_sd = [sb(f"s_sd{i}", [C, 1], F32) for i in range(2)]
        dent = sb("dent", [C, 4], F32)
        rcp = sb("rcp", [C, 4], F32)
        locmax = sb("locmax", [C, 1], F32)
        grow = sb("grow", [1, C], F32)
        g11 = sb("g11", [1, 1], F32)
        gmax_bc = sb("gmax_bc", [C, 1], F32)
        xsc = sb("xsc", [C, 1], F32)
        rs_col = sb("rs_col", [C, 1], F32)
        a_col = [sb(f"a_col{i}", [C, 1], F32) for i in range(2)]
        b_col = [sb(f"b_col{i}", [C, 1], F32) for i in range(2)]

        ti32 = sb("ti32", [C, D], I32)
        pb_sb = [sb(f"pb_sb{i}", [C, D], BF16) for i in range(2)]
        ob = [sb(f"ob{i}", [C, D], F32) for i in range(2)]

        pb = [psum(f"pb{i}", [C, 512]) for i in range(8)]

        o1t = qkln[:, :].bitcast(F32)
        g2t = costab[:, :].bitcast(F32)
        xqT = rt2
        ztb = [rt1[:, :].bitcast(F32), sintab[:, :].bitcast(F32)]
        tib = [ti32[:, :], wqkv_sb[:, 0:2048].bitcast(I32)]
        xa4 = [xa[0][:, :], xa[1][:, :],
               qkp[0][:, :].bitcast(F32), qkp[1][:, :].bitcast(F32)]
        pb4_sb = [pb_sb[0][:, :], pb_sb[1][:, :], attn_sb[:, :], attnT_sb[:, :]]
        wi8_i8 = qkp[0][:, :].bitcast(I8)

        dxs = [sem("dx0"), sem("dx1")]
        dxts = [sem("dxt0"), sem("dxt1")]
        dwin = sem("dwin")
        dsm = sem("dsm")
        dwi8 = sem("dwi8")
        dsin = sem("dsin")
        dps = [sem("dps0"), sem("dps1")]
        dpl = [sem("dpl0"), sem("dpl1")]
        dpb4 = [sem(f"dpb4_{i}") for i in range(4)]
        dpl4 = [sem(f"dpl4_{i}") for i in range(4)]
        sv_g2t = sem("sv_g2t")
        dwr = [sem("dwr0"), sem("dwr1")]
        sv_wc = sem("sv_wc")
        dos = [sem("dos0"), sem("dos1")]
        dcc = sem("dcc")
        sg_i = sem("sg_i")
        st_i = sem("st_i")
        ss_i = sem("ss_i")
        sv_i = sem("sv_i")
        sv_stats = sem("sv_stats")
        ss_stats = sem("ss_stats")
        st_qkv = sem("st_qkv")
        sv_qkevac = sem("sv_qkevac")
        ss_vevac = sem("ss_vevac")
        sv_rope = sem("sv_rope")
        ss_exp = sem("ss_exp")
        sv_elu = sem("sv_elu")
        st_qkT = sem("st_qkT")
        ss_qkTevac = sem("ss_qkTevac")
        sv_qkTek = sem("sv_qkTek")
        st_stile = sem("st_stile")
        sv_smevac = sem("sv_smevac")
        st_attnout = sem("st_attnout")
        st_kv = sem("st_kv")
        sv_divevac = sem("sv_divevac")
        sv_m1d = sem("sv_m1d")
        sv_kvadd = sem("sv_kvadd")
        ss_kvcast = sem("ss_kvcast")
        st_attnT = sem("st_attnT")
        ss_attnTevac = sem("ss_attnTevac")
        st_outproj = sem("st_outproj")
        ss_partial = sem("ss_partial")
        sv_locmax = sem("sv_locmax")
        sv_g11 = sem("sv_g11")
        sg_cc = sem("sg_cc")
        scc_fin = sem("scc_fin")
        sv_xsc = sem("sv_xsc")
        sv_ab = sem("sv_ab")
        ss_z = sem("ss_z")
        sv_round = sem("sv_round")
        ss_xq = sem("ss_xq")
        st_xqT = sem("st_xqT")
        ss_xqTevac = sem("ss_xqTevac")
        st_ffn = sem("st_ffn")
        sv_o1 = sem("sv_o1")
        sg_o2 = sem("sg_o2")

        block = ctx.enter_context(nc.Block())
        IS_TOTAL = 22

        @block.sync
        def _(sync):
            for c in range(2):
                sync.dma_start(out=xa[c][:], in_=x_hbm[c * C:(c + 1) * C, :]).then_inc(dxs[c], 16)
            for nj in range(8):
                if nj >= 2:
                    sync.wait_ge(sv_wc, nj - 1)
                sync.dma_start(out=wi8_i8[:, (nj % 2) * 1024:(nj % 2 + 1) * 1024],
                               in_=wi8_hbm[nj * C:(nj + 1) * C, :]).then_inc(dwr[nj % 2], 16)
                if nj == 1:
                    sync.dma_start(out=wsc_row[:], in_=wsc_hbm).then_inc(dsin, 16)
            for c in range(2, NCH):
                p = c % 2
                sync.wait_ge(sv_stats, c - 1)
                sync.dma_start(out=xa[p][:], in_=x_hbm[c * C:(c + 1) * C, :]).then_inc(dxs[p], 16)
                sync.wait_ge(ss_partial, c - 1)
                sync.dma_start(out=part_hbm[(c - 2) * C:(c - 1) * C, :],
                               in_=partial[c % 2][:]).then_inc(dps[c % 2], 16)
            for cc2 in (NCH - 2, NCH - 1):
                sync.wait_ge(ss_partial, cc2 + 1)
                sync.dma_start(out=part_hbm[cc2 * C:(cc2 + 1) * C, :],
                               in_=partial[cc2 % 2][:]).then_inc(dps[cc2 % 2], 16)
            for c in range(NCH):
                p = c % 2
                b4 = c % 4
                if c < 2:
                    sync.wait_ge(ss_partial, NCH)
                    sync.wait_ge(sv_stats, NCH)
                elif c < 4:
                    sync.wait_ge(st_kv, NCH)
                    sync.wait_ge(ss_vevac, NCH)
                    sync.wait_ge(sv_elu, 2 * NCH)
                else:
                    sync.wait_ge(ss_z, c - 3)
                    sync.wait_ge(sv_g2t, c - 3)
                sync.dma_start(out=xa4[b4], in_=x_hbm[c * C:(c + 1) * C, :]).then_inc(dpb4[b4], 16)
                if c < 2:
                    sync.wait_ge(dps[0], 16 * (NCH // 2))
                    sync.wait_ge(dps[1], 16 * (NCH // 2))
                elif c < 4:
                    sync.wait_ge(st_outproj, NCH)
                else:
                    sync.wait_ge(sg_o2, c - 3)
                sync.dma_start(out=pb4_sb[b4],
                               in_=part_hbm[c * C:(c + 1) * C, :]).then_inc(dpl4[b4], 16)
            sync.wait_ge(dos[0], 16 * (NCH // 2))
            sync.wait_ge(dos[1], 16 * (NCH // 2))

        @block.gpsimd
        def _(g):
            g.dma_start(out=id_bf[:], in_=eye_hbm).then_inc(dsm, 16)
            g.dma_start(out=maskT_sb[:], in_=mask_hbm).then_inc(dsm, 16)
            g.dma_start(out=costab[:], in_=cos_hbm).then_inc(dsm, 16)
            g.dma_start(out=sintab[:], in_=sin_hbm).then_inc(dsm, 16)
            g.dma_start(out=bias_row[:], in_=bias_hbm).then_inc(dsm, 16)
            g.memset(ones_col[:], 1.0)
            g.memset(ones_row[:], 1.0)
            g.memset(ones_row_f[:], 1.0)
            g.memset(ones11_f[:], 1.0)
            g.memset(ind_lo[0:64, :], 1.0)
            g.memset(ind_lo[64:128, :], 0.0)
            g.memset(ind_hi[0:64, :], 0.0)
            g.memset(ind_hi[64:128, :], 1.0)
            g.memset(dv_sb[:], 0.0)
            g.memset(kvf[0][:], 0.0)
            g.memset(kvf[1][:], 0.0)
            g.memset(kv_bf[:], 0.0)
            g.memset(epsln[:], 1e-5)
            g.memset(neghalf[:], -0.5)
            g.memset(c15[:], 1.5)
            g.memset(scsc[:], SCALE)
            g.memset(bass.AP(vpp[0][:, :].tensor, 65, [[H * 66, C], [66, H]]), 1.0)
            g.memset(bass.AP(vpp[1][:, :].tensor, 65, [[H * 66, C], [66, H]]), 1.0)
            g.drain()
            g.sem_inc(sg_i, 1)
            src = bass.AP(wqkv_hbm.tensor, 0, [[3 * D, C], [C * 3 * D, 8], [1, 3 * D]])
            g.dma_start(out=wqkv_sb[:, :].rearrange("p (f o) -> p f o", f=8), in_=src).then_inc(dwin, 16)
            src = bass.AP(wout_hbm.tensor, 0, [[D, C], [C * D, 8], [1, D]])
            g.dma_start(out=wout_sb[:, :].rearrange("p (f o) -> p f o", f=8), in_=src).then_inc(dwin, 16)
            def kvcast(c):
                g.wait_ge(sv_kvadd, c + 1)
                g.drain()
                g.tensor_copy(kv_bf[:], kvf[(c + 1) % 2][:])
                g.drain()
                g.sem_inc(ss_kvcast, 1)

            for c in range(NCH):
                p = c % 2
                if c >= 2:
                    g.wait_ge(st_qkv, 3 * (c - 2) + 3)
                src = bass.AP(xT_hbm.tensor, c * C, [[T_TOK, C], [C * T_TOK, 8], [1, C]])
                g.dma_start(out=xTa[p][:, :].rearrange("p (f t) -> p f t", f=8), in_=src).then_inc(dxts[p], 16)
                if c >= 2:
                    kvcast(c - 2)
            g.wait_ge(sv_locmax, 1)
            g.dma_start(out=cc_in.ap(), in_=locmax[:]).then_inc(dcc, 16)
            g.wait_ge(dcc, 16)
            g.collective_compute(
                "AllReduce", ALU.max,
                replica_groups=[list(range(8))],
                ins=[cc_in.ap()],
                outs=[cc_out.ap()],
            ).then_inc(scc_fin, 1)
            kvcast(NCH - 2)
            g.wait_ge(scc_fin, 1)
            g.dma_start(out=grow[0:1, :], in_=bass.AP(cc_out, 0, [[0, 1], [1, C]])).then_inc(dcc, 16)
            g.wait_ge(dcc, 32)
            g.wait_ge(sv_g11, 1)
            g.dma_start(out=gm_dram.ap(), in_=g11[0:1, 0:1]).then_inc(dcc, 16)
            g.wait_ge(dcc, 48)
            g.dma_start(out=gmax_bc[:], in_=bass.AP(gm_dram, 0, [[0, C], [1, 1]])).then_inc(dcc, 16)
            g.wait_ge(dcc, 64)
            g.drain()
            g.sem_inc(sg_cc, 1)
            for c in range(NCH):
                p = c % 2
                g.wait_ge(sv_g2t, c + 1)
                g.wait_ge(dpl4[c % 4], 16 * (c // 4 + 1))
                if c >= 2:
                    g.wait_ge(dos[p], 16 * (c // 2))
                g.drain()
                g.tensor_tensor(ob[p][:], g2t[:], pb4_sb[c % 4], ALU.add)
                g.drain()
                g.sem_inc(sg_o2, 1)
                g.dma_start(out=out_hbm[c * C:(c + 1) * C, :],
                            in_=ob[p][:]).then_inc(dos[p], 16)

        @block.tensor
        def _(t):
            t.wait_ge(sg_i, 1)
            t.wait_ge(dsm, 16 * 5)
            g_idx = 0
            for nj in range(8):
                t.wait_ge(sv_wc, nj + 1)
                for grp in range(2):
                    bk = 6 + grp
                    if g_idx >= 2:
                        t.wait_ge(ss_i, g_idx - 1)
                    for i in range(4):
                        kc = grp * 4 + i
                        ins = t.matmul(pb[bk][0:C, i * C:(i + 1) * C],
                                       wi8_stage[:, kc * C:(kc + 1) * C], id_bf[:],
                                       start=(i == 0), stop=(i == 3))
                    ins.then_inc(st_i, 1)
                    g_idx += 1
            t.wait_ge(dwin, 16 * 2)
            for obk in range(4):
                for f in range(8):
                    ins = t.matmul(pb[obk][0:1, 0:512], ones_col[:, 0:1],
                                   wqkv_sb[:, f * 3 * D + obk * 512: f * 3 * D + obk * 512 + 512],
                                   start=(f == 0), stop=(f == 7))
            ins.then_inc(st_i, 1)
            t.wait_ge(ss_i, 17)
            for obk in range(4):
                ins = t.matmul(pb[obk][0:C, 0:512], ones_row[0:1, :],
                               swqk_row[0:1, obk * 512:(obk + 1) * 512],
                               start=True, stop=True)
            ins.then_inc(st_i, 1)
            for ov in range(2):
                for f in range(8):
                    ins = t.matmul(pb[4 + ov][0:1, 0:512], ones_col[:, 0:1],
                                   wqkv_sb[:, f * 3 * D + 2048 + ov * 512: f * 3 * D + 2048 + ov * 512 + 512],
                                   start=(f == 0), stop=(f == 7))
            ins.then_inc(st_i, 1)
            t.wait_ge(ss_i, 19)
            for fc in range(8):
                ins = t.matmul(pb[4][0:C, fc:fc + 1], swv_row[0:1, fc * C:(fc + 1) * C],
                               ones11_f[0:1, 0:1], start=(fc == 0), stop=(fc == 7))
            ins.then_inc(st_i, 1)
            t.wait_ge(dsin, 16)
            t.wait_ge(ss_i, 18)
            for obk in range(2):
                ins = t.matmul(pb[obk][0:C, 0:512], ones_row_f[0:1, :],
                               wsc_row[0:1, obk * 512:(obk + 1) * 512], start=True, stop=True)
            ins.then_inc(st_i, 1)
            t.wait_ge(sv_i, 1)
            t.wait_ge(ss_i, 16)
            for ov in range(2):
                for fc in range(8):
                    ins = t.matmul(pb[6 + ov][0:H, 0:512], dv_sb[:, fc * H:(fc + 1) * H],
                                   wout_sb[:, fc * D + ov * 512: fc * D + ov * 512 + 512],
                                   start=(fc == 0), stop=(fc == 7))
            ins.then_inc(st_i, 1)
            t.wait_ge(ss_i, IS_TOTAL)

            BMAP = [2, 3, 6, 7]

            def wave(c, w):
                p = c % 2
                if w == 0:
                    t.wait_ge(dxts[p], 16 * (c // 2 + 1))
                    if c >= 1:
                        t.wait_ge(ss_vevac, c)
                else:
                    t.wait_ge(sv_qkevac, 2 * c + w)
                for f in range(8):
                    for j in range(2):
                        obk = 2 * w + j
                        ins = t.matmul(pb[j][0:C, 0:512],
                                       xTa[p][:, f * C:(f + 1) * C],
                                       wqkv_sb[:, f * 3 * D + obk * 512: f * 3 * D + obk * 512 + 512],
                                       start=(f == 0), stop=(f == 7))
                ins.then_inc(st_qkv, 1)

            def transposes(c):
                for g8 in range(8):
                    bk = BMAP[g8 % 4]
                    t.wait_ge(sv_elu, 2 * c + (1 if g8 < 4 else 2))
                    if g8 == 0 and c >= 1:
                        t.wait_ge(sv_kvadd, c)
                    if g8 == 2 and c >= 1:
                        t.wait_ge(sv_divevac, 4 * c)
                        t.wait_ge(ss_attnTevac, 3 * c)
                    if g8 >= 4:
                        t.wait_ge(ss_qkTevac, 4 * c + g8 - 3)
                    for i in range(4):
                        hh = g8 * 4 + i
                        ins = t.matmul(pb[bk][0:DH, i * C:(i + 1) * C],
                                       qkp[c % 2][:, hh * DH:(hh + 1) * DH], id_bf[:],
                                       start=(i == 0), stop=(i == 3))
                    ins.then_inc(st_qkT, 1)

            def stile(c):
                for g4 in range(4):
                    bk = BMAP[g4]
                    t.wait_ge(ss_qkTevac, 4 * c + g4 + 1)
                    t.wait_ge(sv_qkTek, 4 * c + g4 + 1)
                    for i in range(4):
                        h = g4 * 4 + i
                        ins = t.matmul(pb[bk][0:C, i * C:(i + 1) * C],
                                       qkT[:, (H + h) * C:(H + h + 1) * C],
                                       qkT[:, h * C:(h + 1) * C],
                                       start=(i == 0), stop=(i == 3))
                    ins.then_inc(st_stile, 1)

            def outkv(c):
                t.wait_ge(ss_kvcast, c)
                t.wait_ge(ss_vevac, c + 1)
                t.wait_ge(sv_qkevac, 2 * (c + 1))
                t.wait_ge(sv_qkTek, 4 * (c + 1))
                if c >= 1:
                    t.wait_ge(ss_partial, c)
                    t.wait_ge(sv_kvadd, c)
                for b in range(4):
                    t.wait_ge(sv_smevac, 4 * c + b + 1)
                    for i in range(4):
                        h = 4 * b + i
                        slot = i * C
                        t.matmul(pb[4 + b][0:C, slot:slot + 66],
                                 smb[b][:, i * C:(i + 1) * C],
                                 vpp[c % 2][:, h * 66:(h + 1) * 66],
                                 start=(i == 0), stop=False)
                        ins = t.matmul(pb[4 + b][0:C, slot:slot + 66],
                                       qkT[:, h * C:(h + 1) * C],
                                       kv_bf[:, h * 66:(h + 1) * 66],
                                       start=False, stop=(i == 3))
                    if b == 3:
                        for h in range(16):
                            ins = t.matmul(pb[7][0:DH, 450 + h * 2:450 + (h + 1) * 2],
                                           qkp[c % 2][:, D + h * DH: D + (h + 1) * DH],
                                           vpp[c % 2][:, h * 66 + 64:(h + 1) * 66],
                                           start=(h == 0), stop=(h == 15))
                    ins.then_inc(st_attnout, 1)
                t.wait_ge(sv_smevac, 4 * (c + 1))
                for h in range(8):
                    t.matmul(pb[2][0:DH, h * 64:(h + 1) * 64],
                             qkp[c % 2][:, D + h * DH: D + (h + 1) * DH],
                             vpp[c % 2][:, h * 66:h * 66 + 64],
                             start=(h == 0), stop=(h == 7))
                for h in range(8, 16):
                    ins = t.matmul(pb[3][0:DH, (h - 8) * 64:(h - 7) * 64],
                                   qkp[c % 2][:, D + h * DH: D + (h + 1) * DH],
                                   vpp[c % 2][:, h * 66:h * 66 + 64],
                                   start=(h == 8), stop=(h == 15))
                ins.then_inc(st_kv, 1)

            def attn_tail(c):
                t.wait_ge(sv_divevac, 4 * (c + 1))
                for i in range(4):
                    ins = t.matmul(pb[4][0:C, i * C:(i + 1) * C],
                                   attn_sb[:, i * C:(i + 1) * C], id_bf[:],
                                   start=(i == 0), stop=(i == 3))
                ins.then_inc(st_attnT, 1)
                for i in range(4):
                    ins = t.matmul(pb[5][0:C, i * C:(i + 1) * C],
                                   attn_sb[:, (4 + i) * C:(5 + i) * C], id_bf[:],
                                   start=(i == 0), stop=(i == 3))
                ins.then_inc(st_attnT, 1)
                t.wait_ge(sv_m1d, c + 1)
                t.matmul(pb[6][0:H, 0:C], m1d_bf[:, 0:H], id_bf[:],
                         start=True, stop=True).then_inc(st_attnT, 1)
                t.wait_ge(ss_attnTevac, 3 * c + 1)
                for j in range(4):
                    t.matmul(pb[4][0:C, 0:512], attnT_sb[:, j * C:(j + 1) * C],
                             wout_sb[:, j * D: j * D + 512], start=(j == 0), stop=False)
                t.wait_ge(ss_attnTevac, 3 * c + 2)
                for j in range(4, 8):
                    t.matmul(pb[4][0:C, 0:512], attnT_sb[:, j * C:(j + 1) * C],
                             wout_sb[:, j * D: j * D + 512], start=False, stop=False)
                for j in range(8):
                    t.matmul(pb[5][0:C, 0:512], attnT_sb[:, j * C:(j + 1) * C],
                             wout_sb[:, j * D + 512: (j + 1) * D], start=(j == 0), stop=False)
                t.matmul(pb[4][0:C, 0:512], ones_row[0:1, :], bias_row[0:1, 0:512],
                         start=False, stop=False)
                t.matmul(pb[5][0:C, 0:512], ones_row[0:1, :], bias_row[0:1, 512:1024],
                         start=False, stop=False)
                t.wait_ge(ss_attnTevac, 3 * (c + 1))
                t.matmul(pb[4][0:C, 0:512], m1dT[0:H, :], posG[0:H, 0:512],
                         start=False, stop=True)
                t.matmul(pb[5][0:C, 0:512], m1dT[0:H, :], posG[0:H, 512:1024],
                         start=False, stop=True).then_inc(st_outproj, 1)

            for c in range(NCH):
                wave(c, 0)
                if c >= 2:
                    attn_tail(c - 2)
                if c >= 1:
                    transposes(c - 1)
                wave(c, 1)
                if c >= 1:
                    stile(c - 1)
                wave(c, 2)
                if c >= 1:
                    outkv(c - 1)
            attn_tail(NCH - 2)
            transposes(NCH - 1)
            stile(NCH - 1)
            outkv(NCH - 1)
            attn_tail(NCH - 1)

            for c in range(NCH):
                p = c % 2
                t.wait_ge(ss_xq, c + 1)
                for gx in range(2):
                    bk = 6 + gx
                    if c == 0 and gx == 0:
                        t.wait_ge(sv_divevac, 4 * NCH)
                        t.wait_ge(sv_kvadd, NCH)
                    if c >= 1:
                        t.wait_ge(ss_xqTevac, 2 * c - 1 + gx)
                    for i in range(4):
                        j = gx * 4 + i
                        ins = t.matmul(pb[bk][0:C, i * C:(i + 1) * C],
                                       xTa[p][:, j * C:(j + 1) * C], id_bf[:],
                                       start=(i == 0), stop=(i == 3))
                    ins.then_inc(st_xqT, 1)
                fb = 2 * (c % 2)
                if c == 0:
                    t.wait_ge(ss_vevac, NCH)
                elif c >= 2:
                    t.wait_ge(sv_o1, c - 1)
                t.wait_ge(ss_xqTevac, 2 * (c + 1))
                for kc in range(8):
                    t.matmul(pb[fb][0:C, 0:512], xqT[:, kc * C:(kc + 1) * C],
                             wffnT_sb[:, kc * D: kc * D + 512],
                             start=(kc == 0), stop=(kc == 7))
                    ins = t.matmul(pb[fb + 1][0:C, 0:512], xqT[:, kc * C:(kc + 1) * C],
                                   wffnT_sb[:, kc * D + 512: (kc + 1) * D],
                                   start=(kc == 0), stop=(kc == 7))
                ins.then_inc(st_ffn, 1)

        @block.vector
        def _(v):
            for nj in range(8):
                v.wait_ge(dwr[nj % 2], 16 * (nj // 2 + 1))
                if nj >= 1:
                    v.wait_ge(st_i, 2 * nj)
                v.drain()
                v.tensor_copy(wi8_stage[:], wi8_i8[:, (nj % 2) * 1024:(nj % 2 + 1) * 1024])
                v.drain()
                v.sem_inc(sv_wc, 1)
            v.wait_ge(ss_i, 20)
            v.wait_ge(sg_i, 1)
            for fc in range(8):
                v.tensor_scalar(dv_sb[:, fc * H + 2 * fc: fc * H + 2 * fc + 1],
                                ind_lo[:, 0:1], swv_cols[:, fc:fc + 1], None, ALU.mult)
                v.tensor_scalar(dv_sb[:, fc * H + 2 * fc + 1: fc * H + 2 * fc + 2],
                                ind_hi[:, 0:1], swv_cols[:, fc:fc + 1], None, ALU.mult)
            v.drain()
            v.sem_inc(sv_i, 1)
            v.wait_ge(ss_i, IS_TOTAL)
            v.wait_ge(dsm, 16 * 5)

            def stats(c):
                p = c % 2
                v.wait_ge(dxs[p], 16 * (c // 2 + 1))
                if c >= 2:
                    v.wait_ge(ss_stats, c - 1)
                v.drain()
                v.bn_stats(bn6[:, 0:6], xa[p][:, 0:512])
                v.bn_stats(bn6[:, 6:12], xa[p][:, 512:1024])
                v.drain()
                v.bn_aggr(bn2[p][:], bn6[:])
                v.tensor_reduce(maxx_all[:, c:c + 1], xa[p][:], axis=AX, op=ALU.max)
                v.tensor_reduce(minx_all[:, c:c + 1], xa[p][:], axis=AX, op=ALU.min)
                v.drain()
                v.tensor_scalar(mean_all[:, c:c + 1], bn2[p][:, 0:1], -1.0, None, ALU.mult)
                v.drain()
                v.sem_inc(sv_stats, 1)

            def qk_evac(c, half):
                v.wait_ge(st_qkv, 3 * c + half + 1)
                v.drain()
                for b in range(2):
                    v.scalar_tensor_tensor(qkln[:, half * D + b * 512:half * D + (b + 1) * 512],
                                           swqk_bc[:, half * D + b * 512:half * D + (b + 1) * 512],
                                           mean_all[:, c:c + 1],
                                           pb[b][0:C, 0:512],
                                           ALU.mult, ALU.add)
                v.drain()
                v.sem_inc(sv_qkevac, 1)

            def rope_half(c, half):
                off = half * D
                if half == 0:
                    v.wait_ge(ss_stats, c + 1)
                if c >= 2:
                    v.wait_ge(st_qkT, 8 * (c - 2) + 8)
                    v.wait_ge(st_kv, c - 1)
                cosap = bass.AP(costab[:, :].tensor, c * DH, [[NCH * DH, C], [0, 16], [1, DH]])
                sinap = bass.AP(sintab[:, :].tensor, c * DH,
                                [[NCH * DH, C], [0, 16], [32, 2], [1, 32]])
                qk3 = qkln[:, off:off + D].rearrange("p (g n) -> p g n", g=16)
                v.tensor_tensor(rt1[:, off:off + D].rearrange("p (g n) -> p g n", g=16),
                                qk3, cosap, ALU.mult)
                gat = bass.AP(qkln[:, :].tensor, off + 32,
                              [[2 * D, C], [DH, 16], [-32, 2], [1, 32]])
                v.tensor_tensor(rt2[:, off:off + D].rearrange("p (g r n) -> p g r n", g=16, r=2),
                                gat, sinap, ALU.mult)
                v.drain()
                v.tensor_add(qkp[c % 2][:, off:off + D], rt1[:, off:off + D], rt2[:, off:off + D])
                v.drain()
                v.sem_inc(sv_rope, 1)
                v.tensor_scalar(rt1[:, off:off + D], qkp[c % 2][:, off:off + D],
                                srst_all[:, c:c + 1], 0.0, ALU.mult, ALU.max)

            def elu_half(c, half):
                off = half * D
                v.wait_ge(ss_exp, 2 * c + half + 1)
                v.drain()
                v.scalar_tensor_tensor(qkp[c % 2][:, off:off + D], rt2[:, off:off + D], 1.0,
                                       rt1[:, off:off + D], ALU.min, ALU.add)
                v.drain()
                v.sem_inc(sv_elu, 1)

            def qkT_evacs_k(c):
                for g8 in range(4, 8):
                    v.wait_ge(st_qkT, 8 * c + g8 + 1)
                    v.drain()
                    v.tensor_copy(qkT[0:DH, g8 * 512:(g8 + 1) * 512],
                                  pb[[2, 3, 6, 7][g8 - 4]][0:DH, 0:512])
                    v.drain()
                    v.sem_inc(sv_qkTek, 1)

            def mask_evacs(c, lo, hi):
                for g4 in range(lo, hi):
                    v.wait_ge(st_stile, 4 * c + g4 + 1)
                    if c >= 1:
                        v.wait_ge(st_attnout, 4 * (c - 1) + g4 + 1)
                    v.drain()
                    mk = bass.AP(maskT_sb[:, :].tensor, 0, [[C, C], [0, 4], [1, C]])
                    v.tensor_tensor(smb[g4][:, :].rearrange("p (i n) -> p i n", i=4),
                                    pb[[2, 3, 6, 7][g4]][0:C, 0:512].rearrange("p (i n) -> p i n", i=4),
                                    mk, ALU.mult)
                    v.drain()
                    v.sem_inc(sv_smevac, 1)

            def div_evacs(c):
                for b in range(4):
                    v.wait_ge(st_attnout, 4 * c + b + 1)
                    v.drain()
                    denap = bass.AP(pb[4 + b][:, :].tensor, 65, [[512, C], [C, 4]])
                    v.tensor_scalar(dent[:, 0:4], denap, 1e-6, None, ALU.max)
                    v.drain()
                    v.reciprocal(rcp[:, 0:4], dent[:, 0:4])
                    v.drain()
                    numap = bass.AP(pb[4 + b][:, :].tensor, 0, [[512, C], [C, 4], [1, 64]])
                    rcpb = bass.AP(rcp[:, :].tensor, 0, [[4, C], [1, 4], [0, 64]])
                    v.tensor_tensor(attn_sb[:, b * 256:(b + 1) * 256].rearrange("p (i n) -> p i n", i=4),
                                    numap, rcpb, ALU.mult)
                    m1ap = bass.AP(pb[4 + b][:, :].tensor, 64, [[512, C], [C, 4]])
                    v.tensor_tensor(m1d[:, 4 * b:4 * b + 4], m1ap, rcp[:, 0:4], ALU.mult)
                    v.drain()
                    v.sem_inc(sv_divevac, 1)
                v.drain()
                v.tensor_copy(m1d_bf[:], m1d[:])
                v.drain()
                v.sem_inc(sv_m1d, 1)

            def kv_add(c):
                v.wait_ge(st_kv, c + 1)
                if c >= 1:
                    v.wait_ge(ss_kvcast, c)
                v.drain()
                for half in range(2):
                    dst = bass.AP(kvf[(c + 1) % 2][:, :].tensor, half * 8 * 66,
                                  [[H * 66, DH], [66, 8], [1, 64]])
                    srcf = bass.AP(kvf[c % 2][:, :].tensor, half * 8 * 66,
                                   [[H * 66, DH], [66, 8], [1, 64]])
                    v.tensor_tensor(dst, srcf,
                                    pb[2 + half][0:DH, 0:512].rearrange("p (h n) -> p h n", h=8),
                                    ALU.add)
                dste = bass.AP(kvf[(c + 1) % 2][:, :].tensor, 64, [[H * 66, DH], [66, H], [1, 2]])
                srce = bass.AP(kvf[c % 2][:, :].tensor, 64, [[H * 66, DH], [66, H], [1, 2]])
                v.tensor_tensor(dste, srce,
                                pb[7][0:DH, 450:482].rearrange("p (h n) -> p h n", h=H),
                                ALU.add)
                v.drain()
                v.sem_inc(sv_kvadd, 1)

            stats(0)
            for c in range(NCH):
                qk_evac(c, 0)
                rope_half(c, 0)
                if c == NCH - 1:
                    v.wait_ge(ss_stats, NCH)
                    v.drain()
                    v.tensor_add(tmp1c[:], maxx_all[:], mean_all[:])
                    v.tensor_add(tmp2c[:], minx_all[:], mean_all[:])
                    v.drain()
                    v.tensor_scalar(minx_all[:], tmp2c[:], -1.0, None, ALU.mult)
                    v.drain()
                    v.tensor_tensor(tmp2c[:], tmp1c[:], minx_all[:], ALU.max)
                    v.drain()
                    v.tensor_mul(tmp1c[:], tmp2c[:], rstd_all[:])
                    v.drain()
                    v.reduce_max(locmax[:], tmp1c[:], axis=AX)
                    v.drain()
                    v.sem_inc(sv_locmax, 1)
                if c >= 1:
                    qkT_evacs_k(c - 1)
                qk_evac(c, 1)
                if c >= 1:
                    mask_evacs(c - 1, 0, 4)
                rope_half(c, 1)
                elu_half(c, 0)
                elu_half(c, 1)
                if c >= 1:
                    div_evacs(c - 1)
                    kv_add(c - 1)
                if c + 1 < NCH:
                    stats(c + 1)
            qkT_evacs_k(NCH - 1)
            mask_evacs(NCH - 1, 0, 4)
            div_evacs(NCH - 1)
            kv_add(NCH - 1)

            v.wait_ge(dcc, 32)
            v.drain()
            v.reduce_max(g11[0:1, 0:1], grow[0:1, :], axis=AX)
            v.drain()
            v.sem_inc(sv_g11, 1)
            v.wait_ge(sg_cc, 1)
            v.drain()
            v.tensor_scalar(xsc[:], gmax_bc[:], INV127, 1e-5, ALU.mult, ALU.max)
            v.drain()
            v.reciprocal(rs_col[:], xsc[:])
            v.drain()
            v.tensor_scalar(sws_bc[:], ws_bc[:], xsc[:, 0:1], None, ALU.mult)
            v.drain()
            v.sem_inc(sv_xsc, 1)

            for c in range(NCH):
                p = c % 2
                if c >= 2:
                    v.wait_ge(ss_z, c - 1)
                v.drain()
                v.tensor_scalar(a_col[p][:], rstd_all[:, c:c + 1], rs_col[:, 0:1], None, ALU.mult)
                v.drain()
                v.tensor_scalar(b_col[p][:], mean_all[:, c:c + 1], a_col[p][:, 0:1], None, ALU.mult)
                v.drain()
                v.sem_inc(sv_ab, 1)
                v.wait_ge(ss_z, c + 1)
                if c >= 2:
                    v.wait_ge(ss_xq, c - 1)
                v.drain()
                v.tensor_scalar(tib[p], ztb[p], CM, -CM, ALU.add, ALU.add)
                v.drain()
                v.sem_inc(sv_round, 1)
                if c >= 1:
                    ob2 = 2 * ((c - 1) % 2)
                    v.wait_ge(st_ffn, c)
                    v.drain()
                    v.tensor_mul(o1t[:, 0:512], pb[ob2][0:C, 0:512], sws_bc[:, 0:512])
                    v.tensor_mul(o1t[:, 512:1024], pb[ob2 + 1][0:C, 0:512], sws_bc[:, 512:1024])
                    v.drain()
                    v.sem_inc(sv_o1, 1)
                    if c >= 2:
                        v.wait_ge(sg_o2, c - 1)
                    v.wait_ge(dpb4[(c - 1) % 4], 16 * ((c - 1) // 4 + 1))
                    v.drain()
                    v.tensor_tensor(g2t[:], o1t[:], xa4[(c - 1) % 4], ALU.add)
                    v.drain()
                    v.sem_inc(sv_g2t, 1)

            def o1g2t_last():
                ob2 = 2 * ((NCH - 1) % 2)
                v.wait_ge(st_ffn, NCH)
                v.drain()
                v.tensor_mul(o1t[:, 0:512], pb[ob2][0:C, 0:512], sws_bc[:, 0:512])
                v.tensor_mul(o1t[:, 512:1024], pb[ob2 + 1][0:C, 0:512], sws_bc[:, 512:1024])
                v.drain()
                v.sem_inc(sv_o1, 1)
                v.wait_ge(sg_o2, NCH - 1)
                v.wait_ge(dpb4[(NCH - 1) % 4], 16 * ((NCH - 1) // 4 + 1))
                v.drain()
                v.tensor_tensor(g2t[:], o1t[:], xa4[(NCH - 1) % 4], ALU.add)
                v.drain()
                v.sem_inc(sv_g2t, 1)
            o1g2t_last()

        @block.scalar
        def _(s):
            k = 0
            for nj in range(8):
                for grp in range(2):
                    k += 1
                    s.wait_ge(st_i, k)
                    dst = wffnT_sb[:, :].rearrange("p (kc n) -> p kc n", kc=8)[:, grp * 4:(grp + 1) * 4,
                                                                              nj * C:(nj + 1) * C]
                    s.copy(dst, pb[6 + grp][0:C, 0:512].rearrange("p (i n) -> p i n", i=4))
                    s.drain()
                    s.sem_inc(ss_i, 1)
            s.wait_ge(st_i, 17)
            for obk in range(4):
                s.copy(swqk_row[0:1, obk * 512:(obk + 1) * 512], pb[obk][0:1, 0:512])
            s.drain()
            s.sem_inc(ss_i, 1)
            s.wait_ge(st_i, 18)
            for obk in range(4):
                s.copy(swqk_bc[:, obk * 512:(obk + 1) * 512], pb[obk][0:C, 0:512])
            s.drain()
            s.sem_inc(ss_i, 1)
            s.wait_ge(st_i, 19)
            for ov in range(2):
                s.copy(swv_row[0:1, ov * 512:(ov + 1) * 512], pb[4 + ov][0:1, 0:512])
            s.drain()
            s.sem_inc(ss_i, 1)
            s.wait_ge(st_i, 20)
            s.copy(swv_cols[:, 0:8], pb[4][0:C, 0:8])
            s.drain()
            s.sem_inc(ss_i, 1)
            s.wait_ge(st_i, 21)
            for obk in range(2):
                s.copy(ws_bc[:, obk * 512:(obk + 1) * 512], pb[obk][0:C, 0:512])
            s.drain()
            s.sem_inc(ss_i, 1)
            s.wait_ge(st_i, 22)
            for ov in range(2):
                s.copy(posG[0:H, ov * 512:(ov + 1) * 512], pb[6 + ov][0:H, 0:512])
            s.drain()
            s.sem_inc(ss_i, 1)

            def rstd(c):
                p = c % 2
                s.wait_ge(sv_stats, c + 1)
                s.drain()
                s.activation(s_ve[p][:], bn2[p][:, 1:2], ACTF.Identity, bias=epsln[:, 0:1])
                s.drain()
                s.activation(s_tt[p][:], s_ve[p][:], ACTF.Ln)
                s.drain()
                s.activation(s_sd[p][:], s_tt[p][:], ACTF.Exp, scale=neghalf[:, 0:1])
                s.drain()
                s.activation(s_r2[p][:], s_sd[p][:], ACTF.Square)
                s.drain()
                s.mul(s_tt[p][:], s_ve[p][:], s_r2[p][:, 0:1])
                s.drain()
                s.activation(s_uu[p][:], s_tt[p][:], ACTF.Identity,
                             scale=neghalf[:, 0:1], bias=c15[:, 0:1])
                s.drain()
                s.mul(rstd_all[:, c:c + 1], s_sd[p][:], s_uu[p][:, 0:1])
                s.drain()
                s.activation(srst_all[:, c:c + 1], rstd_all[:, c:c + 1], ACTF.Identity,
                             scale=scsc[:, 0:1])
                s.drain()
                s.sem_inc(ss_stats, 1)

            def vevac(c):
                s.wait_ge(st_qkv, 3 * c + 3)
                s.drain()
                for ov in range(2):
                    dst = bass.AP(vpp[c % 2][:, :].tensor, ov * 8 * 66, [[H * 66, C], [66, 8], [1, 64]])
                    s.activation(dst, pb[ov][0:C, 0:512].rearrange("p (h n) -> p h n", h=8),
                                 ACTF.Identity, scale=rstd_all[:, c:c + 1])
                s.drain()
                rmdst = bass.AP(vpp[c % 2][:, :].tensor, 64, [[H * 66, C], [66, H]])
                negm_b = bass.AP(mean_all[:, :].tensor, c, [[NCH, C], [0, H]])
                s.activation(rmdst, negm_b, ACTF.Identity, scale=rstd_all[:, c:c + 1])
                s.drain()
                s.sem_inc(ss_vevac, 1)

            def exps_half(c, half):
                off = half * D
                s.wait_ge(sv_rope, 2 * c + half + 1)
                s.drain()
                s.activation(rt2[:, off:off + D], qkp[c % 2][:, off:off + D], ACTF.Exp,
                             scale=srst_all[:, c:c + 1])
                s.drain()
                s.sem_inc(ss_exp, 1)

            def qkT_evacs(c):
                for g8 in range(4):
                    s.wait_ge(st_qkT, 8 * c + g8 + 1)
                    s.drain()
                    s.copy(qkT[0:DH, g8 * 512:(g8 + 1) * 512], pb[[2, 3, 6, 7][g8]][0:DH, 0:512])
                    s.drain()
                    s.sem_inc(ss_qkTevac, 1)

            def attnT_evacs(c):
                for gx in range(2):
                    s.wait_ge(st_attnT, 3 * c + gx + 1)
                    s.drain()
                    s.copy(attnT_sb[:, gx * 512:(gx + 1) * 512], pb[4 + gx][0:C, 0:512])
                    s.drain()
                    s.sem_inc(ss_attnTevac, 1)
                s.wait_ge(st_attnT, 3 * c + 3)
                s.drain()
                s.copy(m1dT[0:H, :], pb[6][0:H, 0:C])
                s.drain()
                s.sem_inc(ss_attnTevac, 1)

            def partial_evac(c):
                s.wait_ge(st_outproj, c + 1)
                if c >= 2:
                    s.wait_ge(dps[c % 2], 16 * (c // 2))
                s.drain()
                s.copy(partial[c % 2][:, 0:512], pb[4][0:C, 0:512])
                s.copy(partial[c % 2][:, 512:1024], pb[5][0:C, 0:512])
                s.drain()
                s.sem_inc(ss_partial, 1)

            for c in range(NCH):
                if c >= 2:
                    attnT_evacs(c - 2)
                rstd(c)
                if c >= 1:
                    qkT_evacs(c - 1)
                if c >= 2:
                    partial_evac(c - 2)
                exps_half(c, 0)
                exps_half(c, 1)
                vevac(c)
            attnT_evacs(NCH - 2)
            partial_evac(NCH - 2)
            qkT_evacs(NCH - 1)
            attnT_evacs(NCH - 1)
            partial_evac(NCH - 1)

            s.wait_ge(st_qkv, 3 * NCH)
            s.wait_ge(dps[0], 16 * (NCH // 2))
            s.wait_ge(dps[1], 16 * (NCH // 2))
            for c in range(NCH):
                p = c % 2
                s.wait_ge(sv_ab, c + 1)
                s.wait_ge(dpb4[c % 4], 16 * (c // 4 + 1))
                if c >= 2:
                    s.wait_ge(sv_round, c - 1)
                s.drain()
                s.activation(ztb[p], xa4[c % 4], ACTF.Identity,
                             scale=a_col[p][:, 0:1], bias=b_col[p][:, 0:1])
                s.drain()
                s.sem_inc(ss_z, 1)
                s.wait_ge(sv_round, c + 1)
                if c >= 2:
                    s.wait_ge(st_xqT, 2 * (c - 1))
                s.drain()
                s.copy(xTa[p][:], tib[p])
                s.drain()
                s.sem_inc(ss_xq, 1)
                for gx in range(2):
                    s.wait_ge(st_xqT, 2 * c + gx + 1)
                    s.drain()
                    s.copy(xqT[:, gx * 512:(gx + 1) * 512], pb[6 + gx][0:C, 0:512])
                    s.drain()
                    s.sem_inc(ss_xqTevac, 1)

    return nc


def _host_tables():
    half = DH // 2
    inv = (np.float32(1.0) / (np.float32(10000.0) **
           (np.arange(half, dtype=np.float32) / np.float32(half)))).astype(np.float32)
    pos = np.arange(T_TOK, dtype=np.float32)
    fr = pos[:, None] * inv[None, :]
    co = np.cos(fr).astype(np.float32)
    si = np.sin(fr).astype(np.float32)
    cost = np.zeros((C, NCH * DH), np.float32)
    sint = np.zeros((C, NCH * DH), np.float32)
    for c in range(NCH):
        rows = slice(c * C, (c + 1) * C)
        cost[:, c * DH:c * DH + half] = co[rows, :]
        cost[:, c * DH + half:(c + 1) * DH] = co[rows, :]
        sint[:, c * DH:c * DH + half] = -si[rows, :]
        sint[:, c * DH + half:(c + 1) * DH] = si[rows, :]
    maskT = np.triu(np.ones((C, C), np.float32))
    eye = np.eye(C, dtype=np.float32)
    return cost, sint, maskT, eye


def kernel(L0_attn_x, L0_attn_w_qkv, L0_attn_w_out, L0_ffn_weight_int8,
           L0_ffn_weight_scale, L0_ffn_bias):
    x = np.ascontiguousarray(np.asarray(L0_attn_x, dtype=np.float32))
    wqkv = np.ascontiguousarray(np.asarray(L0_attn_w_qkv, dtype=np.float32))
    wout = np.ascontiguousarray(np.asarray(L0_attn_w_out, dtype=np.float32))
    wi8 = np.ascontiguousarray(np.asarray(L0_ffn_weight_int8, dtype=np.int8))
    wsc = np.ascontiguousarray(np.asarray(L0_ffn_weight_scale, dtype=np.float32).reshape(1, D))
    bias = np.ascontiguousarray(np.asarray(L0_ffn_bias, dtype=np.float32).reshape(1, D))
    cost, sint, maskT, eye = _host_tables()

    nc = build_nc()
    in_maps = []
    for b in range(8):
        xb = np.ascontiguousarray(x[b])
        in_maps.append({
            "x": xb,
            "xT": np.ascontiguousarray(xb.T),
            "wqkv": wqkv,
            "wout": wout,
            "wi8": wi8,
            "wsc": wsc,
            "bias": bias,
            "cost": cost,
            "sint": sint,
            "maskT": maskT,
            "eye": eye,
        })
    res = run_bass_kernel_spmd(nc, in_maps, core_ids=list(range(8)))
    out = np.stack([np.asarray(r["out"]) for r in res.results], axis=0)
    return out.astype(np.float32)


# version 21 (final)
# speedup vs baseline: 1.0507x; 1.0081x over previous
import numpy as np
import ml_dtypes
import concourse.bass as bass
import concourse.mybir as mybir
from concourse.bass_utils import run_bass_kernel_spmd
from contextlib import ExitStack

F32 = mybir.dt.float32
BF16 = mybir.dt.bfloat16
I8 = mybir.dt.int8
I32 = mybir.dt.int32
AX = mybir.AxisListType.X
ALU = mybir.AluOpType
ACTF = mybir.ActivationFunctionType

T_TOK = 4096
D = 1024
H = 16
DH = 64
C = 128
NCH = T_TOK // C
SCALE = float(DH ** (-0.25))
CM = 12582912.0
INV127 = float(np.float32(1.0) / np.float32(127.0))


def build_nc():
    nc = bass.Bass("TRN2", target_bir_lowering=False)

    x_hbm = nc.dram_tensor("x", [T_TOK, D], F32, kind="ExternalInput").ap()
    xT_hbm = nc.dram_tensor("xT", [D, T_TOK], F32, kind="ExternalInput").ap()
    wqkv_hbm = nc.dram_tensor("wqkv", [D, 3 * D], F32, kind="ExternalInput").ap()
    wout_hbm = nc.dram_tensor("wout", [D, D], F32, kind="ExternalInput").ap()
    wi8_hbm = nc.dram_tensor("wi8", [D, D], I8, kind="ExternalInput").ap()
    wsc_hbm = nc.dram_tensor("wsc", [1, D], F32, kind="ExternalInput").ap()
    bias_hbm = nc.dram_tensor("bias", [1, D], F32, kind="ExternalInput").ap()
    cos_hbm = nc.dram_tensor("cost", [C, NCH * DH], F32, kind="ExternalInput").ap()
    sin_hbm = nc.dram_tensor("sint", [C, NCH * DH], F32, kind="ExternalInput").ap()
    mask_hbm = nc.dram_tensor("maskT", [C, C], F32, kind="ExternalInput").ap()
    eye_hbm = nc.dram_tensor("eye", [C, C], F32, kind="ExternalInput").ap()
    out_hbm = nc.dram_tensor("out", [T_TOK, D], F32, kind="ExternalOutput").ap()
    part_hbm = nc.dram_tensor("part_scratch", [T_TOK, D], BF16).ap()
    cc_in = nc.dram_tensor("cc_in", [C, 1], F32)
    cc_out = nc.dram_tensor("cc_out", [C, 1], F32)
    gm_dram = nc.dram_tensor("gm_dram", [1, 1], F32)

    ctx = ExitStack()
    sb = lambda name, sh, dt: ctx.enter_context(nc.sbuf_tensor(name, sh, dt))
    psum = lambda name, sh: ctx.enter_context(nc.psum_tensor(name, sh, F32))
    sem = lambda name: ctx.enter_context(nc.semaphore(name))

    with ctx:
        wqkv_sb = sb("wqkv_sb", [C, 8 * 3 * D], BF16)
        wout_sb = sb("wout_sb", [C, 8 * D], BF16)
        wffnT_sb = sb("wffnT_sb", [C, 8 * D], BF16)
        swqk_bc = sb("swqk_bc", [C, 2 * D], BF16)
        ws_bc = sb("ws_bc", [C, D], F32)
        sws_bc = sb("sws_bc", [C, D], F32)
        bias_row = sb("bias_row", [1, D], BF16)
        wsc_row = sb("wsc_row", [1, D], F32)
        swqk_row = sb("swqk_row", [1, 2 * D], BF16)
        swv_row = sb("swv_row", [1, D], F32)
        swv_cols = sb("swv_cols", [C, 8], F32)
        dv_sb = sb("dv_sb", [C, 8 * H], BF16)
        posG = sb("posG", [H, D], BF16)
        costab = sb("costab", [C, NCH * DH], BF16)
        sintab = sb("sintab", [C, NCH * DH], BF16)
        maskT_sb = sb("maskT_sb", [C, C], BF16)
        id_bf = sb("id_bf", [C, C], BF16)
        ones_col = sb("ones_col", [C, 1], BF16)
        ones_row = sb("ones_row", [1, C], BF16)
        ones_row_f = sb("ones_row_f", [1, C], F32)
        ones11_f = sb("ones11_f", [1, 1], F32)
        ind_lo = sb("ind_lo", [C, 1], F32)
        ind_hi = sb("ind_hi", [C, 1], F32)
        wi8_stage = sb("wi8_stage", [C, D], BF16)
        epsln = sb("epsln", [C, 1], F32)
        neghalf = sb("neghalf", [C, 1], F32)
        c15 = sb("c15", [C, 1], F32)
        scsc = sb("scsc", [C, 1], F32)

        xa = [sb(f"xa{i}", [C, D], F32) for i in range(2)]
        xTa = [sb(f"xTa{i}", [C, D], BF16) for i in range(2)]
        qkln = sb("qkln", [C, 2 * D], BF16)
        rt1 = sb("rt1", [C, 2 * D], BF16)
        rt2 = sb("rt2", [C, 2 * D], BF16)
        qkp = [sb(f"qkp{i}", [C, 2 * D], BF16) for i in range(2)]
        qkT = sb("qkT", [DH, 2 * H * C], BF16)
        vpp = [sb(f"vpp{i}", [C, H * 66], BF16) for i in range(2)]
        smb2 = [sb(f"smb{i}", [C, 4 * C], BF16) for i in range(2)]
        smb = [smb2[0][:, :], smb2[1][:, :],
               wi8_stage[:, 0:512], wi8_stage[:, 512:1024]]
        attn_sb = sb("attn_sb", [C, D], BF16)
        attnT_sb = sb("attnT_sb", [C, D], BF16)
        m1d = sb("m1d", [C, H], F32)
        m1d_bf = sb("m1d_bf", [C, H], BF16)
        m1dT = sb("m1dT", [H, C], BF16)
        kvf = [sb(f"kvf{i}", [DH, H * 66], F32) for i in range(2)]
        kv_bf = sb("kv_bf", [DH, H * 66], BF16)
        partial = [sb(f"partial{i}", [C, D], BF16) for i in range(2)]

        bn6 = sb("bn6", [C, 12], F32)
        bn2 = [sb(f"bn2_{i}", [C, 2], F32) for i in range(2)]
        mean_all = sb("mean_all", [C, NCH], F32)
        rstd_all = sb("rstd_all", [C, NCH], F32)
        srst_all = sb("srst_all", [C, NCH], F32)
        maxx_all = sb("maxx_all", [C, NCH], F32)
        minx_all = sb("minx_all", [C, NCH], F32)
        tmp1c = sb("tmp1c", [C, NCH], F32)
        tmp2c = sb("tmp2c", [C, NCH], F32)
        s_ve = [sb(f"s_ve{i}", [C, 1], F32) for i in range(2)]
        s_r2 = [sb(f"s_r2{i}", [C, 1], F32) for i in range(2)]
        s_tt = [sb(f"s_tt{i}", [C, 1], F32) for i in range(2)]
        s_uu = [sb(f"s_uu{i}", [C, 1], F32) for i in range(2)]
        s_sd = [sb(f"s_sd{i}", [C, 1], F32) for i in range(2)]
        dent = sb("dent", [C, 4], F32)
        rcp = sb("rcp", [C, 4], F32)
        locmax = sb("locmax", [C, 1], F32)
        grow = sb("grow", [1, C], F32)
        g11 = sb("g11", [1, 1], F32)
        gmax_bc = sb("gmax_bc", [C, 1], F32)
        xsc = sb("xsc", [C, 1], F32)
        rs_col = sb("rs_col", [C, 1], F32)
        a_col = [sb(f"a_col{i}", [C, 1], F32) for i in range(2)]
        b_col = [sb(f"b_col{i}", [C, 1], F32) for i in range(2)]

        ti32 = sb("ti32", [C, D], I32)
        pb_sb = [sb(f"pb_sb{i}", [C, D], BF16) for i in range(2)]
        ob = [sb(f"ob{i}", [C, D], F32) for i in range(2)]

        pb = [psum(f"pb{i}", [C, 512]) for i in range(8)]

        o1t = qkln[:, :].bitcast(F32)
        g2t = costab[:, :].bitcast(F32)
        xqT = rt2
        ztb = [rt1[:, :].bitcast(F32), sintab[:, :].bitcast(F32)]
        tib = [ti32[:, :], wqkv_sb[:, 0:2048].bitcast(I32)]
        xa4 = [xa[0][:, :], xa[1][:, :],
               qkp[0][:, :].bitcast(F32), qkp[1][:, :].bitcast(F32)]
        pb4_sb = [pb_sb[0][:, :], pb_sb[1][:, :], attn_sb[:, :], attnT_sb[:, :]]
        wi8_i8 = qkp[0][:, :].bitcast(I8)

        dxs = [sem("dx0"), sem("dx1")]
        dxts = [sem("dxt0"), sem("dxt1")]
        dwin = sem("dwin")
        dsm = sem("dsm")
        dwi8 = sem("dwi8")
        dsin = sem("dsin")
        dps = [sem("dps0"), sem("dps1")]
        dpl = [sem("dpl0"), sem("dpl1")]
        dpb4 = [sem(f"dpb4_{i}") for i in range(4)]
        dpl4 = [sem(f"dpl4_{i}") for i in range(4)]
        sv_g2t = sem("sv_g2t")
        dwr = [sem("dwr0"), sem("dwr1")]
        sv_wc = sem("sv_wc")
        dos = [sem("dos0"), sem("dos1")]
        dcc = sem("dcc")
        sg_i = sem("sg_i")
        st_i = sem("st_i")
        ss_i = sem("ss_i")
        sv_i = sem("sv_i")
        sv_stats = sem("sv_stats")
        ss_stats = sem("ss_stats")
        st_qkv = sem("st_qkv")
        sv_qkevac = sem("sv_qkevac")
        ss_vevac = sem("ss_vevac")
        sv_rope = sem("sv_rope")
        ss_exp = sem("ss_exp")
        sv_elu = sem("sv_elu")
        st_qkT = sem("st_qkT")
        ss_qkTevac = sem("ss_qkTevac")
        sv_qkTek = sem("sv_qkTek")
        st_stile = sem("st_stile")
        sv_smevac = sem("sv_smevac")
        st_attnout = sem("st_attnout")
        st_kv = sem("st_kv")
        sv_divevac = sem("sv_divevac")
        sv_m1d = sem("sv_m1d")
        sv_kvadd = sem("sv_kvadd")
        ss_kvcast = sem("ss_kvcast")
        st_attnT = sem("st_attnT")
        ss_attnTevac = sem("ss_attnTevac")
        st_outproj = sem("st_outproj")
        ss_partial = sem("ss_partial")
        sv_locmax = sem("sv_locmax")
        sv_g11 = sem("sv_g11")
        sg_cc = sem("sg_cc")
        scc_fin = sem("scc_fin")
        sv_xsc = sem("sv_xsc")
        sv_ab = sem("sv_ab")
        ss_z = sem("ss_z")
        sv_round = sem("sv_round")
        ss_xq = sem("ss_xq")
        st_xqT = sem("st_xqT")
        ss_xqTevac = sem("ss_xqTevac")
        st_ffn = sem("st_ffn")
        sv_o1 = sem("sv_o1")
        sg_o2 = sem("sg_o2")

        block = ctx.enter_context(nc.Block())
        IS_TOTAL = 22

        @block.sync
        def _(sync):
            for c in range(2):
                sync.dma_start(out=xa[c][:], in_=x_hbm[c * C:(c + 1) * C, :]).then_inc(dxs[c], 16)
            for nj in range(8):
                if nj >= 2:
                    sync.wait_ge(sv_wc, nj - 1)
                sync.dma_start(out=wi8_i8[:, (nj % 2) * 1024:(nj % 2 + 1) * 1024],
                               in_=wi8_hbm[nj * C:(nj + 1) * C, :]).then_inc(dwr[nj % 2], 16)
                if nj == 1:
                    sync.dma_start(out=wsc_row[:], in_=wsc_hbm).then_inc(dsin, 16)
            for c in range(2, NCH):
                p = c % 2
                sync.wait_ge(sv_stats, c - 1)
                sync.dma_start(out=xa[p][:], in_=x_hbm[c * C:(c + 1) * C, :]).then_inc(dxs[p], 16)
                sync.wait_ge(ss_partial, c - 1)
                sync.dma_start(out=part_hbm[(c - 2) * C:(c - 1) * C, :],
                               in_=partial[c % 2][:]).then_inc(dps[c % 2], 16)
            for cc2 in (NCH - 2, NCH - 1):
                sync.wait_ge(ss_partial, cc2 + 1)
                sync.dma_start(out=part_hbm[cc2 * C:(cc2 + 1) * C, :],
                               in_=partial[cc2 % 2][:]).then_inc(dps[cc2 % 2], 16)
            for c in range(NCH):
                p = c % 2
                b4 = c % 4
                if c < 2:
                    sync.wait_ge(ss_partial, NCH)
                    sync.wait_ge(sv_stats, NCH)
                elif c < 4:
                    sync.wait_ge(st_kv, NCH)
                    sync.wait_ge(ss_vevac, NCH)
                    sync.wait_ge(sv_elu, 2 * NCH)
                else:
                    sync.wait_ge(ss_z, c - 3)
                    sync.wait_ge(sv_g2t, c - 3)
                sync.dma_start(out=xa4[b4], in_=x_hbm[c * C:(c + 1) * C, :]).then_inc(dpb4[b4], 16)
                if c < 2:
                    sync.wait_ge(dps[0], 16 * (NCH // 2))
                    sync.wait_ge(dps[1], 16 * (NCH // 2))
                elif c < 4:
                    sync.wait_ge(st_outproj, NCH)
                else:
                    sync.wait_ge(sg_o2, c - 3)
                sync.dma_start(out=pb4_sb[b4],
                               in_=part_hbm[c * C:(c + 1) * C, :]).then_inc(dpl4[b4], 16)
            sync.wait_ge(dos[0], 16 * (NCH // 2))
            sync.wait_ge(dos[1], 16 * (NCH // 2))

        @block.gpsimd
        def _(g):
            g.dma_start(out=id_bf[:], in_=eye_hbm).then_inc(dsm, 16)
            g.dma_start(out=maskT_sb[:], in_=mask_hbm).then_inc(dsm, 16)
            g.dma_start(out=costab[:], in_=cos_hbm).then_inc(dsm, 16)
            g.dma_start(out=sintab[:], in_=sin_hbm).then_inc(dsm, 16)
            g.dma_start(out=bias_row[:], in_=bias_hbm).then_inc(dsm, 16)
            g.memset(ones_col[:], 1.0)
            g.memset(ones_row[:], 1.0)
            g.memset(ones_row_f[:], 1.0)
            g.memset(ones11_f[:], 1.0)
            g.memset(ind_lo[0:64, :], 1.0)
            g.memset(ind_lo[64:128, :], 0.0)
            g.memset(ind_hi[0:64, :], 0.0)
            g.memset(ind_hi[64:128, :], 1.0)
            g.memset(dv_sb[:], 0.0)
            g.memset(kvf[0][:], 0.0)
            g.memset(kvf[1][:], 0.0)
            g.memset(kv_bf[:], 0.0)
            g.memset(epsln[:], 1e-5)
            g.memset(neghalf[:], -0.5)
            g.memset(c15[:], 1.5)
            g.memset(scsc[:], SCALE)
            g.memset(bass.AP(vpp[0][:, :].tensor, 65, [[H * 66, C], [66, H]]), 1.0)
            g.memset(bass.AP(vpp[1][:, :].tensor, 65, [[H * 66, C], [66, H]]), 1.0)
            g.drain()
            g.sem_inc(sg_i, 1)
            src = bass.AP(wqkv_hbm.tensor, 0, [[3 * D, C], [C * 3 * D, 8], [1, 3 * D]])
            g.dma_start(out=wqkv_sb[:, :].rearrange("p (f o) -> p f o", f=8), in_=src).then_inc(dwin, 16)
            src = bass.AP(wout_hbm.tensor, 0, [[D, C], [C * D, 8], [1, D]])
            g.dma_start(out=wout_sb[:, :].rearrange("p (f o) -> p f o", f=8), in_=src).then_inc(dwin, 16)
            def kvcast(c):
                g.wait_ge(sv_kvadd, c + 1)
                g.drain()
                g.tensor_copy(kv_bf[:], kvf[(c + 1) % 2][:])
                g.drain()
                g.sem_inc(ss_kvcast, 1)

            for c in range(NCH):
                p = c % 2
                if c >= 2:
                    g.wait_ge(st_qkv, 3 * (c - 2) + 3)
                src = bass.AP(xT_hbm.tensor, c * C, [[T_TOK, C], [C * T_TOK, 8], [1, C]])
                g.dma_start(out=xTa[p][:, :].rearrange("p (f t) -> p f t", f=8), in_=src).then_inc(dxts[p], 16)
                if c >= 2:
                    kvcast(c - 2)
            g.wait_ge(sv_locmax, 1)
            g.dma_start(out=cc_in.ap(), in_=locmax[:]).then_inc(dcc, 16)
            g.wait_ge(dcc, 16)
            g.collective_compute(
                "AllReduce", ALU.max,
                replica_groups=[list(range(8))],
                ins=[cc_in.ap()],
                outs=[cc_out.ap()],
            ).then_inc(scc_fin, 1)
            kvcast(NCH - 2)
            g.wait_ge(scc_fin, 1)
            g.dma_start(out=grow[0:1, :], in_=bass.AP(cc_out, 0, [[0, 1], [1, C]])).then_inc(dcc, 16)
            g.wait_ge(dcc, 32)
            g.wait_ge(sv_g11, 1)
            g.dma_start(out=gm_dram.ap(), in_=g11[0:1, 0:1]).then_inc(dcc, 16)
            g.wait_ge(dcc, 48)
            g.dma_start(out=gmax_bc[:], in_=bass.AP(gm_dram, 0, [[0, C], [1, 1]])).then_inc(dcc, 16)
            g.wait_ge(dcc, 64)
            g.drain()
            g.sem_inc(sg_cc, 1)
            for c in range(NCH):
                p = c % 2
                g.wait_ge(sv_g2t, c + 1)
                g.wait_ge(dpl4[c % 4], 16 * (c // 4 + 1))
                if c >= 2:
                    g.wait_ge(dos[p], 16 * (c // 2))
                g.drain()
                g.tensor_tensor(ob[p][:], g2t[:], pb4_sb[c % 4], ALU.add)
                g.drain()
                g.sem_inc(sg_o2, 1)
                g.dma_start(out=out_hbm[c * C:(c + 1) * C, :],
                            in_=ob[p][:]).then_inc(dos[p], 16)

        @block.tensor
        def _(t):
            t.wait_ge(sg_i, 1)
            t.wait_ge(dsm, 16 * 5)
            g_idx = 0
            for nj in range(8):
                t.wait_ge(sv_wc, nj + 1)
                for grp in range(2):
                    bk = 6 + grp
                    if g_idx >= 2:
                        t.wait_ge(ss_i, g_idx - 1)
                    for i in range(4):
                        kc = grp * 4 + i
                        ins = t.matmul(pb[bk][0:C, i * C:(i + 1) * C],
                                       wi8_stage[:, kc * C:(kc + 1) * C], id_bf[:],
                                       start=(i == 0), stop=(i == 3))
                    ins.then_inc(st_i, 1)
                    g_idx += 1
            t.wait_ge(dwin, 16 * 2)
            for obk in range(4):
                for f in range(8):
                    ins = t.matmul(pb[obk][0:1, 0:512], ones_col[:, 0:1],
                                   wqkv_sb[:, f * 3 * D + obk * 512: f * 3 * D + obk * 512 + 512],
                                   start=(f == 0), stop=(f == 7))
            ins.then_inc(st_i, 1)
            t.wait_ge(ss_i, 17)
            for obk in range(4):
                ins = t.matmul(pb[obk][0:C, 0:512], ones_row[0:1, :],
                               swqk_row[0:1, obk * 512:(obk + 1) * 512],
                               start=True, stop=True)
            ins.then_inc(st_i, 1)
            for ov in range(2):
                for f in range(8):
                    ins = t.matmul(pb[4 + ov][0:1, 0:512], ones_col[:, 0:1],
                                   wqkv_sb[:, f * 3 * D + 2048 + ov * 512: f * 3 * D + 2048 + ov * 512 + 512],
                                   start=(f == 0), stop=(f == 7))
            ins.then_inc(st_i, 1)
            t.wait_ge(ss_i, 19)
            for fc in range(8):
                ins = t.matmul(pb[4][0:C, fc:fc + 1], swv_row[0:1, fc * C:(fc + 1) * C],
                               ones11_f[0:1, 0:1], start=(fc == 0), stop=(fc == 7))
            ins.then_inc(st_i, 1)
            t.wait_ge(dsin, 16)
            t.wait_ge(ss_i, 18)
            for obk in range(2):
                ins = t.matmul(pb[obk][0:C, 0:512], ones_row_f[0:1, :],
                               wsc_row[0:1, obk * 512:(obk + 1) * 512], start=True, stop=True)
            ins.then_inc(st_i, 1)
            t.wait_ge(sv_i, 1)
            t.wait_ge(ss_i, 16)
            for ov in range(2):
                for fc in range(8):
                    ins = t.matmul(pb[6 + ov][0:H, 0:512], dv_sb[:, fc * H:(fc + 1) * H],
                                   wout_sb[:, fc * D + ov * 512: fc * D + ov * 512 + 512],
                                   start=(fc == 0), stop=(fc == 7))
            ins.then_inc(st_i, 1)
            t.wait_ge(ss_i, IS_TOTAL)

            BMAP = [2, 3, 6, 7]

            def wave(c, w):
                p = c % 2
                if w == 0:
                    t.wait_ge(dxts[p], 16 * (c // 2 + 1))
                    if c >= 1:
                        t.wait_ge(ss_vevac, c)
                else:
                    t.wait_ge(sv_qkevac, 2 * c + w)
                for f in range(8):
                    for j in range(2):
                        obk = 2 * w + j
                        ins = t.matmul(pb[j][0:C, 0:512],
                                       xTa[p][:, f * C:(f + 1) * C],
                                       wqkv_sb[:, f * 3 * D + obk * 512: f * 3 * D + obk * 512 + 512],
                                       start=(f == 0), stop=(f == 7))
                ins.then_inc(st_qkv, 1)

            def transposes(c):
                for g8 in range(8):
                    bk = BMAP[g8 % 4]
                    t.wait_ge(sv_elu, 2 * c + (1 if g8 < 4 else 2))
                    if g8 == 0 and c >= 1:
                        t.wait_ge(sv_kvadd, c)
                    if g8 == 2 and c >= 1:
                        t.wait_ge(sv_divevac, 4 * c)
                        t.wait_ge(ss_attnTevac, 3 * c)
                    if g8 >= 4:
                        t.wait_ge(ss_qkTevac, 4 * c + g8 - 3)
                    for i in range(4):
                        hh = g8 * 4 + i
                        ins = t.matmul(pb[bk][0:DH, i * C:(i + 1) * C],
                                       qkp[c % 2][:, hh * DH:(hh + 1) * DH], id_bf[:],
                                       start=(i == 0), stop=(i == 3))
                    ins.then_inc(st_qkT, 1)

            def stile(c):
                for g4 in range(4):
                    bk = BMAP[g4]
                    t.wait_ge(ss_qkTevac, 4 * c + g4 + 1)
                    t.wait_ge(sv_qkTek, 4 * c + g4 + 1)
                    for i in range(4):
                        h = g4 * 4 + i
                        ins = t.matmul(pb[bk][0:C, i * C:(i + 1) * C],
                                       qkT[:, (H + h) * C:(H + h + 1) * C],
                                       qkT[:, h * C:(h + 1) * C],
                                       start=(i == 0), stop=(i == 3))
                    ins.then_inc(st_stile, 1)

            def outkv(c):
                t.wait_ge(ss_kvcast, c)
                t.wait_ge(ss_vevac, c + 1)
                t.wait_ge(sv_qkevac, 2 * (c + 1))
                t.wait_ge(sv_qkTek, 4 * (c + 1))
                if c >= 1:
                    t.wait_ge(ss_partial, c)
                    t.wait_ge(sv_kvadd, c)
                for b in range(4):
                    t.wait_ge(sv_smevac, 4 * c + b + 1)
                    for i in range(4):
                        h = 4 * b + i
                        slot = i * C
                        t.matmul(pb[4 + b][0:C, slot:slot + 66],
                                 smb[b][:, i * C:(i + 1) * C],
                                 vpp[c % 2][:, h * 66:(h + 1) * 66],
                                 start=(i == 0), stop=False)
                        ins = t.matmul(pb[4 + b][0:C, slot:slot + 66],
                                       qkT[:, h * C:(h + 1) * C],
                                       kv_bf[:, h * 66:(h + 1) * 66],
                                       start=False, stop=(i == 3))
                    if b == 3:
                        for h in range(16):
                            ins = t.matmul(pb[7][0:DH, 450 + h * 2:450 + (h + 1) * 2],
                                           qkp[c % 2][:, D + h * DH: D + (h + 1) * DH],
                                           vpp[c % 2][:, h * 66 + 64:(h + 1) * 66],
                                           start=(h == 0), stop=(h == 15))
                    ins.then_inc(st_attnout, 1)
                t.wait_ge(sv_smevac, 4 * (c + 1))
                for h in range(8):
                    t.matmul(pb[2][0:DH, h * 64:(h + 1) * 64],
                             qkp[c % 2][:, D + h * DH: D + (h + 1) * DH],
                             vpp[c % 2][:, h * 66:h * 66 + 64],
                             start=(h == 0), stop=(h == 7))
                for h in range(8, 16):
                    ins = t.matmul(pb[3][0:DH, (h - 8) * 64:(h - 7) * 64],
                                   qkp[c % 2][:, D + h * DH: D + (h + 1) * DH],
                                   vpp[c % 2][:, h * 66:h * 66 + 64],
                                   start=(h == 8), stop=(h == 15))
                ins.then_inc(st_kv, 1)

            def attn_tail(c):
                t.wait_ge(sv_divevac, 4 * (c + 1))
                for i in range(4):
                    ins = t.matmul(pb[4][0:C, i * C:(i + 1) * C],
                                   attn_sb[:, i * C:(i + 1) * C], id_bf[:],
                                   start=(i == 0), stop=(i == 3))
                ins.then_inc(st_attnT, 1)
                for i in range(4):
                    ins = t.matmul(pb[5][0:C, i * C:(i + 1) * C],
                                   attn_sb[:, (4 + i) * C:(5 + i) * C], id_bf[:],
                                   start=(i == 0), stop=(i == 3))
                ins.then_inc(st_attnT, 1)
                t.wait_ge(sv_m1d, c + 1)
                t.matmul(pb[6][0:H, 0:C], m1d_bf[:, 0:H], id_bf[:],
                         start=True, stop=True).then_inc(st_attnT, 1)
                t.wait_ge(ss_attnTevac, 3 * c + 1)
                for j in range(4):
                    t.matmul(pb[4][0:C, 0:512], attnT_sb[:, j * C:(j + 1) * C],
                             wout_sb[:, j * D: j * D + 512], start=(j == 0), stop=False)
                t.wait_ge(ss_attnTevac, 3 * c + 2)
                for j in range(4, 8):
                    t.matmul(pb[4][0:C, 0:512], attnT_sb[:, j * C:(j + 1) * C],
                             wout_sb[:, j * D: j * D + 512], start=False, stop=False)
                for j in range(8):
                    t.matmul(pb[5][0:C, 0:512], attnT_sb[:, j * C:(j + 1) * C],
                             wout_sb[:, j * D + 512: (j + 1) * D], start=(j == 0), stop=False)
                t.matmul(pb[4][0:C, 0:512], ones_row[0:1, :], bias_row[0:1, 0:512],
                         start=False, stop=False)
                t.matmul(pb[5][0:C, 0:512], ones_row[0:1, :], bias_row[0:1, 512:1024],
                         start=False, stop=False)
                t.wait_ge(ss_attnTevac, 3 * (c + 1))
                t.matmul(pb[4][0:C, 0:512], m1dT[0:H, :], posG[0:H, 0:512],
                         start=False, stop=True)
                t.matmul(pb[5][0:C, 0:512], m1dT[0:H, :], posG[0:H, 512:1024],
                         start=False, stop=True).then_inc(st_outproj, 1)

            for c in range(NCH):
                wave(c, 0)
                if c >= 2:
                    attn_tail(c - 2)
                if c >= 1:
                    transposes(c - 1)
                wave(c, 1)
                if c >= 1:
                    stile(c - 1)
                wave(c, 2)
                if c >= 1:
                    outkv(c - 1)
            attn_tail(NCH - 2)
            transposes(NCH - 1)
            stile(NCH - 1)
            outkv(NCH - 1)
            attn_tail(NCH - 1)

            for c in range(NCH):
                p = c % 2
                t.wait_ge(ss_xq, c + 1)
                for gx in range(2):
                    bk = 6 + gx
                    if c == 0 and gx == 0:
                        t.wait_ge(sv_divevac, 4 * NCH)
                        t.wait_ge(sv_kvadd, NCH)
                    if c >= 1:
                        t.wait_ge(ss_xqTevac, 2 * c - 1 + gx)
                    for i in range(4):
                        j = gx * 4 + i
                        ins = t.matmul(pb[bk][0:C, i * C:(i + 1) * C],
                                       xTa[p][:, j * C:(j + 1) * C], id_bf[:],
                                       start=(i == 0), stop=(i == 3))
                    ins.then_inc(st_xqT, 1)
                fb = 2 * (c % 2)
                if c == 0:
                    t.wait_ge(ss_vevac, NCH)
                elif c >= 2:
                    t.wait_ge(sv_o1, c - 1)
                t.wait_ge(ss_xqTevac, 2 * c + 1)
                for kc in range(8):
                    if kc == 4:
                        t.wait_ge(ss_xqTevac, 2 * (c + 1))
                    t.matmul(pb[fb][0:C, 0:512], xqT[:, kc * C:(kc + 1) * C],
                             wffnT_sb[:, kc * D: kc * D + 512],
                             start=(kc == 0), stop=(kc == 7))
                    ins = t.matmul(pb[fb + 1][0:C, 0:512], xqT[:, kc * C:(kc + 1) * C],
                                   wffnT_sb[:, kc * D + 512: (kc + 1) * D],
                                   start=(kc == 0), stop=(kc == 7))
                ins.then_inc(st_ffn, 1)

        @block.vector
        def _(v):
            for nj in range(8):
                v.wait_ge(dwr[nj % 2], 16 * (nj // 2 + 1))
                if nj >= 1:
                    v.wait_ge(st_i, 2 * nj)
                v.drain()
                v.tensor_copy(wi8_stage[:], wi8_i8[:, (nj % 2) * 1024:(nj % 2 + 1) * 1024])
                v.drain()
                v.sem_inc(sv_wc, 1)
            v.wait_ge(ss_i, 20)
            v.wait_ge(sg_i, 1)
            for fc in range(8):
                v.tensor_scalar(dv_sb[:, fc * H + 2 * fc: fc * H + 2 * fc + 1],
                                ind_lo[:, 0:1], swv_cols[:, fc:fc + 1], None, ALU.mult)
                v.tensor_scalar(dv_sb[:, fc * H + 2 * fc + 1: fc * H + 2 * fc + 2],
                                ind_hi[:, 0:1], swv_cols[:, fc:fc + 1], None, ALU.mult)
            v.drain()
            v.sem_inc(sv_i, 1)
            v.wait_ge(ss_i, IS_TOTAL)
            v.wait_ge(dsm, 16 * 5)

            def stats(c):
                p = c % 2
                v.wait_ge(dxs[p], 16 * (c // 2 + 1))
                if c >= 2:
                    v.wait_ge(ss_stats, c - 1)
                v.drain()
                v.bn_stats(bn6[:, 0:6], xa[p][:, 0:512])
                v.bn_stats(bn6[:, 6:12], xa[p][:, 512:1024])
                v.drain()
                v.bn_aggr(bn2[p][:], bn6[:])
                v.tensor_reduce(maxx_all[:, c:c + 1], xa[p][:], axis=AX, op=ALU.max)
                v.tensor_reduce(minx_all[:, c:c + 1], xa[p][:], axis=AX, op=ALU.min)
                v.drain()
                v.tensor_scalar(mean_all[:, c:c + 1], bn2[p][:, 0:1], -1.0, None, ALU.mult)
                v.drain()
                v.sem_inc(sv_stats, 1)

            def qk_evac(c, half):
                v.wait_ge(st_qkv, 3 * c + half + 1)
                v.drain()
                for b in range(2):
                    v.scalar_tensor_tensor(qkln[:, half * D + b * 512:half * D + (b + 1) * 512],
                                           swqk_bc[:, half * D + b * 512:half * D + (b + 1) * 512],
                                           mean_all[:, c:c + 1],
                                           pb[b][0:C, 0:512],
                                           ALU.mult, ALU.add)
                v.drain()
                v.sem_inc(sv_qkevac, 1)

            def rope_half(c, half):
                off = half * D
                if half == 0:
                    v.wait_ge(ss_stats, c + 1)
                if c >= 2:
                    v.wait_ge(st_qkT, 8 * (c - 2) + 8)
                    v.wait_ge(st_kv, c - 1)
                cosap = bass.AP(costab[:, :].tensor, c * DH, [[NCH * DH, C], [0, 16], [1, DH]])
                sinap = bass.AP(sintab[:, :].tensor, c * DH,
                                [[NCH * DH, C], [0, 16], [32, 2], [1, 32]])
                qk3 = qkln[:, off:off + D].rearrange("p (g n) -> p g n", g=16)
                v.tensor_tensor(rt1[:, off:off + D].rearrange("p (g n) -> p g n", g=16),
                                qk3, cosap, ALU.mult)
                gat = bass.AP(qkln[:, :].tensor, off + 32,
                              [[2 * D, C], [DH, 16], [-32, 2], [1, 32]])
                v.tensor_tensor(rt2[:, off:off + D].rearrange("p (g r n) -> p g r n", g=16, r=2),
                                gat, sinap, ALU.mult)
                v.drain()
                v.tensor_add(qkp[c % 2][:, off:off + D], rt1[:, off:off + D], rt2[:, off:off + D])
                v.drain()
                v.sem_inc(sv_rope, 1)
                v.tensor_scalar(rt1[:, off:off + D], qkp[c % 2][:, off:off + D],
                                srst_all[:, c:c + 1], 0.0, ALU.mult, ALU.max)

            def elu_half(c, half):
                off = half * D
                v.wait_ge(ss_exp, 2 * c + half + 1)
                v.drain()
                v.scalar_tensor_tensor(qkp[c % 2][:, off:off + D], rt2[:, off:off + D], 1.0,
                                       rt1[:, off:off + D], ALU.min, ALU.add)
                v.drain()
                v.sem_inc(sv_elu, 1)

            def qkT_evacs_k(c):
                for g8 in range(4, 8):
                    v.wait_ge(st_qkT, 8 * c + g8 + 1)
                    v.drain()
                    v.tensor_copy(qkT[0:DH, g8 * 512:(g8 + 1) * 512],
                                  pb[[2, 3, 6, 7][g8 - 4]][0:DH, 0:512])
                    v.drain()
                    v.sem_inc(sv_qkTek, 1)

            def mask_evacs(c, lo, hi):
                for g4 in range(lo, hi):
                    v.wait_ge(st_stile, 4 * c + g4 + 1)
                    if c >= 1:
                        v.wait_ge(st_attnout, 4 * (c - 1) + g4 + 1)
                    v.drain()
                    mk = bass.AP(maskT_sb[:, :].tensor, 0, [[C, C], [0, 4], [1, C]])
                    v.tensor_tensor(smb[g4][:, :].rearrange("p (i n) -> p i n", i=4),
                                    pb[[2, 3, 6, 7][g4]][0:C, 0:512].rearrange("p (i n) -> p i n", i=4),
                                    mk, ALU.mult)
                    v.drain()
                    v.sem_inc(sv_smevac, 1)

            def div_evacs(c):
                for b in range(4):
                    v.wait_ge(st_attnout, 4 * c + b + 1)
                    v.drain()
                    denap = bass.AP(pb[4 + b][:, :].tensor, 65, [[512, C], [C, 4]])
                    v.tensor_scalar(dent[:, 0:4], denap, 1e-6, None, ALU.max)
                    v.drain()
                    v.reciprocal(rcp[:, 0:4], dent[:, 0:4])
                    v.drain()
                    numap = bass.AP(pb[4 + b][:, :].tensor, 0, [[512, C], [C, 4], [1, 64]])
                    rcpb = bass.AP(rcp[:, :].tensor, 0, [[4, C], [1, 4], [0, 64]])
                    v.tensor_tensor(attn_sb[:, b * 256:(b + 1) * 256].rearrange("p (i n) -> p i n", i=4),
                                    numap, rcpb, ALU.mult)
                    m1ap = bass.AP(pb[4 + b][:, :].tensor, 64, [[512, C], [C, 4]])
                    v.tensor_tensor(m1d[:, 4 * b:4 * b + 4], m1ap, rcp[:, 0:4], ALU.mult)
                    v.drain()
                    v.sem_inc(sv_divevac, 1)
                v.drain()
                v.tensor_copy(m1d_bf[:], m1d[:])
                v.drain()
                v.sem_inc(sv_m1d, 1)

            def kv_add(c):
                v.wait_ge(st_kv, c + 1)
                if c >= 1:
                    v.wait_ge(ss_kvcast, c)
                v.drain()
                for half in range(2):
                    dst = bass.AP(kvf[(c + 1) % 2][:, :].tensor, half * 8 * 66,
                                  [[H * 66, DH], [66, 8], [1, 64]])
                    srcf = bass.AP(kvf[c % 2][:, :].tensor, half * 8 * 66,
                                   [[H * 66, DH], [66, 8], [1, 64]])
                    v.tensor_tensor(dst, srcf,
                                    pb[2 + half][0:DH, 0:512].rearrange("p (h n) -> p h n", h=8),
                                    ALU.add)
                dste = bass.AP(kvf[(c + 1) % 2][:, :].tensor, 64, [[H * 66, DH], [66, H], [1, 2]])
                srce = bass.AP(kvf[c % 2][:, :].tensor, 64, [[H * 66, DH], [66, H], [1, 2]])
                v.tensor_tensor(dste, srce,
                                pb[7][0:DH, 450:482].rearrange("p (h n) -> p h n", h=H),
                                ALU.add)
                v.drain()
                v.sem_inc(sv_kvadd, 1)

            stats(0)
            for c in range(NCH):
                qk_evac(c, 0)
                rope_half(c, 0)
                if c == NCH - 1:
                    v.wait_ge(ss_stats, NCH)
                    v.drain()
                    v.tensor_add(tmp1c[:], maxx_all[:], mean_all[:])
                    v.tensor_add(tmp2c[:], minx_all[:], mean_all[:])
                    v.drain()
                    v.tensor_scalar(minx_all[:], tmp2c[:], -1.0, None, ALU.mult)
                    v.drain()
                    v.tensor_tensor(tmp2c[:], tmp1c[:], minx_all[:], ALU.max)
                    v.drain()
                    v.tensor_mul(tmp1c[:], tmp2c[:], rstd_all[:])
                    v.drain()
                    v.reduce_max(locmax[:], tmp1c[:], axis=AX)
                    v.drain()
                    v.sem_inc(sv_locmax, 1)
                if c >= 1:
                    qkT_evacs_k(c - 1)
                qk_evac(c, 1)
                if c >= 1:
                    mask_evacs(c - 1, 0, 4)
                rope_half(c, 1)
                elu_half(c, 0)
                elu_half(c, 1)
                if c >= 1:
                    div_evacs(c - 1)
                    kv_add(c - 1)
                if c + 1 < NCH:
                    stats(c + 1)
            qkT_evacs_k(NCH - 1)
            mask_evacs(NCH - 1, 0, 4)
            div_evacs(NCH - 1)
            kv_add(NCH - 1)

            v.wait_ge(dcc, 32)
            v.drain()
            v.reduce_max(g11[0:1, 0:1], grow[0:1, :], axis=AX)
            v.drain()
            v.sem_inc(sv_g11, 1)
            v.wait_ge(sg_cc, 1)
            v.drain()
            v.tensor_scalar(xsc[:], gmax_bc[:], INV127, 1e-5, ALU.mult, ALU.max)
            v.drain()
            v.reciprocal(rs_col[:], xsc[:])
            v.drain()
            v.tensor_scalar(sws_bc[:], ws_bc[:], xsc[:, 0:1], None, ALU.mult)
            v.drain()
            v.sem_inc(sv_xsc, 1)

            for c in range(NCH):
                p = c % 2
                if c >= 2:
                    v.wait_ge(ss_z, c - 1)
                v.drain()
                v.tensor_scalar(a_col[p][:], rstd_all[:, c:c + 1], rs_col[:, 0:1], None, ALU.mult)
                v.drain()
                v.tensor_scalar(b_col[p][:], mean_all[:, c:c + 1], a_col[p][:, 0:1], None, ALU.mult)
                v.drain()
                v.sem_inc(sv_ab, 1)
                v.wait_ge(ss_z, c + 1)
                if c >= 2:
                    v.wait_ge(ss_xq, c - 1)
                v.drain()
                v.tensor_scalar(tib[p], ztb[p], CM, -CM, ALU.add, ALU.add)
                v.drain()
                v.sem_inc(sv_round, 1)
                if c >= 1:
                    ob2 = 2 * ((c - 1) % 2)
                    v.wait_ge(st_ffn, c)
                    v.drain()
                    v.tensor_mul(o1t[:, 0:512], pb[ob2][0:C, 0:512], sws_bc[:, 0:512])
                    v.tensor_mul(o1t[:, 512:1024], pb[ob2 + 1][0:C, 0:512], sws_bc[:, 512:1024])
                    v.drain()
                    v.sem_inc(sv_o1, 1)
                    if c >= 2:
                        v.wait_ge(sg_o2, c - 1)
                    v.wait_ge(dpb4[(c - 1) % 4], 16 * ((c - 1) // 4 + 1))
                    v.drain()
                    v.tensor_tensor(g2t[:], o1t[:], xa4[(c - 1) % 4], ALU.add)
                    v.drain()
                    v.sem_inc(sv_g2t, 1)

            def o1g2t_last():
                ob2 = 2 * ((NCH - 1) % 2)
                v.wait_ge(st_ffn, NCH)
                v.drain()
                v.tensor_mul(o1t[:, 0:512], pb[ob2][0:C, 0:512], sws_bc[:, 0:512])
                v.tensor_mul(o1t[:, 512:1024], pb[ob2 + 1][0:C, 0:512], sws_bc[:, 512:1024])
                v.drain()
                v.sem_inc(sv_o1, 1)
                v.wait_ge(sg_o2, NCH - 1)
                v.wait_ge(dpb4[(NCH - 1) % 4], 16 * ((NCH - 1) // 4 + 1))
                v.drain()
                v.tensor_tensor(g2t[:], o1t[:], xa4[(NCH - 1) % 4], ALU.add)
                v.drain()
                v.sem_inc(sv_g2t, 1)
            o1g2t_last()

        @block.scalar
        def _(s):
            k = 0
            for nj in range(8):
                for grp in range(2):
                    k += 1
                    s.wait_ge(st_i, k)
                    dst = wffnT_sb[:, :].rearrange("p (kc n) -> p kc n", kc=8)[:, grp * 4:(grp + 1) * 4,
                                                                              nj * C:(nj + 1) * C]
                    s.copy(dst, pb[6 + grp][0:C, 0:512].rearrange("p (i n) -> p i n", i=4))
                    s.drain()
                    s.sem_inc(ss_i, 1)
            s.wait_ge(st_i, 17)
            for obk in range(4):
                s.copy(swqk_row[0:1, obk * 512:(obk + 1) * 512], pb[obk][0:1, 0:512])
            s.drain()
            s.sem_inc(ss_i, 1)
            s.wait_ge(st_i, 18)
            for obk in range(4):
                s.copy(swqk_bc[:, obk * 512:(obk + 1) * 512], pb[obk][0:C, 0:512])
            s.drain()
            s.sem_inc(ss_i, 1)
            s.wait_ge(st_i, 19)
            for ov in range(2):
                s.copy(swv_row[0:1, ov * 512:(ov + 1) * 512], pb[4 + ov][0:1, 0:512])
            s.drain()
            s.sem_inc(ss_i, 1)
            s.wait_ge(st_i, 20)
            s.copy(swv_cols[:, 0:8], pb[4][0:C, 0:8])
            s.drain()
            s.sem_inc(ss_i, 1)
            s.wait_ge(st_i, 21)
            for obk in range(2):
                s.copy(ws_bc[:, obk * 512:(obk + 1) * 512], pb[obk][0:C, 0:512])
            s.drain()
            s.sem_inc(ss_i, 1)
            s.wait_ge(st_i, 22)
            for ov in range(2):
                s.copy(posG[0:H, ov * 512:(ov + 1) * 512], pb[6 + ov][0:H, 0:512])
            s.drain()
            s.sem_inc(ss_i, 1)

            def rstd(c):
                p = c % 2
                s.wait_ge(sv_stats, c + 1)
                s.drain()
                s.activation(s_ve[p][:], bn2[p][:, 1:2], ACTF.Identity, bias=epsln[:, 0:1])
                s.drain()
                s.activation(s_tt[p][:], s_ve[p][:], ACTF.Ln)
                s.drain()
                s.activation(s_sd[p][:], s_tt[p][:], ACTF.Exp, scale=neghalf[:, 0:1])
                s.drain()
                s.activation(s_r2[p][:], s_sd[p][:], ACTF.Square)
                s.drain()
                s.mul(s_tt[p][:], s_ve[p][:], s_r2[p][:, 0:1])
                s.drain()
                s.activation(s_uu[p][:], s_tt[p][:], ACTF.Identity,
                             scale=neghalf[:, 0:1], bias=c15[:, 0:1])
                s.drain()
                s.mul(rstd_all[:, c:c + 1], s_sd[p][:], s_uu[p][:, 0:1])
                s.drain()
                s.activation(srst_all[:, c:c + 1], rstd_all[:, c:c + 1], ACTF.Identity,
                             scale=scsc[:, 0:1])
                s.drain()
                s.sem_inc(ss_stats, 1)

            def vevac(c):
                s.wait_ge(st_qkv, 3 * c + 3)
                s.drain()
                for ov in range(2):
                    dst = bass.AP(vpp[c % 2][:, :].tensor, ov * 8 * 66, [[H * 66, C], [66, 8], [1, 64]])
                    s.activation(dst, pb[ov][0:C, 0:512].rearrange("p (h n) -> p h n", h=8),
                                 ACTF.Identity, scale=rstd_all[:, c:c + 1])
                s.drain()
                rmdst = bass.AP(vpp[c % 2][:, :].tensor, 64, [[H * 66, C], [66, H]])
                negm_b = bass.AP(mean_all[:, :].tensor, c, [[NCH, C], [0, H]])
                s.activation(rmdst, negm_b, ACTF.Identity, scale=rstd_all[:, c:c + 1])
                s.drain()
                s.sem_inc(ss_vevac, 1)

            def exps_half(c, half):
                off = half * D
                s.wait_ge(sv_rope, 2 * c + half + 1)
                s.drain()
                s.activation(rt2[:, off:off + D], qkp[c % 2][:, off:off + D], ACTF.Exp,
                             scale=srst_all[:, c:c + 1])
                s.drain()
                s.sem_inc(ss_exp, 1)

            def qkT_evacs(c):
                for g8 in range(4):
                    s.wait_ge(st_qkT, 8 * c + g8 + 1)
                    s.drain()
                    s.copy(qkT[0:DH, g8 * 512:(g8 + 1) * 512], pb[[2, 3, 6, 7][g8]][0:DH, 0:512])
                    s.drain()
                    s.sem_inc(ss_qkTevac, 1)

            def attnT_evacs(c):
                for gx in range(2):
                    s.wait_ge(st_attnT, 3 * c + gx + 1)
                    s.drain()
                    s.copy(attnT_sb[:, gx * 512:(gx + 1) * 512], pb[4 + gx][0:C, 0:512])
                    s.drain()
                    s.sem_inc(ss_attnTevac, 1)
                s.wait_ge(st_attnT, 3 * c + 3)
                s.drain()
                s.copy(m1dT[0:H, :], pb[6][0:H, 0:C])
                s.drain()
                s.sem_inc(ss_attnTevac, 1)

            def partial_evac(c):
                s.wait_ge(st_outproj, c + 1)
                if c >= 2:
                    s.wait_ge(dps[c % 2], 16 * (c // 2))
                s.drain()
                s.copy(partial[c % 2][:, 0:512], pb[4][0:C, 0:512])
                s.copy(partial[c % 2][:, 512:1024], pb[5][0:C, 0:512])
                s.drain()
                s.sem_inc(ss_partial, 1)

            for c in range(NCH):
                if c >= 2:
                    attnT_evacs(c - 2)
                rstd(c)
                if c >= 1:
                    qkT_evacs(c - 1)
                if c >= 2:
                    partial_evac(c - 2)
                exps_half(c, 0)
                exps_half(c, 1)
                vevac(c)
            attnT_evacs(NCH - 2)
            partial_evac(NCH - 2)
            qkT_evacs(NCH - 1)
            attnT_evacs(NCH - 1)
            partial_evac(NCH - 1)

            s.wait_ge(st_qkv, 3 * NCH)
            s.wait_ge(dps[0], 16 * (NCH // 2))
            s.wait_ge(dps[1], 16 * (NCH // 2))
            for c in range(NCH):
                p = c % 2
                s.wait_ge(sv_ab, c + 1)
                s.wait_ge(dpb4[c % 4], 16 * (c // 4 + 1))
                if c >= 2:
                    s.wait_ge(sv_round, c - 1)
                s.drain()
                s.activation(ztb[p], xa4[c % 4], ACTF.Identity,
                             scale=a_col[p][:, 0:1], bias=b_col[p][:, 0:1])
                s.drain()
                s.sem_inc(ss_z, 1)
                s.wait_ge(sv_round, c + 1)
                if c >= 2:
                    s.wait_ge(st_xqT, 2 * (c - 1))
                s.drain()
                s.copy(xTa[p][:], tib[p])
                s.drain()
                s.sem_inc(ss_xq, 1)
                for gx in range(2):
                    s.wait_ge(st_xqT, 2 * c + gx + 1)
                    s.drain()
                    s.copy(xqT[:, gx * 512:(gx + 1) * 512], pb[6 + gx][0:C, 0:512])
                    s.drain()
                    s.sem_inc(ss_xqTevac, 1)

    return nc


def _host_tables():
    half = DH // 2
    inv = (np.float32(1.0) / (np.float32(10000.0) **
           (np.arange(half, dtype=np.float32) / np.float32(half)))).astype(np.float32)
    pos = np.arange(T_TOK, dtype=np.float32)
    fr = pos[:, None] * inv[None, :]
    co = np.cos(fr).astype(np.float32)
    si = np.sin(fr).astype(np.float32)
    cost = np.zeros((C, NCH * DH), np.float32)
    sint = np.zeros((C, NCH * DH), np.float32)
    for c in range(NCH):
        rows = slice(c * C, (c + 1) * C)
        cost[:, c * DH:c * DH + half] = co[rows, :]
        cost[:, c * DH + half:(c + 1) * DH] = co[rows, :]
        sint[:, c * DH:c * DH + half] = -si[rows, :]
        sint[:, c * DH + half:(c + 1) * DH] = si[rows, :]
    maskT = np.triu(np.ones((C, C), np.float32))
    eye = np.eye(C, dtype=np.float32)
    return cost, sint, maskT, eye


def kernel(L0_attn_x, L0_attn_w_qkv, L0_attn_w_out, L0_ffn_weight_int8,
           L0_ffn_weight_scale, L0_ffn_bias):
    x = np.ascontiguousarray(np.asarray(L0_attn_x, dtype=np.float32))
    wqkv = np.ascontiguousarray(np.asarray(L0_attn_w_qkv, dtype=np.float32))
    wout = np.ascontiguousarray(np.asarray(L0_attn_w_out, dtype=np.float32))
    wi8 = np.ascontiguousarray(np.asarray(L0_ffn_weight_int8, dtype=np.int8))
    wsc = np.ascontiguousarray(np.asarray(L0_ffn_weight_scale, dtype=np.float32).reshape(1, D))
    bias = np.ascontiguousarray(np.asarray(L0_ffn_bias, dtype=np.float32).reshape(1, D))
    cost, sint, maskT, eye = _host_tables()

    nc = build_nc()
    in_maps = []
    for b in range(8):
        xb = np.ascontiguousarray(x[b])
        in_maps.append({
            "x": xb,
            "xT": np.ascontiguousarray(xb.T),
            "wqkv": wqkv,
            "wout": wout,
            "wi8": wi8,
            "wsc": wsc,
            "bias": bias,
            "cost": cost,
            "sint": sint,
            "maskT": maskT,
            "eye": eye,
        })
    res = run_bass_kernel_spmd(nc, in_maps, core_ids=list(range(8)))
    out = np.stack([np.asarray(r["out"]) for r in res.results], axis=0)
    return out.astype(np.float32)
